# version 38
# baseline (speedup 1.0000x reference)
import numpy as np
import ml_dtypes
from contextlib import ExitStack
import concourse.bass as bass
import concourse.mybir as mybir
from concourse.bass_utils import run_bass_kernel_spmd

F32 = mybir.dt.float32
BF16 = mybir.dt.bfloat16
I32 = mybir.dt.int32
AF = mybir.ActivationFunctionType
ALU = mybir.AluOpType
AX = mybir.AxisListType

S = 2048
D = 1024
H = 16
DK = 64
F = 4096
NT = S // 128
NQG = S // 512
NCH = D // 128
NFC = F // 128
EPS = 1e-5
LN8 = float(np.log(0.125))

ENGS = ("sync", "scalar", "vector", "tensor", "gpsimd")


class Ct:
    __slots__ = ("h", "n")

    def __init__(self, h):
        self.h = h
        self.n = 0


class Sched:

    def __init__(self, cts):
        self.ct = cts
        self.streams = {e: [] for e in ENGS}

    def op(self, eng, fn, inc=None, k=1):
        if inc is not None:
            c = self.ct[inc]
            c.n += k

            def g(e, fn=fn, h=c.h, k=k):
                fn(e).then_inc(h, k)
            self.streams[eng].append(g)
        else:
            self.streams[eng].append(fn)

    def wait(self, eng, name, val=None):
        c = self.ct[name]
        v = c.n if val is None else val
        self.streams[eng].append(lambda e, h=c.h, v=v: e.wait_ge(h, v))

    def snap(self, name):
        return self.ct[name].n


def run_block(nc, name, sch):
    with nc.Block(name) as blk:
        for eng in ENGS:
            fns = sch.streams[eng]
            if fns:
                getattr(blk, eng)(lambda e, fns=fns: [f(e) for f in fns])


def fap(t, free_dims, offset=0):
    fe = t.shape[1]
    return bass.AP(t, offset, [[fe, 128]] + [list(d) for d in free_dims])


def pap(t, p0, np_, free_dims, offset=0):
    fe = t.shape[1]
    return bass.AP(t, offset + p0 * fe, [list(d) for d in free_dims[:0]] or None) if False else \
        bass.AP(t, offset + p0 * fe, [[fe, np_]] + [list(d) for d in free_dims])


def strided_parts(t, pstep, pcount, free_dims, offset=0):
    fe = t.shape[1]
    return bass.AP(t, offset, [[fe * pstep, pcount]] + [list(d) for d in free_dims])


MANUAL_BASE = 16896
MANUAL_TOP = MANUAL_BASE + 163840
R1, R2, R3, R4, R5 = (MANUAL_BASE, MANUAL_BASE + 32768, MANUAL_BASE + 65536,
                      MANUAL_BASE + 98304, MANUAL_BASE + 131072)


def build_nc(upto=9):
    nc = bass.Bass("TRN2", target_bir_lowering=False)
    nc.sbuf_base = MANUAL_TOP

    x_d = nc.dram_tensor("x", [S, D], F32, kind="ExternalInput")
    wq_d = nc.dram_tensor("wq", [D, D], F32, kind="ExternalInput")
    wk_d = nc.dram_tensor("wk", [D, D], F32, kind="ExternalInput")
    wv_d = nc.dram_tensor("wv", [D, D], F32, kind="ExternalInput")
    wo_d = nc.dram_tensor("wo", [D, D], F32, kind="ExternalInput")
    gam_d = nc.dram_tensor("ln_gamma", [D], F32, kind="ExternalInput")
    bet_d = nc.dram_tensor("ln_beta", [D], F32, kind="ExternalInput")
    q1_d = nc.dram_tensor("fc1_q", [D, F], I32, kind="ExternalInput")
    s1_d = nc.dram_tensor("fc1_scales", [D // 128, F], F32, kind="ExternalInput")
    b1_d = nc.dram_tensor("fc1_bias", [F], F32, kind="ExternalInput")
    q2_d = nc.dram_tensor("fc2_q", [F, D], I32, kind="ExternalInput")
    s2_d = nc.dram_tensor("fc2_scales", [F // 128, D], F32, kind="ExternalInput")
    b2_d = nc.dram_tensor("fc2_bias", [D], F32, kind="ExternalInput")
    ang_d = {}
    for nm in ("ths_hi", "ths_lo", "sgs", "thc_hi", "thc_lo", "sgc"):
        ang_d[nm] = nc.dram_tensor(nm, [128, S], F32, kind="ExternalInput")
    idn_d = nc.dram_tensor("ident", [128, 128], F32, kind="ExternalInput")
    msk_d = nc.dram_tensor("cmask", [128, 128], F32, kind="ExternalInput")
    out_d = nc.dram_tensor("out", [S, D], F32, kind="ExternalOutput")

    sdram = nc.dram_tensor("s_spill", [S, D], F32, kind="Internal")
    hdram = nc.dram_tensor("h_spill", [128, NCH * S], BF16, kind="Internal")
    g2dram = nc.dram_tensor("g2_row", [16, 128], F32, kind="Internal")
    w1dram = nc.dram_tensor("w1deq", [128, NCH * F], BF16, kind="Internal")
    w2dram = nc.dram_tensor("w2deq", [F, D], BF16, kind="Internal")

    cts = {}

    with ExitStack() as es:
        for name in ("dC", "dX0", "dX1", "dW", "dW0", "dW1", "dW2", "dW3",
                     "dR0", "dR1", "dS", "dS0", "dS1", "dG0", "dG1", "dGW", "dH",
                     "cPE", "cVE", "cSC", "cGP"):
            cts[name] = Ct(es.enter_context(nc.semaphore(name)))

        def sb(stack, name, cols, dt):
            return stack.enter_context(nc.sbuf_tensor(name, [128, cols], dt))

        gam_cm = sb(es, "gam_cm", NCH, F32)
        bet_cm = sb(es, "bet_cm", NCH, F32)
        g2_a = sb(es, "g2_a", NT, F32)
        b2h_rep = sb(es, "b2h_rep", D, F32)
        b1_cm = sb(es, "b1_cm", NFC, F32)
        idn_bf = sb(es, "idn_bf", 128, BF16)
        idn_f32 = sb(es, "idn_f32", 128, F32)
        msk_bf = sb(es, "msk_bf", 128, BF16)
        ones_bf = sb(es, "ones_bf", 1, BF16)
        selq = sb(es, "selq", 4, BF16)
        ones_row = sb(es, "ones_row", 128, BF16)
        negmu_a = sb(es, "negmu_a", NT, F32)
        r1_a = sb(es, "r1_a", NT, F32)
        rg_a = sb(es, "rg_a", NT, F32)
        negub = sb(es, "negub", 1, F32)
        stat = sb(es, "stat", 16, F32)
        biasc = sb(es, "biasc", 2, F32)
        statb = sb(es, "statb", 2, BF16)

        xn_fm = nc.alloc_sbuf_tensor_at("xn_fm", [128, NCH * S], BF16, offset=R1).ap().tensor
        q_att = nc.alloc_sbuf_tensor_at("q_att", [128, (H // 2) * S], BF16, offset=R2).ap().tensor
        k_att = nc.alloc_sbuf_tensor_at("k_att", [128, (H // 2) * S], BF16, offset=R3).ap().tensor
        v_tm = nc.alloc_sbuf_tensor_at("v_tm", [128, NT * D], BF16, offset=R4).ap().tensor
        cos_t = nc.alloc_sbuf_tensor_at("cos_sb", [128, S], BF16, offset=R5).ap().tensor
        sin_t = nc.alloc_sbuf_tensor_at("sin_sb", [128, S], BF16, offset=R5 + 4096).ap().tensor
        wv_bf = nc.alloc_sbuf_tensor_at("wv_bf", [128, NCH * D], BF16, offset=R5 + 8192).ap().tensor
        o_sb = nc.alloc_sbuf_tensor_at("o_sb", [128, NCH * S], BF16, offset=R5).ap().tensor
        dq_i0 = nc.alloc_sbuf_tensor_at("dq_i0", [128, 2048], I32, offset=R1).ap().tensor
        dq_s0 = nc.alloc_sbuf_tensor_at("dq_s0", [128, 2048], F32, offset=R1 + 8192).ap().tensor
        dq_o0 = nc.alloc_sbuf_tensor_at("dq_o0", [128, 2048], BF16, offset=R1 + 16384).ap().tensor
        s_st = [nc.alloc_sbuf_tensor_at(f"s_st{i}", [128, D], F32, offset=R1 + 20480 + 4096 * i).ap().tensor
                for i in range(2)]
        h_fm = nc.alloc_sbuf_tensor_at("h_fm", [128, NCH * S], BF16, offset=R4).ap().tensor
        w2a = nc.alloc_sbuf_tensor_at("w2a", [128, 16 * D], BF16, offset=R3).ap().tensor
        w2b = nc.alloc_sbuf_tensor_at("w2b", [128, 16 * D], BF16, offset=R5).ap().tensor
        y_sl = nc.alloc_sbuf_tensor_at("y_sl", [128, NFC * 1024], BF16, offset=R1).ap().tensor
        with ExitStack() as ps:
            idn_f = sb(ps, "idn_f", 128, F32)
            msk_f = sb(ps, "msk_f", 128, F32)
            cs_a = sb(ps, "cs_a", S, F32)
            cs_b = sb(ps, "cs_b", S, F32)
            sch = Sched(cts)
            for ti, (hi, lo, sg, dst_t) in enumerate(
                    (("ths_hi", "ths_lo", "sgs", sin_t),
                     ("thc_hi", "thc_lo", "sgc", cos_t))):
                if ti == 1:
                    sch.wait("sync", "cVE")
                sch.op("sync", lambda e, hi=hi: e.dma_start(
                    out=cs_a[:], in_=ang_d[hi][:]), inc="dX0", k=16)
                sch.op("sync", lambda e, lo=lo: e.dma_start(
                    out=cs_b[:], in_=ang_d[lo][:]), inc="dX0", k=16)
                sch.wait("vector", "dX0")
                sch.op("vector", lambda e: e.tensor_add(cs_a[:], cs_a[:], cs_b[:]), inc="cVE")
                ve_th = sch.snap("cVE")
                sch.wait("sync", "cVE", ve_th)
                sch.op("sync", lambda e, sg=sg: e.dma_start(
                    out=cs_b[:], in_=ang_d[sg][:]), inc="dX0", k=16)
                sch.wait("scalar", "cVE", ve_th)
                sch.op("scalar", lambda e: e.activation(
                    cs_a[:], cs_a[:], AF.Sin), inc="cSC")
                sch.wait("vector", "cSC")
                sch.wait("vector", "dX0")
                sch.op("vector", lambda e, dst_t=dst_t: e.tensor_mul(
                    dst_t[:], cs_a[:], cs_b[:]), inc="cVE")
            for fc in range(NCH):
                sch.op("sync", lambda e, fc=fc: e.dma_start(
                    out=gam_cm[:, fc:fc + 1],
                    in_=bass.AP(gam_d, 128 * fc, [[1, 128], [1, 1]])), inc="dC", k=16)
                sch.op("sync", lambda e, fc=fc: e.dma_start(
                    out=bet_cm[:, fc:fc + 1],
                    in_=bass.AP(bet_d, 128 * fc, [[1, 128], [1, 1]])), inc="dC", k=16)
            sch.op("sync", lambda e: e.dma_start(
                out=b2h_rep[:], in_=bass.AP(b2_d, 0, [[0, 128], [1, D]])), inc="dC", k=16)
            for oc in range(NFC):
                sch.op("sync", lambda e, oc=oc: e.dma_start(
                    out=b1_cm[:, oc:oc + 1],
                    in_=bass.AP(b1_d, 128 * oc, [[1, 128], [1, 1]])), inc="dC", k=16)
            sch.op("sync", lambda e: e.dma_start(out=idn_f[:], in_=idn_d[:]), inc="dC", k=16)
            sch.op("sync", lambda e: e.dma_start(out=msk_f[:], in_=msk_d[:]), inc="dC", k=16)
            sch.wait("vector", "dC")
            sch.op("vector", lambda e: e.tensor_copy(idn_bf[:], idn_f[:]), inc="cVE")
            sch.op("vector", lambda e: e.tensor_copy(idn_f32[:], idn_f[:]), inc="cVE")
            sch.op("vector", lambda e: e.tensor_copy(msk_bf[:], msk_f[:]), inc="cVE")
            sch.op("vector", lambda e: e.tensor_scalar_mul(b2h_rep[:], b2h_rep[:], 0.5), inc="cVE")

            sch.op("gpsimd", lambda e: e.memset(ones_bf[:], 1.0), inc="cGP")
            sch.op("gpsimd", lambda e: e.memset(selq[:], 0.0), inc="cGP")
            sch.wait("gpsimd", "cGP")
            for j in range(4):
                sch.op("gpsimd", lambda e, j=j: e.memset(
                    selq[32 * j:32 * (j + 1), j:j + 1], 1.0), inc="cGP")
            sch.op("gpsimd", lambda e: e.memset(biasc[:, 0:1], EPS), inc="cGP")
            sch.op("gpsimd", lambda e: e.memset(biasc[:, 1:2], LN8), inc="cGP")
            sch.op("gpsimd", lambda e: e.memset(ones_row[:], 1.0), inc="cGP")
            run_block(nc, "b0", sch)
        if upto < 1:
            return nc

        with ExitStack() as ps:
            xbuf = [sb(ps, f"xb{i}", D, F32) for i in range(2)]
            xnb = [sb(ps, f"xnb{i}", D, BF16) for i in range(2)]
            sqsc = [sb(ps, f"sqsc{i}", D, F32) for i in range(2)]
            st2 = sb(ps, "st2", 80, F32)
            pT = [ps.enter_context(nc.psum_tensor(f"pT{i}", [128, D], BF16)) for i in range(2)]
            pTf = ps.enter_context(nc.psum_tensor("pTf", [128, 128], F32))

            sch = Sched(cts)
            for fc in range(NCH):
                sch.op("gpsimd", lambda e, fc=fc: e.dma_start(
                    out=wv_bf[:, 1024 * fc:1024 * (fc + 1)],
                    in_=bass.AP(wv_d, 128 * fc * D, [[D, 128], [1, D]])), inc="dG0", k=16)
            dg_wv = sch.snap("dG0")
            ve_xn = {}
            sc_sq = {}
            sc_evac2 = {}
            pe_tp = {}
            for tt in range(NT):
                b = tt % 2
                xb, xn1, pt = xbuf[b], xnb[b], pT[b]
                if tt >= 2:
                    sch.wait("sync", "cVE", ve_xn[tt - 2])
                    sch.wait("sync", "cSC", sc_sq[tt - 2])
                sch.op("sync", lambda e, xb=xb, tt=tt: e.dma_start(
                    out=xb[:], in_=x_d[128 * tt:128 * (tt + 1), :]), inc=f"dX{b}", k=16)
                v = sch.snap(f"dX{b}")
                sch.wait("vector", f"dX{b}", v)
                sch.op("vector", lambda e, xb=xb, tt=tt: e.reduce_sum(
                    st2[:, tt:tt + 1], xb[:], axis=AX.X), inc="cVE")
                sch.wait("scalar", f"dX{b}", v)
                sch.op("scalar", lambda e, xb=xb, tt=tt, b=b: e.activation(
                    sqsc[b][:], xb[:], AF.Square,
                    accum_out=st2[:, 16 + tt:17 + tt]), inc="cSC")
                sc_sq[tt] = sch.snap("cSC")
                sch.wait("vector", "cSC", sc_sq[tt])
                sch.wait("vector", "cVE")
                sch.op("vector", lambda e, tt=tt: e.tensor_scalar_mul(
                    negmu_a[:, tt:tt + 1], st2[:, tt:tt + 1], -1.0 / D), inc="cVE")
                sch.op("vector", lambda e, tt=tt: e.tensor_scalar_mul(
                    st2[:, 32 + tt:33 + tt], st2[:, 16 + tt:17 + tt], 1.0 / D), inc="cVE")
                ve_1 = sch.snap("cVE")
                sch.wait("scalar", "cVE", ve_1)
                sch.op("scalar", lambda e, tt=tt: e.activation(
                    st2[:, 48 + tt:49 + tt], negmu_a[:, tt:tt + 1], AF.Square), inc="cSC")
                sc_1 = sch.snap("cSC")
                sch.wait("vector", "cSC", sc_1)
                sch.wait("vector", "cVE")
                sch.op("vector", lambda e, tt=tt: e.scalar_tensor_tensor(
                    st2[:, 48 + tt:49 + tt], st2[:, 48 + tt:49 + tt], -1.0,
                    st2[:, 32 + tt:33 + tt], ALU.mult, ALU.add), inc="cVE")
                ve_2 = sch.snap("cVE")
                sch.wait("scalar", "cVE", ve_2)
                sch.op("scalar", lambda e, tt=tt: e.activation(
                    st2[:, 32 + tt:33 + tt], st2[:, 48 + tt:49 + tt], AF.Ln,
                    bias=biasc[:, 0:1]), inc="cSC")
                sch.wait("scalar", "cSC")
                sch.op("scalar", lambda e, tt=tt: e.activation(
                    r1_a[:, tt:tt + 1], st2[:, 32 + tt:33 + tt], AF.Exp, scale=-0.5), inc="cSC")
                sch.op("scalar", lambda e, tt=tt: e.activation(
                    st2[:, 64 + tt:65 + tt], st2[:, 32 + tt:33 + tt], AF.Exp, scale=-1.0), inc="cSC")
                sc_2 = sch.snap("cSC")
                sch.wait("vector", "cSC", sc_2)
                sch.op("vector", lambda e, tt=tt: e.tensor_mul(
                    st2[:, 48 + tt:49 + tt], st2[:, 48 + tt:49 + tt],
                    st2[:, 64 + tt:65 + tt]), inc="cVE")
                ve_3 = sch.snap("cVE")
                sch.wait("scalar", "cVE", ve_3)
                sch.op("scalar", lambda e, tt=tt: e.activation(
                    st2[:, 48 + tt:49 + tt], st2[:, 48 + tt:49 + tt], AF.Ln,
                    bias=biasc[:, 0:1]), inc="cSC")
                sch.wait("scalar", "cSC")
                sch.op("scalar", lambda e, tt=tt: e.activation(
                    st2[:, 48 + tt:49 + tt], st2[:, 48 + tt:49 + tt], AF.Exp,
                    scale=-0.5), inc="cSC")
                sc_3 = sch.snap("cSC")
                sch.wait("vector", "cSC", sc_3)
                sch.op("vector", lambda e, tt=tt: e.tensor_copy(
                    g2_a[:, tt:tt + 1], st2[:, 48 + tt:49 + tt]), inc="cVE")
                sch.op("vector", lambda e, tt=tt: e.tensor_mul(
                    rg_a[:, tt:tt + 1], r1_a[:, tt:tt + 1],
                    st2[:, 48 + tt:49 + tt]), inc="cVE")
                sch.wait("vector", "cVE")
                if tt >= 2:
                    sch.wait("vector", "cPE", pe_tp[tt - 2])
                sch.op("vector", lambda e, xb=xb, xn1=xn1, tt=tt: e.tensor_scalar(
                    xn1[:], xb[:], negmu_a[:, tt:tt + 1], r1_a[:, tt:tt + 1],
                    ALU.add, ALU.mult), inc="cVE")
                ve_xn[tt] = sch.snap("cVE")
                sch.wait("tensor", "cVE", ve_xn[tt])
                if tt >= 2:
                    sch.wait("tensor", "cSC", sc_evac2[tt - 2])
                for fc in range(NCH):
                    sch.op("tensor", lambda e, xn1=xn1, pt=pt, fc=fc: e.transpose(
                        pt[:, 128 * fc:128 * (fc + 1)],
                        xn1[:, 128 * fc:128 * (fc + 1)], idn_bf[:]), inc="cPE")
                pe_tp[tt] = sch.snap("cPE")
                sch.wait("scalar", "cPE", pe_tp[tt])
                sch.op("scalar", lambda e, pt=pt, tt=tt: e.activation(
                    fap(xn_fm, [[S, NCH], [1, 128]], offset=128 * tt),
                    fap(pt, [[128, NCH], [1, 128]]), AF.Copy), inc="cSC")
                sc_evac2[tt] = sch.snap("cSC")
            ve_stats = sch.snap("cVE")
            sc_all_sq = sch.snap("cSC")
            sch.wait("gpsimd", "cVE", ve_stats)
            sch.wait("gpsimd", "cSC", sc_all_sq)
            sch.op("gpsimd", lambda e: e.memset(sqsc[0][:, 0:128], 0.0), inc="cGP")
            sch.wait("vector", "cGP")
            sch.op("vector", lambda e: e.tensor_copy(
                bass.AP(sqsc[0], 0, [[1024, 128], [1, 16]]), g2_a[:]), inc="cVE")
            ve_g2c = sch.snap("cVE")
            sch.wait("tensor", "cVE", ve_g2c)
            sch.op("tensor", lambda e: e.transpose(
                pTf[:, 0:128], bass.AP(sqsc[0], 0, [[1024, 128], [1, 128]]),
                idn_f32[:]), inc="cPE")
            pe_g2t = sch.snap("cPE")
            sch.wait("vector", "cPE", pe_g2t)
            sch.op("vector", lambda e: e.tensor_copy(
                bass.AP(sqsc[1], 0, [[1024, 16], [1, 128]]),
                bass.AP(pTf, 0, [[128, 16], [1, 128]])), inc="cVE")
            ve_g2e = sch.snap("cVE")
            sch.wait("scalar", "cVE", ve_g2e)
            sch.op("scalar", lambda e: e.dma_start(
                out=bass.AP(g2dram, 0, [[128, 16], [1, 128]]),
                in_=bass.AP(sqsc[1], 0, [[1024, 16], [1, 128]])), inc="dH", k=16)
            b1_sc_all = sch.snap("cSC")
            run_block(nc, "b1", sch)
        if upto < 2:
            return nc

        with ExitStack() as ps:
            pbig = ps.enter_context(nc.psum_tensor("pbig2", [128, 2048], F32))
            p4 = ps.enter_context(nc.psum_tensor("p4_2", [128, 512], F32))
            p5 = ps.enter_context(nc.psum_tensor("p5_2", [128, 512], F32))
            p6 = ps.enter_context(nc.psum_tensor("p6_2", [128, 512], F32))
            p7 = ps.enter_context(nc.psum_tensor("p7_2", [128, 512], F32))

            sch = Sched(cts)
            sch.wait("vector", "dC")
            sch.wait("gpsimd", "dC")
            wE = [sb(ps, f"wE{i}", NCH * 128, BF16) for i in range(2)]
            wO = [sb(ps, f"wO{i}", NCH * 128, BF16) for i in range(2)]
            wstg = [sb(ps, f"wstg{i}", 1024, F32) for i in range(2)]
            hs = ExitStack()
            g2_rep = hs.enter_context(nc.sbuf_tensor("g2_rep", [128, S], BF16))
            h_scr = [hs.enter_context(nc.sbuf_tensor(f"h_scr{i}", [128, S], BF16)) for i in range(2)]

            sch.wait("gpsimd", "dH")
            sch.op("gpsimd", lambda e: e.dma_start(
                out=g2_rep[:], in_=bass.AP(g2dram, 0, [[0, 128], [1, S]])), inc="dGW", k=16)
            dh_rep = sch.snap("dGW")
            sch.wait("gpsimd", "dGW", dh_rep)
            gp_hm = {}
            ve_hts = {}
            ds_hsp = {}

            def emit_h_mul(fc):
                if fc < 2:
                    sch.wait("gpsimd", "cSC", b1_sc_all)
                if fc >= 2:
                    sch.wait("gpsimd", "cVE", ve_hts[fc - 2])
                    sch.wait("gpsimd", f"dS{fc % 2}", ds_hsp[fc - 2])
                sch.op("gpsimd", lambda e, fc=fc: e.tensor_mul(
                    h_scr[fc % 2][:], xn_fm[:, S * fc:S * (fc + 1)], g2_rep[:]), inc="cGP")
                gp_hm[fc] = sch.snap("cGP")

            def emit_h_ts(fc):
                sch.wait("vector", "cGP", gp_hm[fc])
                if fc >= 2:
                    sch.wait("vector", f"dS{fc % 2}", ds_hsp[fc - 2])
                sch.op("vector", lambda e, fc=fc: e.tensor_scalar(
                    h_scr[fc % 2][:], h_scr[fc % 2][:], gam_cm[:, fc:fc + 1],
                    bet_cm[:, fc:fc + 1], ALU.mult, ALU.add), inc="cVE")
                ve_hts[fc] = sch.snap("cVE")
                sch.wait("scalar", "cVE", ve_hts[fc])
                sch.op("scalar", lambda e, fc=fc: e.dma_start(
                    out=bass.AP(hdram, S * fc, [[NCH * S, 128], [1, S]]),
                    in_=h_scr[fc % 2][:]), inc=f"dS{fc % 2}", k=16)
                ds_hsp[fc] = sch.snap(f"dS{fc % 2}")

            emit_h_mul(0)
            emit_h_mul(1)
            ve_vev = {}
            sch.wait("tensor", "dG0", dg_wv)
            for tt in range(NT):
                za = p4[:] if tt % 2 == 0 else p6[:]
                zb = p5[:] if tt % 2 == 0 else p7[:]
                sch.wait("tensor", "cSC", sc_evac2[tt])
                if tt >= 2:
                    sch.wait("tensor", "cVE", ve_vev[tt - 2])
                for fc in range(NCH):
                    sch.op("tensor", lambda e, fc=fc, tt=tt, za=za: e.matmul(
                        za, xn_fm[:, S * fc + 128 * tt:S * fc + 128 * (tt + 1)],
                        wv_bf[:, 1024 * fc:1024 * fc + 512],
                        start=(fc == 0), stop=(fc == NCH - 1)), inc="cPE")
                    sch.op("tensor", lambda e, fc=fc, tt=tt, zb=zb: e.matmul(
                        zb, xn_fm[:, S * fc + 128 * tt:S * fc + 128 * (tt + 1)],
                        wv_bf[:, 1024 * fc + 512:1024 * (fc + 1)],
                        start=(fc == 0), stop=(fc == NCH - 1)), inc="cPE")
                pe_v = sch.snap("cPE")
                sch.wait("vector", "cPE", pe_v)
                sch.op("vector", lambda e, tt=tt, za=za: e.tensor_copy(
                    v_tm[:, 1024 * tt:1024 * tt + 512], za), inc="cVE")
                sch.op("vector", lambda e, tt=tt, zb=zb: e.tensor_copy(
                    v_tm[:, 1024 * tt + 512:1024 * (tt + 1)], zb), inc="cVE")
                ve_vev[tt] = sch.snap("cVE")
                if tt < NCH:
                    emit_h_ts(tt)
                    if tt + 2 < NCH:
                        emit_h_mul(tt + 2)
            ve_v_all = sch.snap("cVE")
            hs.close()
            qe_s = sb(ps, "qe_s", S, BF16)
            qo_s = sb(ps, "qo_s", S, BF16)
            qeo_bf = [sb(ps, f"qeo{i}", 1024, BF16) for i in range(2)]
            rp_t = [sb(ps, f"rp_t{i}", 1024, BF16) for i in range(2)]
            nsq_sm = [sb(ps, f"nsq_sm{i}", 1024, BF16) for i in range(2)]
            sch.wait("vector", "cGP", gp_hm[NCH - 1])
            sch.wait("vector", "dS0", ds_hsp[NCH - 2])
            sch.wait("scalar", "dS1", ds_hsp[NCH - 1])

            ve_rope = {}
            sc_evac = {}
            gp_rope = {}
            dr_rep = {}
            ve_slab = {}
            pe_qk = {}
            dw_slab = {}

            def emit_slab_dmas(it):
                ti2, cp2 = it // 4, it % 4
                w_src = wq_d if ti2 == 0 else wk_d
                for pi, coff in enumerate((128 * cp2, 512 + 128 * cp2)):
                    if (it - 1, pi) in ve_slab:
                        sch.wait("sync", "cVE", ve_slab[(it - 1, pi)])
                    for fc in range(NCH):
                        sch.op("sync", lambda e, fc=fc, coff=coff, w_src=w_src, pi=pi: e.dma_start(
                            out=wstg[pi][:, 128 * fc:128 * (fc + 1)],
                            in_=bass.AP(w_src, 128 * fc * D + coff,
                                        [[D, 128], [1, 128]])), inc=f"dW{pi}", k=16)
                    dw_slab[(it, pi)] = sch.snap(f"dW{pi}")

            def emit_slab_casts(it):
                wp = it % 2
                for pi in range(2):
                    sch.wait("vector", f"dW{pi}", dw_slab[(it, pi)])
                    if it >= 2:
                        sch.wait("vector", "cPE", pe_qk[it - 2])
                    sch.op("vector", lambda e, pi=pi, wp=wp,
                           wdst=(wE if pi == 0 else wO): e.tensor_copy(
                        wdst[wp][:], wstg[pi][:]), inc="cVE")
                    ve_slab[(it, pi)] = sch.snap("cVE")

            first_nmax = {0: True, 1: True}
            gp_sq = {}
            ve_nr = {}
            pe_nr = {}

            def emit_sel(it2, tg2):
                ti2 = it2 // 4
                sch.wait("tensor", "cGP", gp_sq[(it2, tg2)])
                if ve_nr:
                    sch.wait("tensor", "cVE", ve_nr[max(ve_nr)])
                else:
                    sch.wait("tensor", "cVE", ve_v_all)
                sch.op("tensor", lambda e, tg2=tg2: e.matmul(
                    p7[0:4, :], selq[:, 0:4], nsq_sm[tg2 % 2][:, 0:512],
                    start=True, stop=False), inc="cPE")
                sch.op("tensor", lambda e, tg2=tg2: e.matmul(
                    p7[0:4, :], selq[:, 0:4], nsq_sm[tg2 % 2][:, 512:1024],
                    start=False, stop=True), inc="cPE")
                pe_nr[(it2, tg2)] = sch.snap("cPE")
                sch.wait("vector", "cPE", pe_nr[(it2, tg2)])
                sch.op("vector", lambda e: e.reduce_max(
                    stat[0:4, 12:13], p7[0:4, :], axis=AX.X), inc="cVE")
                sch.wait("vector", "cVE")
                if first_nmax[ti2]:
                    sch.op("vector", lambda e, ti2=ti2: e.tensor_copy(
                        stat[0:4, 10 + ti2:11 + ti2], stat[0:4, 12:13]), inc="cVE")
                    first_nmax[ti2] = False
                else:
                    sch.op("vector", lambda e, ti2=ti2: e.tensor_max(
                        stat[0:4, 10 + ti2:11 + ti2],
                        stat[0:4, 10 + ti2:11 + ti2], stat[0:4, 12:13]), inc="cVE")
                ve_nr[(it2, tg2)] = sch.snap("cVE")
            emit_slab_dmas(0)
            emit_slab_casts(0)
            for ti, (w_d_, att) in enumerate(((wq_d, q_att), (wk_d, k_att))):
                for cp in range(4):
                    it = ti * 4 + cp
                    wp = it % 2
                    if it + 1 < 8:
                        emit_slab_dmas(it + 1)
                    sch.wait("tensor", "cVE", ve_slab[(it, 1)])
                    if it == 0:
                        sch.wait("tensor", "cVE", ve_v_all)
                    for tgq in range(4):
                        sl = tgq % 2
                        pe_ = pbig[:, 1024 * sl:1024 * sl + 512]
                        po_ = pbig[:, 1024 * sl + 512:1024 * sl + 1024]
                        key = (it, tgq - 2) if tgq >= 2 else (it - 1, tgq + 2)
                        if key in sc_evac:
                            sch.wait("tensor", "cSC", sc_evac[key])
                        for fc in range(NCH):
                            sch.op("tensor", lambda e, fc=fc, tgq=tgq, pe_=pe_, wp=wp: e.matmul(
                                pe_, wE[wp][:, 128 * fc:128 * (fc + 1)],
                                xn_fm[:, S * fc + 512 * tgq:S * fc + 512 * (tgq + 1)],
                                start=(fc == 0), stop=(fc == NCH - 1)), inc="cPE")
                        for fc in range(NCH):
                            sch.op("tensor", lambda e, fc=fc, tgq=tgq, po_=po_, wp=wp: e.matmul(
                                po_, wO[wp][:, 128 * fc:128 * (fc + 1)],
                                xn_fm[:, S * fc + 512 * tgq:S * fc + 512 * (tgq + 1)],
                                start=(fc == 0), stop=(fc == NCH - 1)), inc="cPE")
                        pe_now = sch.snap("cPE")
                        rp = rp_t[tgq % 2]
                        qeo = qeo_bf[tgq % 2]
                        sch.wait("scalar", "cPE", pe_now)
                        key2 = (it, tgq - 2) if tgq >= 2 else (it - 1, tgq + 2)
                        if key2 in ve_rope:
                            sch.wait("scalar", "cVE", ve_rope[key2])
                        if key2 in gp_sq:
                            sch.wait("scalar", "cGP", gp_sq[key2])
                        sch.op("scalar", lambda e, pe_=pe_, qeo=qeo: e.activation(
                            qeo[:, 0:512], pe_, AF.Copy), inc="cSC")
                        sch.op("scalar", lambda e, po_=po_, qeo=qeo: e.activation(
                            qeo[:, 512:1024], po_, AF.Copy), inc="cSC")
                        sc_evac[(it, tgq)] = sch.snap("cSC")
                        if (it - 1, tgq) in dr_rep:
                            sch.wait("vector", f"dR{(it - 1) % 2}", dr_rep[(it - 1, tgq)])
                        sch.wait("vector", "cSC", sc_evac[(it, tgq)])
                        sch.op("vector", lambda e, qeo=qeo, tgq=tgq, rp=rp: e.tensor_mul(
                            rp[:, 0:512], qeo[:, 0:512], cos_t[:, 512 * tgq:512 * (tgq + 1)]), inc="cVE")
                        sch.op("vector", lambda e, qeo=qeo, tgq=tgq, rp=rp: e.tensor_mul(
                            rp[:, 512:1024], qeo[:, 512:1024], sin_t[:, 512 * tgq:512 * (tgq + 1)]), inc="cVE")
                        sch.wait("vector", "cVE")
                        sch.op("vector", lambda e, tgq=tgq, rp=rp: e.tensor_sub(
                            qe_s[:, 512 * tgq:512 * (tgq + 1)],
                            rp[:, 0:512], rp[:, 512:1024]), inc="cVE")
                        sch.wait("vector", "cVE")
                        sch.op("vector", lambda e, qeo=qeo, tgq=tgq, rp=rp: e.tensor_mul(
                            rp[:, 0:512], qeo[:, 512:1024], cos_t[:, 512 * tgq:512 * (tgq + 1)]), inc="cVE")
                        sch.op("vector", lambda e, qeo=qeo, tgq=tgq, rp=rp: e.tensor_mul(
                            rp[:, 512:1024], qeo[:, 0:512], sin_t[:, 512 * tgq:512 * (tgq + 1)]), inc="cVE")
                        ve_rope[(it, tgq)] = sch.snap("cVE")
                        sch.wait("vector", "cVE")
                        sch.op("vector", lambda e, tgq=tgq, rp=rp: e.tensor_add(
                            qo_s[:, 512 * tgq:512 * (tgq + 1)],
                            rp[:, 0:512], rp[:, 512:1024]), inc="cVE")
                        gp_rope[(it, tgq)] = sch.snap("cVE")
                        sch.wait("gpsimd", "cVE", gp_rope[(it, tgq)])
                        gkey = (it, tgq - 2) if tgq >= 2 else (it - 1, tgq + 2)
                        if gkey in pe_nr:
                            sch.wait("gpsimd", "cPE", pe_nr[gkey])
                        sch.op("gpsimd", lambda e, qeo=qeo, tgq=tgq: e.tensor_mul(
                            nsq_sm[tgq % 2][:], qeo[:], qeo[:]), inc="cGP")
                        gp_sq[(it, tgq)] = sch.snap("cGP")
                        if tgq >= 1:
                            emit_sel(it, tgq - 1)
                        elif it >= 1:
                            emit_sel(it - 1, 3)
                    if it + 1 < 8:
                        emit_slab_casts(it + 1)
                    ve_rope_all = sch.snap("cVE")
                    sch.wait("scalar", "cVE", ve_rope_all)
                    for j in range(4):
                        h = 4 * cp + j
                        m, r = h // 2, 64 * (h % 2)
                        sch.op("scalar", lambda e, j=j, m=m, r=r, att=att: e.dma_start(
                            out=att[r:r + 32, S * m:S * (m + 1)],
                            in_=qe_s[32 * j:32 * (j + 1), :]), inc=f"dR{it % 2}", k=16)
                        sch.op("scalar", lambda e, j=j, m=m, r=r, att=att: e.dma_start(
                            out=att[r + 32:r + 64, S * m:S * (m + 1)],
                            in_=qo_s[32 * j:32 * (j + 1), :]), inc=f"dR{it % 2}", k=16)
                    for tgq in range(4):
                        dr_rep[(it, tgq)] = sch.snap(f"dR{it % 2}")
                    pe_qk[it] = sch.snap("cPE")

            emit_sel(7, 3)
            sch.wait("vector", "cVE", ve_nr[max(ve_nr)])
            for ti2 in range(2):
                sch.op("vector", lambda e, ti2=ti2: e.tensor_copy(
                    statb[0:4, ti2:ti2 + 1], stat[0:4, 10 + ti2:11 + ti2]), inc="cVE")
            ve_cmb = sch.snap("cVE")
            sch.wait("tensor", "cVE", ve_cmb)
            for ti2 in range(2):
                sch.op("tensor", lambda e, ti2=ti2: e.matmul(
                    p6[0:1, ti2:ti2 + 1], ones_row[0:4, 0:1], statb[0:4, ti2:ti2 + 1],
                    start=True, stop=True), inc="cPE")
            pe_cmb = sch.snap("cPE")
            sch.wait("vector", "cPE", pe_cmb)
            sch.op("vector", lambda e: e.tensor_copy(
                stat[:1, 10:12], p6[0:1, 0:2]), inc="cVE")
            sch.wait("scalar", "cVE")
            sch.op("scalar", lambda e: e.activation(
                stat[:1, 13:14], stat[:1, 10:11], AF.Ln), inc="cSC")
            sch.wait("scalar", "cSC")
            sch.op("scalar", lambda e: e.activation(
                stat[:1, 14:15], stat[:1, 11:12], AF.Ln), inc="cSC")
            sch.wait("vector", "cSC")
            sch.op("vector", lambda e: e.tensor_add(
                stat[:1, 15:16], stat[:1, 13:14], stat[:1, 14:15]), inc="cVE")
            sch.wait("scalar", "cVE")
            sch.op("scalar", lambda e: e.activation(
                stat[:1, 12:13], stat[:1, 15:16], AF.Exp, scale=0.5,
                bias=biasc[0:1, 1:2]), inc="cSC")
            sch.wait("vector", "cSC")
            sch.op("vector", lambda e: e.tensor_scalar(
                stat[:1, 12:13], stat[:1, 12:13], 1.0, -1.0, ALU.add, ALU.mult), inc="cVE")
            sch.wait("vector", "cVE")
            sch.op("vector", lambda e: e.tensor_copy(statb[:1, 0:1], stat[:1, 12:13]), inc="cVE")
            ve_ub = sch.snap("cVE")
            sch.wait("tensor", "cVE", ve_ub)
            sch.op("tensor", lambda e: e.matmul(
                p6[:, 0:1], ones_row[0:1, :], statb[0:1, 0:1],
                start=True, stop=True), inc="cPE")
            sch.wait("vector", "cPE")
            sch.op("vector", lambda e: e.tensor_copy(negub[:], p6[:, 0:1]), inc="cVE")
            run_block(nc, "b2", sch)

        with ExitStack() as ps:
            pbig = ps.enter_context(nc.psum_tensor("pbig3", [128, 2048], F32))
            p4 = ps.enter_context(nc.psum_tensor("p4_3", [128, 512], F32))
            p5 = ps.enter_context(nc.psum_tensor("p5_3", [128, 512], F32))
            p6 = ps.enter_context(nc.psum_tensor("p6_3", [128, 512], F32))
            p7 = ps.enter_context(nc.psum_tensor("p7_3", [128, 512], F32))
            wo_bf = sb(ps, "wo_bf", NCH * D, BF16)
            pt_b = [sb(ps, f"ptb{i}", 2048, BF16) for i in range(3)]
            ax = ExitStack()
            lrec = sb(ax, "lrec", 512, F32)
            lrecb = sb(ax, "lrecb", 512, BF16)
            rep_sb = sb(ax, "rep_sb", 512, F32)

            sch = Sched(cts)
            sch.wait("tensor", "dR0")
            sch.wait("tensor", "dR1")
            sch.wait("sync", "dR0")
            sch.wait("sync", "dR1")
            sch.wait("gpsimd", "dR0")
            sch.wait("gpsimd", "dR1")
            for fc in range(NCH):
                sch.op("gpsimd", lambda e, fc=fc: e.dma_start(
                    out=wo_bf[:, 1024 * fc:1024 * (fc + 1)],
                    in_=bass.AP(wo_d, 128 * fc * D, [[D, 128], [1, D]])), inc="dGW", k=16)
            dg_wo = sch.snap("dGW")

            with ExitStack() as ds:
                dq_i = [dq_i0] * 2
                dq_s = [dq_s0] * 2
                dq_o = [dq_o0] * 2

                dq_jobs = []
                for c in range(NCH):
                    for hlf in range(2):
                        dq_jobs.append(("w1", c, hlf))
                for c in range(NFC):
                    dq_jobs.append(("w2", c, 0))
                ndq = len(dq_jobs)
                dq_ve = {}
                dq_dg = {}

                def emit_dq_job(ji):
                    kind, c, hlf = dq_jobs[ji]
                    if kind == "w1":
                        cols = 2048
                        qsrc = bass.AP(q1_d, 128 * c * F + 2048 * hlf, [[F, 128], [1, 2048]])
                        ssrc = bass.AP(s1_d, c * F + 2048 * hlf, [[0, 128], [1, 2048]])
                        dsth = bass.AP(w1dram, 16384 * hlf + 128 * c,
                                       [[NCH * F, 128], [1024, 16], [1, 128]])
                    else:
                        cols = 1024
                        qsrc = bass.AP(q2_d, 128 * c * D, [[D, 128], [1, 1024]])
                        ssrc = bass.AP(s2_d, c * D, [[0, 128], [1, 1024]])
                        dsth = bass.AP(w2dram, 128 * c * D, [[D, 128], [1, 1024]])
                    bi = ji % 2
                    if ji >= 1:
                        sch.wait("sync", "cVE", dq_ve[ji - 1])
                    sch.op("sync", lambda e, qsrc=qsrc, cols=cols, bi=bi: e.dma_start(
                        out=dq_i[bi][:, 0:cols], in_=qsrc), inc=f"dW{bi}", k=16)
                    sch.op("sync", lambda e, ssrc=ssrc, cols=cols, bi=bi: e.dma_start(
                        out=dq_s[bi][:, 0:cols], in_=ssrc), inc=f"dW{bi}", k=16)
                    v = sch.snap(f"dW{bi}")
                    if ji >= 1:
                        sch.wait("vector", "dW3", dq_dg[ji - 1])
                    sch.wait("vector", f"dW{bi}", v)
                    sch.op("vector", lambda e, cols=cols, bi=bi: e.scalar_tensor_tensor(
                        dq_o[bi][:, 0:cols], dq_i[bi][:, 0:cols], -128.0,
                        dq_s[bi][:, 0:cols], ALU.add, ALU.mult), inc="cVE")
                    dq_ve[ji] = sch.snap("cVE")
                    sch.wait("sync", "cVE", dq_ve[ji])
                    sch.op("sync", lambda e, dsth=dsth, cols=cols, bi=bi: e.dma_start(
                        out=dsth, in_=dq_o[bi][:, 0:cols]), inc="dW3", k=16)
                    dq_dg[ji] = sch.snap("dW3")

                dq_next = 0
                dq_quota = [0]
                sc_exp = {}
                pe_avl = {}
                gp_msk = {}
                ve_omul_par = [0, 0]
                ve_repevac_par = [0, 0]
                obank = [p4, p5]
                lbank = [p6, p7]
                pending_ep = [None]
                ve_rc_box = [0]
                gidx_c = [0]
                for qg in range(NQG):
                    for m in range(8):
                        K = 4 * qg + 4
                        gp = gidx_c[0] % 2

                        def emit_scores_exp(kt, qg=qg, m=m, K=K):
                            half = kt % 2
                            pt = pt_b[kt % 3]
                            sc0 = pbig[:, 1024 * half:1024 * half + 512]
                            sc1 = pbig[:, 1024 * half + 512:1024 * half + 1024]
                            prevexp = None
                            if kt >= 2:
                                prevexp = sc_exp[(qg, m, kt - 2)]
                            else:
                                pg = (qg, m - 1) if m > 0 else ((qg - 1, 7) if qg > 0 else None)
                                if pg is not None:
                                    pk = 4 * pg[0] + 4
                                    kk = pk - 2 + kt
                                    if (pg[0], pg[1], kk) in sc_exp:
                                        prevexp = sc_exp[(pg[0], pg[1], kk)]
                            if prevexp is not None:
                                sch.wait("tensor", "cSC", prevexp)
                            for jj in range(2):
                                r = 64 * jj
                                sch.op("tensor", lambda e, jj=jj, m=m, r=r, kt=kt, qg=qg,
                                       dst=(sc0 if jj == 0 else sc1): e.matmul(
                                    dst,
                                    k_att[r:r + 64, S * m + 128 * kt:S * m + 128 * (kt + 1)],
                                    q_att[r:r + 64, S * m + 512 * qg:S * m + 512 * (qg + 1)],
                                    start=True, stop=True, tile_position=(r, 0)), inc="cPE")
                            pe_sc = sch.snap("cPE")
                            sch.wait("scalar", "cPE", pe_sc)
                            prevuse = None
                            if (qg, m, kt - 3) in pe_avl:
                                prevuse = pe_avl[(qg, m, kt - 3)]
                            else:
                                pg = (qg, m - 1) if m > 0 else ((qg - 1, 7) if qg > 0 else None)
                                if pg is not None:
                                    pk = 4 * pg[0] + 4
                                    kk = pk - 3 + kt
                                    if (pg[0], pg[1], kk) in pe_avl:
                                        prevuse = pe_avl[(pg[0], pg[1], kk)]
                            if prevuse is not None:
                                sch.wait("scalar", "cPE", prevuse)
                            sch.op("scalar", lambda e, pt=pt, half=half: e.activation(
                                pt[:, 0:1024], pbig[:, 1024 * half:1024 * half + 1024],
                                AF.Exp, bias=negub[:, 0:1], scale=0.125), inc="cSC")
                            sc_exp[(qg, m, kt)] = sch.snap("cSC")
                            if kt >= 4 * qg:
                                off = (kt - 4 * qg) * 128
                                sch.wait("gpsimd", "cSC", sc_exp[(qg, m, kt)])
                                for jj in range(2):
                                    base = 512 * jj
                                    if off > 0:
                                        sch.op("gpsimd", lambda e, pt=pt, base=base, off=off: e.memset(
                                            pt[:, base:base + off], 0.0), inc="cGP")
                                    sch.op("gpsimd", lambda e, pt=pt, base=base, off=off: e.tensor_mul(
                                        pt[:, base + off:base + off + 128],
                                        pt[:, base + off:base + off + 128], msk_bf[:]), inc="cGP")
                                gp_msk[(qg, m, kt)] = sch.snap("cGP")

                        def emit_av(kt, qg=qg, m=m, K=K, gp=gp):
                            pt = pt_b[kt % 3]
                            if (qg, m, kt) in gp_msk:
                                sch.wait("tensor", "cGP", gp_msk[(qg, m, kt)])
                            else:
                                sch.wait("tensor", "cSC", sc_exp[(qg, m, kt)])
                            if kt == 0:
                                sch.wait("tensor", "cVE", ve_omul_par[gp])
                                sch.wait("tensor", "cVE", ve_repevac_par[gp])
                            for jj in range(2):
                                h = 2 * m + jj
                                cpos = 64 * jj
                                sch.op("tensor", lambda e, pt=pt, h=h, kt=kt, cpos=cpos,
                                       K=K, gp=gp: e.matmul(
                                    obank[gp][cpos:cpos + 64, :],
                                    v_tm[:, 1024 * kt + 64 * h:1024 * kt + 64 * (h + 1)],
                                    pt[:, 512 * (cpos // 64):512 * (cpos // 64 + 1)],
                                    start=(kt == 0), stop=(kt == K - 1),
                                    tile_position=(0, cpos),
                                    skip_group_check=True), inc="cPE")
                            for jj in range(2):
                                sch.op("tensor", lambda e, pt=pt, jj=jj, kt=kt, K=K, gp=gp: e.matmul(
                                    bass.AP(lbank[gp], 32 * jj * 512, [[512, 1], [1, 512]]),
                                    ones_bf[:], pt[:, 512 * jj:512 * (jj + 1)],
                                    start=(kt == 0), stop=(kt == K - 1),
                                    tile_position=(0, 32 * jj),
                                    skip_group_check=True), inc="cPE")
                            pe_avl[(qg, m, kt)] = sch.snap("cPE")

                        def mk_epilogue(qg=qg, m=m, K=K, gp=gp):
                            def ep_a():
                                pe_last = pe_avl[(qg, m, K - 1)]
                                sch.wait("vector", "cPE", pe_last)
                                for jj in range(2):
                                    sch.op("vector", lambda e, jj=jj, gp=gp: e.reciprocal(
                                        lrec[32 * jj:32 * jj + 1, :],
                                        bass.AP(lbank[gp], 32 * jj * 512, [[512, 1], [1, 512]])), inc="cVE")
                                sch.wait("vector", "cVE")
                                for jj in range(2):
                                    sch.op("vector", lambda e, jj=jj: e.tensor_copy(
                                        lrecb[32 * jj:32 * jj + 1, :],
                                        lrec[32 * jj:32 * jj + 1, :]), inc="cVE")
                                ve_rc_box[0] = sch.snap("cVE")
                            def ep_b():
                                sch.wait("tensor", "cVE", ve_rc_box[0])
                                for jj in range(2):
                                    sch.op("tensor", lambda e, jj=jj, gp=gp: e.matmul(
                                        bass.AP(lbank[gp], 64 * jj * 512, [[512, 64], [1, 512]]),
                                        ones_row[32 * jj:32 * jj + 1, 0:64],
                                        lrecb[32 * jj:32 * jj + 1, :],
                                        start=True, stop=True,
                                        tile_position=(32 * jj, 64 * jj)), inc="cPE")
                                pe_rep = sch.snap("cPE")
                                sch.wait("vector", "cPE", pe_rep)
                                sch.op("vector", lambda e, gp=gp: e.tensor_copy(
                                    rep_sb[:, 0:512], lbank[gp][:]), inc="cVE")
                                ve_repevac_par[gp] = sch.snap("cVE")
                                sch.wait("vector", "cVE")
                                sch.op("vector", lambda e, m=m, qg=qg, gp=gp: e.tensor_mul(
                                    o_sb[:, S * m + 512 * qg:S * m + 512 * (qg + 1)],
                                    obank[gp][:], rep_sb[:, 0:512]), inc="cVE")
                                ve_omul_par[gp] = sch.snap("cVE")
                            return ep_a, ep_b

                        for kt in range(K):
                            emit_scores_exp(kt)
                            if kt == 1 and pending_ep[0] is not None:
                                pending_ep[0][0]()
                            if kt == 1:
                                while dq_next < dq_quota[0]:
                                    emit_dq_job(dq_next)
                                    dq_next += 1
                            if kt >= 2:
                                emit_av(kt - 2)
                        emit_av(K - 2)
                        emit_av(K - 1)
                        if pending_ep[0] is not None:
                            pending_ep[0][1]()
                            pending_ep[0] = None
                        pending_ep[0] = mk_epilogue()
                        gidx_c[0] += 1
                        gidx = gidx_c[0]
                        if gidx <= 12:
                            dq_quota[0] = min(ndq, gidx // 3)
                        else:
                            dq_quota[0] = min(ndq, 4 + (gidx - 12) * (ndq - 4) // 18)
                pending_ep[0][0]()
                pending_ep[0][1]()
                pending_ep[0] = None
                ve_oevac_prev = max(ve_omul_par)
                while dq_next < ndq:
                    emit_dq_job(dq_next)
                    dq_next += 1
                ve_dq_all = sch.snap("cVE")
                for eng in ("sync", "vector", "scalar", "gpsimd"):
                    sch.wait(eng, "cVE", ve_dq_all)
                    sch.wait(eng, "dG0")
                    sch.wait(eng, "dG1")
                    sch.wait(eng, "dW3")
            ax.close()
            with ExitStack() as ws:
                w1t3 = [sb(ws, f"w1t3_{i}", NCH * 128, BF16) for i in range(3)]
                pe_att_end = sch.snap("cPE")
                sch.wait("sync", "cPE", pe_att_end)
                sch.wait("gpsimd", "cPE", pe_att_end)
                for ci in range(2):
                    sch.op("sync", lambda e, ci=ci: e.dma_start(
                        out=h_fm[:, 4096 * ci:4096 * (ci + 1)],
                        in_=bass.AP(hdram, 4096 * ci, [[NCH * S, 128], [1, 4096]])), inc="dH", k=16)
                for ci in range(2, 4):
                    sch.op("gpsimd", lambda e, ci=ci: e.dma_start(
                        out=h_fm[:, 4096 * ci:4096 * (ci + 1)],
                        in_=bass.AP(hdram, 4096 * ci, [[NCH * S, 128], [1, 4096]])), inc="dG0", k=16)
                dh_hw = sch.snap("dH")
                dh_sw = sch.snap("dG0")
                for fc in range(16):
                    sch.op("gpsimd", lambda e, fc=fc: e.dma_start(
                        out=w2a[:, 1024 * fc:1024 * (fc + 1)],
                        in_=bass.AP(w2dram, 128 * fc * D, [[D, 128], [1, D]])), inc="dG1", k=16)
                dg_w2a = sch.snap("dG1")

                sch.wait("tensor", "dGW", dg_wo)
                sch.wait("tensor", "cVE", ve_oevac_prev)
                ve_sev = {}
                ds_sp = {}
                pe_y = {}
                sc_ye3 = {}
                safe_ocs = list(range(0, 10)) + list(range(14, NFC))
                fc1_visits = [(ph, oc) for ph in range(2) for oc in safe_ocs] \
                    + [(ph, oc) for ph in range(2) for oc in (10, 11, 12, 13)]
                vic = [0]
                spill_fin = [None]

                def emit_fc1_visit(vi):
                    ph, oc = fc1_visits[vi]
                    wb = w1t3[vi % 3]
                    if vi >= 3:
                        sch.wait("sync", "cPE", pe_y[vi - 3])
                    else:
                        sch.wait("sync", "cVE", ve_oevac_prev)
                    sch.op("sync", lambda e, oc=oc, wb=wb: e.dma_start(
                        out=wb[:],
                        in_=bass.AP(w1dram, 1024 * oc,
                                    [[NCH * F, 128], [128, 8], [1, 128]])), inc=f"dW{vi % 3}", k=16)
                    dwv = sch.snap(f"dW{vi % 3}")
                    if vi == 0:
                        sch.wait("tensor", "dH", dh_hw)
                        sch.wait("tensor", "dG0", dh_sw)
                    sch.wait("tensor", f"dW{vi % 3}", dwv)
                    yp = p6 if vi % 2 == 0 else p7
                    if vi >= 2:
                        sch.wait("tensor", "cSC", sc_ye3[vi - 2])
                    for fc in range(NCH):
                        sch.op("tensor", lambda e, fc=fc, ph=ph, wb=wb, yp=yp: e.matmul(
                            yp[:], wb[:, 128 * fc:128 * (fc + 1)],
                            h_fm[:, S * fc + 512 * ph:S * fc + 512 * (ph + 1)],
                            start=(fc == 0), stop=(fc == NCH - 1)), inc="cPE")
                    pe_y[vi] = sch.snap("cPE")
                    sch.wait("scalar", "cPE", pe_y[vi])
                    if oc in (10, 11, 12, 13):
                        sch.wait("scalar", "dS0", spill_fin[0][0])
                        sch.wait("scalar", "dS1", spill_fin[0][1])
                    sch.op("scalar", lambda e, oc=oc, ph=ph, yp=yp: e.activation(
                        y_sl[:, 1024 * oc + 512 * ph:1024 * oc + 512 * (ph + 1)], yp[:],
                        AF.Silu, bias=b1_cm[:, oc:oc + 1]), inc="cSC")
                    sc_ye3[vi] = sch.snap("cSC")

                for tt in range(NT):
                    za = p4[:] if tt % 2 == 0 else pbig[:, 0:512]
                    zb = p5[:] if tt % 2 == 0 else pbig[:, 512:1024]
                    if tt >= 2:
                        sch.wait("tensor", "cVE", ve_sev[tt - 2])
                    for oc in range(NCH):
                        sch.op("tensor", lambda e, oc=oc, tt=tt, za=za: e.matmul(
                            za, o_sb[:, S * oc + 128 * tt:S * oc + 128 * (tt + 1)],
                            wo_bf[:, 1024 * oc:1024 * oc + 512],
                            start=(oc == 0), stop=(oc == NCH - 1)), inc="cPE")
                        sch.op("tensor", lambda e, oc=oc, tt=tt, zb=zb: e.matmul(
                            zb, o_sb[:, S * oc + 128 * tt:S * oc + 128 * (tt + 1)],
                            wo_bf[:, 1024 * oc + 512:1024 * (oc + 1)],
                            start=(oc == 0), stop=(oc == NCH - 1)), inc="cPE")
                    pe_s = sch.snap("cPE")
                    st = s_st[tt % 2]
                    sch.wait("vector", "cPE", pe_s)
                    if tt >= 2:
                        sch.wait("vector", f"dS{tt % 2}", ds_sp[tt - 2])
                    sch.op("vector", lambda e, st=st, za=za: e.tensor_copy(
                        st[:, 0:512], za), inc="cVE")
                    sch.op("vector", lambda e, st=st, zb=zb: e.tensor_copy(
                        st[:, 512:1024], zb), inc="cVE")
                    ve_sev[tt] = sch.snap("cVE")
                    sch.wait("scalar", "cVE", ve_sev[tt])
                    sch.op("scalar", lambda e, st=st, tt=tt: e.dma_start(
                        out=sdram[128 * tt:128 * (tt + 1), :], in_=st[:]), inc=f"dS{tt % 2}", k=16)
                    ds_sp[tt] = sch.snap(f"dS{tt % 2}")
                    if tt == 7:
                        pe_wo_mid = pe_s
                    if tt == 8:
                        sch.wait("sync", "cPE", pe_wo_mid)
                        for fc in range(16, NFC, 2):
                            sch.op("sync", lambda e, fc=fc: e.dma_start(
                                out=w2b[:, 1024 * (fc - 16):1024 * (fc - 15)],
                                in_=bass.AP(w2dram, 128 * fc * D, [[D, 128], [1, D]])), inc="dW", k=16)
                    if tt == NT - 1:
                        spill_fin[0] = (sch.snap("dS0"), sch.snap("dS1"))
                        pe_wo_end = pe_s
                    if tt >= 2:
                        for _ in range(4):
                            emit_fc1_visit(vic[0])
                            vic[0] += 1
                while vic[0] < len(fc1_visits):
                    emit_fc1_visit(vic[0])
                    vic[0] += 1
                sch.wait("sync", "cPE", pe_wo_end)
                for fc in range(17, NFC, 2):
                    sch.op("sync", lambda e, fc=fc: e.dma_start(
                        out=w2b[:, 1024 * (fc - 16):1024 * (fc - 15)],
                        in_=bass.AP(w2dram, 128 * fc * D, [[D, 128], [1, D]])), inc="dW", k=16)
                dw_w2 = sch.snap("dW")
                b3_fc1_sc = sch.snap("cSC")
                b3_sev_last = ve_sev[NT - 1]
                run_block(nc, "b3", sch)
        if upto < 4:
            return nc

        with ExitStack() as ps:
            pbig = ps.enter_context(nc.psum_tensor("pbig4", [128, 2048], F32))
            p4 = ps.enter_context(nc.psum_tensor("p4_4", [128, 512], F32))
            p5 = ps.enter_context(nc.psum_tensor("p5_4", [128, 512], F32))
            p6 = ps.enter_context(nc.psum_tensor("p6_4", [128, 512], F32))
            p7 = ps.enter_context(nc.psum_tensor("p7_4", [128, 512], F32))
            xb2 = [sb(ps, f"xb2_{i}", D, F32) for i in range(2)]
            sb2 = [sb(ps, f"sb2_{i}", D, F32) for i in range(2)]
            t_f = [sb(ps, f"t_f{i}", D, F32) for i in range(2)]
            f_sb = [sb(ps, f"f_sb{i}", D, F32) for i in range(2)]
            w1t = [sb(ps, f"w1t{i}", NCH * 128, BF16) for i in range(3)]

            sch = Sched(cts)
            sch.wait("sync", "dG0")
            sch.wait("sync", "dG1")
            sch.wait("sync", "dS0")
            sch.wait("sync", "dS1")
            dw_pre = {}
            for oc in range(3):
                sch.op("sync", lambda e, oc=oc: e.dma_start(
                    out=w1t[oc][:],
                    in_=bass.AP(w1dram, 1024 * oc,
                                [[NCH * F, 128], [128, 8], [1, 128]])), inc=f"dW{oc % 3}", k=16)
                dw_pre[oc] = sch.snap(f"dW{oc % 3}")

            pe_z = {}
            ve_ee = {}
            sc_g = {}
            o_done = {}
            ds_out = {}
            NP = 2
            for p in range(NP):
                if p == 1:
                    pe_v2 = {}
                    sc_y2 = {}
                    for oc in range(NFC):
                        wb = w1t[oc % 3]
                        if oc >= 3:
                            sch.wait("sync", "cPE", pe_v2[(oc - 3, 1)])
                            sch.op("sync", lambda e, oc=oc, wb=wb: e.dma_start(
                                out=wb[:],
                                in_=bass.AP(w1dram, 1024 * oc,
                                            [[NCH * F, 128], [128, 8], [1, 128]])), inc=f"dW{oc % 3}", k=16)
                            dw_now = sch.snap(f"dW{oc % 3}")
                        else:
                            dw_now = dw_pre[oc]
                        sch.wait("tensor", f"dW{oc % 3}", dw_now)
                        for ph in range(2):
                            yp = p6 if ph == 0 else p7
                            if oc >= 1:
                                sch.wait("tensor", "cSC", sc_y2[(oc - 1, ph)])
                            for fc in range(NCH):
                                sch.op("tensor", lambda e, fc=fc, ph=ph, wb=wb, yp=yp: e.matmul(
                                    yp[:], wb[:, 128 * fc:128 * (fc + 1)],
                                    h_fm[:, S * fc + 1024 + 512 * ph:S * fc + 1024 + 512 * (ph + 1)],
                                    start=(fc == 0), stop=(fc == NCH - 1)), inc="cPE")
                            pe_v2[(oc, ph)] = sch.snap("cPE")
                            sch.wait("scalar", "cPE", pe_v2[(oc, ph)])
                            if oc < 2:
                                sch.wait("scalar", "cPE", pe_z[(p - 1, 7)])
                            sch.op("scalar", lambda e, oc=oc, ph=ph, yp=yp: e.activation(
                                y_sl[:, 1024 * oc + 512 * ph:1024 * oc + 512 * (ph + 1)], yp[:],
                                AF.Silu, bias=b1_cm[:, oc:oc + 1]), inc="cSC")
                            sc_y2[(oc, ph)] = sch.snap("cSC")
                    sc_fc1_all = sch.snap("cSC")
                else:
                    sc_fc1_all = b3_fc1_sc
                for ttl in range(8):
                    tt = 8 * p + ttl
                    b = tt % 2
                    za = p4[:] if b == 0 else pbig[:, 0:512]
                    zb = p5[:] if b == 0 else pbig[:, 512:1024]
                    if ttl == 0:
                        sch.wait("tensor", "cSC", sc_fc1_all)
                        if p == 0:
                            sch.wait("tensor", "cVE", b3_sev_last)
                            sch.wait("tensor", "dG1", dg_w2a)
                    if tt >= 2:
                        sch.wait("tensor", "cVE", ve_ee[tt - 2])
                    for fc in range(NFC):
                        if p == 0 and ttl == 0 and fc == 16:
                            sch.wait("tensor", "dW", dw_w2)
                        sch.op("tensor", lambda e, fc=fc, ttl=ttl, za=za: e.matmul(
                            za, y_sl[:, 1024 * fc + 128 * ttl:1024 * fc + 128 * (ttl + 1)],
                            (w2a if fc < 16 else w2b)[:, 1024 * (fc % 16):1024 * (fc % 16) + 512],
                            start=(fc == 0), stop=(fc == NFC - 1)), inc="cPE")
                        sch.op("tensor", lambda e, fc=fc, ttl=ttl, zb=zb: e.matmul(
                            zb, y_sl[:, 1024 * fc + 128 * ttl:1024 * fc + 128 * (ttl + 1)],
                            (w2a if fc < 16 else w2b)[:, 1024 * (fc % 16) + 512:1024 * (fc % 16 + 1)],
                            start=(fc == 0), stop=(fc == NFC - 1)), inc="cPE")
                    pe_z[(p, ttl)] = sch.snap("cPE")
                    xb, sbt = xb2[b], sb2[b]
                    if tt >= 2:
                        sem, val = o_done[tt - 2]
                        sch.wait("sync", sem, val)
                    if tt >= 2:
                        sch.wait("sync", "cVE", ve_ee[tt - 2])
                    sch.op("sync", lambda e, xb=xb, tt=tt: e.dma_start(
                        out=xb[:], in_=x_d[128 * tt:128 * (tt + 1), :]), inc=f"dX{tt % 2}", k=16)
                    sch.op("sync", lambda e, sbt=sbt, tt=tt: e.dma_start(
                        out=sbt[:], in_=sdram[128 * tt:128 * (tt + 1), :]), inc=f"dX{tt % 2}", k=16)
                    dl_e = sch.snap(f"dX{tt % 2}")
                    fs = f_sb[b]
                    tf = t_f[b]
                    sch.wait("vector", f"dX{tt % 2}", dl_e)
                    sch.wait("vector", "cPE", pe_z[(p, ttl)])
                    sch.op("vector", lambda e, xb=xb, tt=tt, tf=tf: e.tensor_scalar(
                        tf[:], xb[:], negmu_a[:, tt:tt + 1], r1_a[:, tt:tt + 1],
                        ALU.add, ALU.mult), inc="cVE")
                    sch.wait("vector", "cVE")
                    sch.op("vector", lambda e, tf=tf: e.tensor_add(tf[:], tf[:], b2h_rep[:]), inc="cVE")
                    sch.wait("vector", "cVE")
                    if tt >= 2:
                        sch.wait("vector", "cSC", sc_g[tt - 2])
                        sch.wait("vector", f"dS{tt % 2}", ds_out[tt - 2])
                    sch.op("vector", lambda e, fs=fs, za=za, tf=tf: e.scalar_tensor_tensor(
                        fs[:, 0:512], za, 0.5, tf[:, 0:512], ALU.mult, ALU.add), inc="cVE")
                    sch.op("vector", lambda e, fs=fs, zb=zb, tf=tf: e.scalar_tensor_tensor(
                        fs[:, 512:1024], zb, 0.5, tf[:, 512:1024], ALU.mult, ALU.add), inc="cVE")
                    ve_ee[tt] = sch.snap("cVE")
                    sch.wait("scalar", "cVE", ve_ee[tt])
                    if tt >= 2:
                        sem2, val2 = o_done[tt - 2]
                        sch.wait("scalar", sem2, val2)
                    sch.op("scalar", lambda e, fs=fs: e.activation(
                        fs[:], fs[:], AF.Silu), inc="cSC")
                    sc_g[tt] = sch.snap("cSC")
                    if ttl < 5:
                        sch.wait("gpsimd", "cSC", sc_g[tt])
                        sch.wait("gpsimd", f"dX{tt % 2}", dl_e)
                        if tt >= 2:
                            sch.wait("gpsimd", f"dS{tt % 2}", ds_out[tt - 2])
                        sch.op("gpsimd", lambda e, fs=fs, sbt=sbt: e.tensor_mul(
                            fs[:], fs[:], sbt[:]), inc="cGP")
                        sch.wait("gpsimd", "cGP")
                        sch.op("gpsimd", lambda e, xb=xb, fs=fs: e.tensor_add(
                            fs[:], fs[:], xb[:]), inc="cGP")
                        o_done[tt] = ("cGP", sch.snap("cGP"))
                    else:
                        sch.wait("vector", "cSC", sc_g[tt])
                        sch.op("vector", lambda e, fs=fs, sbt=sbt: e.tensor_mul(
                            fs[:], fs[:], sbt[:]), inc="cVE")
                        sch.wait("vector", "cVE")
                        sch.op("vector", lambda e, xb=xb, fs=fs: e.tensor_add(
                            fs[:], fs[:], xb[:]), inc="cVE")
                        o_done[tt] = ("cVE", sch.snap("cVE"))
                    sem, val = o_done[tt]
                    sch.wait("scalar", sem, val)
                    sch.op("scalar", lambda e, fs=fs, tt=tt: e.dma_start(
                        out=out_d[128 * tt:128 * (tt + 1), :], in_=fs[:]), inc=f"dS{tt % 2}", k=16)
                    ds_out[tt] = sch.snap(f"dS{tt % 2}")
            sch.wait("scalar", "dS0")
            sch.wait("scalar", "dS1")
            run_block(nc, "b4", sch)

    return nc


_CONST_CACHE = {}


def _constants():
    if _CONST_CACHE:
        return _CONST_CACHE
    cols_E = [64 * (4 * cp + j) + 2 * i for cp in range(4) for j in range(4) for i in range(32)]
    _CONST_CACHE["wperm"] = np.array(cols_E + [c + 1 for c in cols_E])
    i = np.arange(32)
    inv_freq = 10000.0 ** (-(2.0 * i) / DK)
    t = np.arange(S, dtype=np.float64)
    ang = np.tile(np.outer(inv_freq, t), (4, 1))

    def split(theta):
        k = np.round(theta / np.pi)
        thr = theta - k * np.pi
        sg = np.where(k % 2 == 0, 1.0, -1.0)
        hi = thr.astype(np.float32)
        lo = (thr - hi.astype(np.float64)).astype(np.float32)
        return hi, lo, sg.astype(np.float32)

    s_hi, s_lo, s_sg = split(ang)
    c_hi, c_lo, c_sg = split(ang + np.pi / 2)
    _CONST_CACHE.update(
        ths_hi=s_hi, ths_lo=s_lo, sgs=s_sg,
        thc_hi=c_hi, thc_lo=c_lo, sgc=c_sg,
        ident=np.eye(128, dtype=np.float32),
        cmask=(np.arange(128)[:, None] <= np.arange(128)[None, :]).astype(np.float32),
    )
    return _CONST_CACHE


def kernel(L0_attn_x, L0_attn_Wq, L0_attn_Wk, L0_attn_Wv, L0_attn_Wo,
           L0_ffn_ln_gamma, L0_ffn_ln_beta, L0_ffn_fc1_q, L0_ffn_fc1_scales,
           L0_ffn_fc1_bias, L0_ffn_fc2_q, L0_ffn_fc2_scales, L0_ffn_fc2_bias):
    c = _constants()
    x = np.ascontiguousarray(np.asarray(L0_attn_x, dtype=np.float32))
    shared = {
        "wq": np.ascontiguousarray(np.asarray(L0_attn_Wq, np.float32)[:, c["wperm"]]),
        "wk": np.ascontiguousarray(np.asarray(L0_attn_Wk, np.float32)[:, c["wperm"]]),
        "wv": np.ascontiguousarray(np.asarray(L0_attn_Wv, np.float32)),
        "wo": np.ascontiguousarray(np.asarray(L0_attn_Wo, np.float32)),
        "ln_gamma": np.ascontiguousarray(np.asarray(L0_ffn_ln_gamma, np.float32)),
        "ln_beta": np.ascontiguousarray(np.asarray(L0_ffn_ln_beta, np.float32)),
        "fc1_q": np.ascontiguousarray(np.asarray(L0_ffn_fc1_q, np.int32)),
        "fc1_scales": np.ascontiguousarray(np.asarray(L0_ffn_fc1_scales, np.float32)),
        "fc1_bias": np.ascontiguousarray(np.asarray(L0_ffn_fc1_bias, np.float32)),
        "fc2_q": np.ascontiguousarray(np.asarray(L0_ffn_fc2_q, np.int32)),
        "fc2_scales": np.ascontiguousarray(np.asarray(L0_ffn_fc2_scales, np.float32)),
        "fc2_bias": np.ascontiguousarray(np.asarray(L0_ffn_fc2_bias, np.float32)),
        "ths_hi": c["ths_hi"], "ths_lo": c["ths_lo"], "sgs": c["sgs"],
        "thc_hi": c["thc_hi"], "thc_lo": c["thc_lo"], "sgc": c["sgc"],
        "ident": c["ident"], "cmask": c["cmask"],
    }
    in_maps = [dict(shared, x=np.ascontiguousarray(x[b])) for b in range(8)]
    nc = build_nc()
    res = run_bass_kernel_spmd(nc, in_maps, core_ids=list(range(8)))
    return np.stack([res.results[b]["out"] for b in range(8)], axis=0)


# version 39
# speedup vs baseline: 1.0223x; 1.0223x over previous
import numpy as np
import ml_dtypes
from contextlib import ExitStack
import concourse.bass as bass
import concourse.mybir as mybir
from concourse.bass_utils import run_bass_kernel_spmd

F32 = mybir.dt.float32
BF16 = mybir.dt.bfloat16
I32 = mybir.dt.int32
AF = mybir.ActivationFunctionType
ALU = mybir.AluOpType
AX = mybir.AxisListType

S = 2048
D = 1024
H = 16
DK = 64
F = 4096
NT = S // 128
NQG = S // 512
NCH = D // 128
NFC = F // 128
EPS = 1e-5
LN8 = float(np.log(0.125))

ENGS = ("sync", "scalar", "vector", "tensor", "gpsimd")


class Ct:
    __slots__ = ("h", "n")

    def __init__(self, h):
        self.h = h
        self.n = 0


class Sched:

    def __init__(self, cts):
        self.ct = cts
        self.streams = {e: [] for e in ENGS}

    def op(self, eng, fn, inc=None, k=1):
        if inc is not None:
            c = self.ct[inc]
            c.n += k

            def g(e, fn=fn, h=c.h, k=k):
                fn(e).then_inc(h, k)
            self.streams[eng].append(g)
        else:
            self.streams[eng].append(fn)

    def wait(self, eng, name, val=None):
        c = self.ct[name]
        v = c.n if val is None else val
        self.streams[eng].append(lambda e, h=c.h, v=v: e.wait_ge(h, v))

    def snap(self, name):
        return self.ct[name].n


def run_block(nc, name, sch):
    with nc.Block(name) as blk:
        for eng in ENGS:
            fns = sch.streams[eng]
            if fns:
                getattr(blk, eng)(lambda e, fns=fns: [f(e) for f in fns])


def fap(t, free_dims, offset=0):
    fe = t.shape[1]
    return bass.AP(t, offset, [[fe, 128]] + [list(d) for d in free_dims])


def pap(t, p0, np_, free_dims, offset=0):
    fe = t.shape[1]
    return bass.AP(t, offset + p0 * fe, [list(d) for d in free_dims[:0]] or None) if False else \
        bass.AP(t, offset + p0 * fe, [[fe, np_]] + [list(d) for d in free_dims])


def strided_parts(t, pstep, pcount, free_dims, offset=0):
    fe = t.shape[1]
    return bass.AP(t, offset, [[fe * pstep, pcount]] + [list(d) for d in free_dims])


MANUAL_BASE = 16896
MANUAL_TOP = MANUAL_BASE + 163840
R1, R2, R3, R4, R5 = (MANUAL_BASE, MANUAL_BASE + 32768, MANUAL_BASE + 65536,
                      MANUAL_BASE + 98304, MANUAL_BASE + 131072)


def build_nc(upto=9):
    nc = bass.Bass("TRN2", target_bir_lowering=False)
    nc.sbuf_base = MANUAL_TOP

    x_d = nc.dram_tensor("x", [S, D], F32, kind="ExternalInput")
    wq_d = nc.dram_tensor("wq", [D, D], F32, kind="ExternalInput")
    wk_d = nc.dram_tensor("wk", [D, D], F32, kind="ExternalInput")
    wv_d = nc.dram_tensor("wv", [D, D], F32, kind="ExternalInput")
    wo_d = nc.dram_tensor("wo", [D, D], F32, kind="ExternalInput")
    gam_d = nc.dram_tensor("ln_gamma", [D], F32, kind="ExternalInput")
    bet_d = nc.dram_tensor("ln_beta", [D], F32, kind="ExternalInput")
    q1_d = nc.dram_tensor("fc1_q", [D, F], I32, kind="ExternalInput")
    s1_d = nc.dram_tensor("fc1_scales", [D // 128, F], F32, kind="ExternalInput")
    b1_d = nc.dram_tensor("fc1_bias", [F], F32, kind="ExternalInput")
    q2_d = nc.dram_tensor("fc2_q", [F, D], I32, kind="ExternalInput")
    s2_d = nc.dram_tensor("fc2_scales", [F // 128, D], F32, kind="ExternalInput")
    b2_d = nc.dram_tensor("fc2_bias", [D], F32, kind="ExternalInput")
    ang_d = {}
    for nm in ("ths_hi", "ths_lo", "sgs", "thc_hi", "thc_lo", "sgc"):
        ang_d[nm] = nc.dram_tensor(nm, [128, S], F32, kind="ExternalInput")
    idn_d = nc.dram_tensor("ident", [128, 128], F32, kind="ExternalInput")
    msk_d = nc.dram_tensor("cmask", [128, 128], F32, kind="ExternalInput")
    out_d = nc.dram_tensor("out", [S, D], F32, kind="ExternalOutput")

    sdram = nc.dram_tensor("s_spill", [S, D], F32, kind="Internal")
    hdram = nc.dram_tensor("h_spill", [128, NCH * S], BF16, kind="Internal")
    g2dram = nc.dram_tensor("g2_row", [16, 128], F32, kind="Internal")
    w1dram = nc.dram_tensor("w1deq", [128, NCH * F], BF16, kind="Internal")
    w2dram = nc.dram_tensor("w2deq", [F, D], BF16, kind="Internal")

    cts = {}

    with ExitStack() as es:
        for name in ("dC", "dX0", "dX1", "dW", "dW0", "dW1", "dW2", "dW3",
                     "dR0", "dR1", "dS", "dS0", "dS1", "dG0", "dG1", "dGW", "dH",
                     "cPE", "cVE", "cSC", "cGP"):
            cts[name] = Ct(es.enter_context(nc.semaphore(name)))

        def sb(stack, name, cols, dt):
            return stack.enter_context(nc.sbuf_tensor(name, [128, cols], dt))

        gam_cm = sb(es, "gam_cm", NCH, F32)
        bet_cm = sb(es, "bet_cm", NCH, F32)
        g2_a = sb(es, "g2_a", NT, F32)
        b2h_rep = sb(es, "b2h_rep", D, F32)
        b1_cm = sb(es, "b1_cm", NFC, F32)
        idn_bf = sb(es, "idn_bf", 128, BF16)
        idn_f32 = sb(es, "idn_f32", 128, F32)
        msk_bf = sb(es, "msk_bf", 128, BF16)
        ones_bf = sb(es, "ones_bf", 1, BF16)
        selq = sb(es, "selq", 4, BF16)
        ones_row = sb(es, "ones_row", 128, BF16)
        negmu_a = sb(es, "negmu_a", NT, F32)
        r1_a = sb(es, "r1_a", NT, F32)
        rg_a = sb(es, "rg_a", NT, F32)
        negub = sb(es, "negub", 1, F32)
        stat = sb(es, "stat", 16, F32)
        biasc = sb(es, "biasc", 2, F32)
        statb = sb(es, "statb", 2, BF16)

        xn_fm = nc.alloc_sbuf_tensor_at("xn_fm", [128, NCH * S], BF16, offset=R1).ap().tensor
        q_att = nc.alloc_sbuf_tensor_at("q_att", [128, (H // 2) * S], BF16, offset=R2).ap().tensor
        k_att = nc.alloc_sbuf_tensor_at("k_att", [128, (H // 2) * S], BF16, offset=R3).ap().tensor
        v_tm = nc.alloc_sbuf_tensor_at("v_tm", [128, NT * D], BF16, offset=R4).ap().tensor
        cos_t = nc.alloc_sbuf_tensor_at("cos_sb", [128, S], BF16, offset=R5).ap().tensor
        sin_t = nc.alloc_sbuf_tensor_at("sin_sb", [128, S], BF16, offset=R5 + 4096).ap().tensor
        wv_bf = nc.alloc_sbuf_tensor_at("wv_bf", [128, NCH * D], BF16, offset=R5 + 8192).ap().tensor
        o_sb = nc.alloc_sbuf_tensor_at("o_sb", [128, NCH * S], BF16, offset=R5).ap().tensor
        dq_i0 = nc.alloc_sbuf_tensor_at("dq_i0", [128, 2048], I32, offset=R1).ap().tensor
        dq_s0 = nc.alloc_sbuf_tensor_at("dq_s0", [128, 2048], F32, offset=R1 + 8192).ap().tensor
        dq_o0 = nc.alloc_sbuf_tensor_at("dq_o0", [128, 2048], BF16, offset=R1 + 16384).ap().tensor
        s_st = [nc.alloc_sbuf_tensor_at(f"s_st{i}", [128, D], F32, offset=R1 + 20480 + 4096 * i).ap().tensor
                for i in range(2)]
        h_fm = nc.alloc_sbuf_tensor_at("h_fm", [128, NCH * S], BF16, offset=R4).ap().tensor
        w2a = nc.alloc_sbuf_tensor_at("w2a", [128, 16 * D], BF16, offset=R3).ap().tensor
        w2b = nc.alloc_sbuf_tensor_at("w2b", [128, 16 * D], BF16, offset=R5).ap().tensor
        y_sl = nc.alloc_sbuf_tensor_at("y_sl", [128, NFC * 1024], BF16, offset=R1).ap().tensor
        with ExitStack() as ps:
            idn_f = sb(ps, "idn_f", 128, F32)
            msk_f = sb(ps, "msk_f", 128, F32)
            cs_a = sb(ps, "cs_a", S, F32)
            cs_b = sb(ps, "cs_b", S, F32)
            sch = Sched(cts)
            for ti, (hi, lo, sg, dst_t) in enumerate(
                    (("ths_hi", "ths_lo", "sgs", sin_t),
                     ("thc_hi", "thc_lo", "sgc", cos_t))):
                if ti == 1:
                    sch.wait("sync", "cVE")
                sch.op("sync", lambda e, hi=hi: e.dma_start(
                    out=cs_a[:], in_=ang_d[hi][:]), inc="dX0", k=16)
                sch.op("sync", lambda e, lo=lo: e.dma_start(
                    out=cs_b[:], in_=ang_d[lo][:]), inc="dX0", k=16)
                sch.wait("vector", "dX0")
                sch.op("vector", lambda e: e.tensor_add(cs_a[:], cs_a[:], cs_b[:]), inc="cVE")
                ve_th = sch.snap("cVE")
                sch.wait("sync", "cVE", ve_th)
                sch.op("sync", lambda e, sg=sg: e.dma_start(
                    out=cs_b[:], in_=ang_d[sg][:]), inc="dX0", k=16)
                sch.wait("scalar", "cVE", ve_th)
                sch.op("scalar", lambda e: e.activation(
                    cs_a[:], cs_a[:], AF.Sin), inc="cSC")
                sch.wait("vector", "cSC")
                sch.wait("vector", "dX0")
                sch.op("vector", lambda e, dst_t=dst_t: e.tensor_mul(
                    dst_t[:], cs_a[:], cs_b[:]), inc="cVE")
            for fc in range(NCH):
                sch.op("sync", lambda e, fc=fc: e.dma_start(
                    out=gam_cm[:, fc:fc + 1],
                    in_=bass.AP(gam_d, 128 * fc, [[1, 128], [1, 1]])), inc="dC", k=16)
                sch.op("sync", lambda e, fc=fc: e.dma_start(
                    out=bet_cm[:, fc:fc + 1],
                    in_=bass.AP(bet_d, 128 * fc, [[1, 128], [1, 1]])), inc="dC", k=16)
            sch.op("sync", lambda e: e.dma_start(
                out=b2h_rep[:], in_=bass.AP(b2_d, 0, [[0, 128], [1, D]])), inc="dC", k=16)
            for oc in range(NFC):
                sch.op("sync", lambda e, oc=oc: e.dma_start(
                    out=b1_cm[:, oc:oc + 1],
                    in_=bass.AP(b1_d, 128 * oc, [[1, 128], [1, 1]])), inc="dC", k=16)
            sch.op("sync", lambda e: e.dma_start(out=idn_f[:], in_=idn_d[:]), inc="dC", k=16)
            sch.op("sync", lambda e: e.dma_start(out=msk_f[:], in_=msk_d[:]), inc="dC", k=16)
            sch.wait("vector", "dC")
            sch.op("vector", lambda e: e.tensor_copy(idn_bf[:], idn_f[:]), inc="cVE")
            sch.op("vector", lambda e: e.tensor_copy(idn_f32[:], idn_f[:]), inc="cVE")
            sch.op("vector", lambda e: e.tensor_copy(msk_bf[:], msk_f[:]), inc="cVE")
            sch.op("vector", lambda e: e.tensor_scalar_mul(b2h_rep[:], b2h_rep[:], 0.5), inc="cVE")

            sch.op("gpsimd", lambda e: e.memset(ones_bf[:], 1.0), inc="cGP")
            sch.op("gpsimd", lambda e: e.memset(selq[:], 0.0), inc="cGP")
            sch.wait("gpsimd", "cGP")
            for j in range(4):
                sch.op("gpsimd", lambda e, j=j: e.memset(
                    selq[32 * j:32 * (j + 1), j:j + 1], 1.0), inc="cGP")
            sch.op("gpsimd", lambda e: e.memset(biasc[:, 0:1], EPS), inc="cGP")
            sch.op("gpsimd", lambda e: e.memset(biasc[:, 1:2], LN8), inc="cGP")
            sch.op("gpsimd", lambda e: e.memset(ones_row[:], 1.0), inc="cGP")
            run_block(nc, "b0", sch)
        if upto < 1:
            return nc

        with ExitStack() as ps:
            xbuf = [sb(ps, f"xb{i}", D, F32) for i in range(2)]
            xnb = [sb(ps, f"xnb{i}", D, BF16) for i in range(2)]
            sqsc = [sb(ps, f"sqsc{i}", D, F32) for i in range(2)]
            st2 = sb(ps, "st2", 80, F32)
            pT = [ps.enter_context(nc.psum_tensor(f"pT{i}", [128, D], BF16)) for i in range(2)]
            pTf = ps.enter_context(nc.psum_tensor("pTf", [128, 128], F32))

            sch = Sched(cts)
            for fc in range(NCH):
                sch.op("gpsimd", lambda e, fc=fc: e.dma_start(
                    out=wv_bf[:, 1024 * fc:1024 * (fc + 1)],
                    in_=bass.AP(wv_d, 128 * fc * D, [[D, 128], [1, D]])), inc="dG0", k=16)
            dg_wv = sch.snap("dG0")
            ve_p1 = {}
            sc_p1 = {}
            for tt in range(NT):
                b = tt % 2
                xb = xbuf[b]
                if tt >= 2:
                    sch.wait("sync", "cVE", ve_p1[tt - 2])
                    sch.wait("sync", "cSC", sc_p1[tt - 2])
                sch.op("sync", lambda e, xb=xb, tt=tt: e.dma_start(
                    out=xb[:], in_=x_d[128 * tt:128 * (tt + 1), :]), inc=f"dX{b}", k=16)
                v = sch.snap(f"dX{b}")
                sch.wait("vector", f"dX{b}", v)
                sch.op("vector", lambda e, xb=xb, tt=tt: e.reduce_sum(
                    st2[:, tt:tt + 1], xb[:], axis=AX.X), inc="cVE")
                ve_p1[tt] = sch.snap("cVE")
                sch.wait("scalar", f"dX{b}", v)
                sch.op("scalar", lambda e, xb=xb, tt=tt: e.activation(
                    sqsc[tt % 2][:], xb[:], AF.Square,
                    accum_out=st2[:, 16 + tt:17 + tt]), inc="cSC")
                sc_p1[tt] = sch.snap("cSC")
            ve_rs_all = sch.snap("cVE")
            sc_sq_all = sch.snap("cSC")
            sch.wait("vector", "cSC", sc_sq_all)
            sch.wait("vector", "cVE", ve_rs_all)
            sch.op("vector", lambda e: e.tensor_scalar_mul(
                negmu_a[:], st2[:, 0:16], -1.0 / D), inc="cVE")
            sch.wait("vector", "cVE")
            sch.op("vector", lambda e: e.tensor_scalar_mul(
                st2[:, 32:48], st2[:, 16:32], 1.0 / D), inc="cVE")
            ve_1 = sch.snap("cVE")
            sch.wait("scalar", "cVE", ve_1)
            sch.op("scalar", lambda e: e.activation(
                st2[:, 48:64], negmu_a[:], AF.Square), inc="cSC")
            sc_1 = sch.snap("cSC")
            sch.wait("vector", "cSC", sc_1)
            sch.op("vector", lambda e: e.scalar_tensor_tensor(
                st2[:, 48:64], st2[:, 48:64], -1.0, st2[:, 32:48],
                ALU.mult, ALU.add), inc="cVE")
            ve_2 = sch.snap("cVE")
            sch.wait("scalar", "cVE", ve_2)
            sch.op("scalar", lambda e: e.activation(
                st2[:, 32:48], st2[:, 48:64], AF.Ln, bias=biasc[:, 0:1]), inc="cSC")
            sch.wait("scalar", "cSC")
            sch.op("scalar", lambda e: e.activation(
                r1_a[:], st2[:, 32:48], AF.Exp, scale=-0.5), inc="cSC")
            sch.op("scalar", lambda e: e.activation(
                st2[:, 64:80], st2[:, 32:48], AF.Exp, scale=-1.0), inc="cSC")
            sc_2 = sch.snap("cSC")
            sch.wait("vector", "cSC", sc_2)
            sch.op("vector", lambda e: e.tensor_mul(
                st2[:, 48:64], st2[:, 48:64], st2[:, 64:80]), inc="cVE")
            ve_3 = sch.snap("cVE")
            sch.wait("scalar", "cVE", ve_3)
            sch.op("scalar", lambda e: e.activation(
                st2[:, 48:64], st2[:, 48:64], AF.Ln, bias=biasc[:, 0:1]), inc="cSC")
            sch.wait("scalar", "cSC")
            sch.op("scalar", lambda e: e.activation(
                st2[:, 48:64], st2[:, 48:64], AF.Exp, scale=-0.5), inc="cSC")
            sc_3 = sch.snap("cSC")
            sch.wait("vector", "cSC", sc_3)
            sch.op("vector", lambda e: e.tensor_copy(g2_a[:], st2[:, 48:64]), inc="cVE")
            sch.op("vector", lambda e: e.tensor_mul(
                rg_a[:], r1_a[:], st2[:, 48:64]), inc="cVE")
            ve_stats = sch.snap("cVE")
            ve_xn = {}
            sc_evac2 = {}
            pe_tp = {}
            dx2 = {}
            sch.wait("vector", "cVE", ve_stats)
            for tt in range(NT):
                b = tt % 2
                xb, xn1, pt = xbuf[b], xnb[b], pT[b]
                if tt >= 2:
                    sch.wait("sync", "cVE", ve_xn[tt - 2])
                else:
                    sch.wait("sync", "cVE", ve_stats)
                    sch.wait("sync", "cSC", sc_3)
                sch.op("sync", lambda e, xb=xb, tt=tt: e.dma_start(
                    out=xb[:], in_=x_d[128 * tt:128 * (tt + 1), :]), inc=f"dX{b}", k=16)
                dx2[tt] = sch.snap(f"dX{b}")
                sch.wait("vector", f"dX{b}", dx2[tt])
                if tt >= 2:
                    sch.wait("vector", "cPE", pe_tp[tt - 2])
                sch.op("vector", lambda e, xb=xb, xn1=xn1, tt=tt: e.tensor_scalar(
                    xn1[:], xb[:], negmu_a[:, tt:tt + 1], r1_a[:, tt:tt + 1],
                    ALU.add, ALU.mult), inc="cVE")
                ve_xn[tt] = sch.snap("cVE")
                sch.wait("tensor", "cVE", ve_xn[tt])
                if tt >= 2:
                    sch.wait("tensor", "cSC", sc_evac2[tt - 2])
                for fc in range(NCH):
                    sch.op("tensor", lambda e, xn1=xn1, pt=pt, fc=fc: e.transpose(
                        pt[:, 128 * fc:128 * (fc + 1)],
                        xn1[:, 128 * fc:128 * (fc + 1)], idn_bf[:]), inc="cPE")
                pe_tp[tt] = sch.snap("cPE")
                sch.wait("scalar", "cPE", pe_tp[tt])
                sch.op("scalar", lambda e, pt=pt, tt=tt: e.activation(
                    fap(xn_fm, [[S, NCH], [1, 128]], offset=128 * tt),
                    fap(pt, [[128, NCH], [1, 128]]), AF.Copy), inc="cSC")
                sc_evac2[tt] = sch.snap("cSC")
            sch.wait("gpsimd", "cVE", ve_stats)
            sch.op("gpsimd", lambda e: e.memset(sqsc[0][:, 0:128], 0.0), inc="cGP")
            sch.wait("vector", "cGP")
            sch.op("vector", lambda e: e.tensor_copy(
                bass.AP(sqsc[0], 0, [[1024, 128], [1, 16]]), g2_a[:]), inc="cVE")
            ve_g2c = sch.snap("cVE")
            sch.wait("tensor", "cVE", ve_g2c)
            sch.op("tensor", lambda e: e.transpose(
                pTf[:, 0:128], bass.AP(sqsc[0], 0, [[1024, 128], [1, 128]]),
                idn_f32[:]), inc="cPE")
            pe_g2t = sch.snap("cPE")
            sch.wait("vector", "cPE", pe_g2t)
            sch.op("vector", lambda e: e.tensor_copy(
                bass.AP(sqsc[1], 0, [[1024, 16], [1, 128]]),
                bass.AP(pTf, 0, [[128, 16], [1, 128]])), inc="cVE")
            ve_g2e = sch.snap("cVE")
            sch.wait("scalar", "cVE", ve_g2e)
            sch.op("scalar", lambda e: e.dma_start(
                out=bass.AP(g2dram, 0, [[128, 16], [1, 128]]),
                in_=bass.AP(sqsc[1], 0, [[1024, 16], [1, 128]])), inc="dH", k=16)
            b1_sc_all = sch.snap("cSC")
            run_block(nc, "b1", sch)
        if upto < 2:
            return nc

        with ExitStack() as ps:
            pbig = ps.enter_context(nc.psum_tensor("pbig2", [128, 2048], F32))
            p4 = ps.enter_context(nc.psum_tensor("p4_2", [128, 512], F32))
            p5 = ps.enter_context(nc.psum_tensor("p5_2", [128, 512], F32))
            p6 = ps.enter_context(nc.psum_tensor("p6_2", [128, 512], F32))
            p7 = ps.enter_context(nc.psum_tensor("p7_2", [128, 512], F32))

            sch = Sched(cts)
            sch.wait("vector", "dC")
            sch.wait("gpsimd", "dC")
            wE = [sb(ps, f"wE{i}", NCH * 128, BF16) for i in range(2)]
            wO = [sb(ps, f"wO{i}", NCH * 128, BF16) for i in range(2)]
            wstg = [sb(ps, f"wstg{i}", 1024, F32) for i in range(2)]
            hs = ExitStack()
            g2_rep = hs.enter_context(nc.sbuf_tensor("g2_rep", [128, S], BF16))
            h_scr = [hs.enter_context(nc.sbuf_tensor(f"h_scr{i}", [128, S], BF16)) for i in range(2)]

            sch.wait("gpsimd", "dH")
            sch.op("gpsimd", lambda e: e.dma_start(
                out=g2_rep[:], in_=bass.AP(g2dram, 0, [[0, 128], [1, S]])), inc="dGW", k=16)
            dh_rep = sch.snap("dGW")
            sch.wait("gpsimd", "dGW", dh_rep)
            gp_hm = {}
            ve_hts = {}
            ds_hsp = {}

            def emit_h_mul(fc):
                if fc < 2:
                    sch.wait("gpsimd", "cSC", b1_sc_all)
                if fc >= 2:
                    sch.wait("gpsimd", "cVE", ve_hts[fc - 2])
                    sch.wait("gpsimd", f"dS{fc % 2}", ds_hsp[fc - 2])
                sch.op("gpsimd", lambda e, fc=fc: e.tensor_mul(
                    h_scr[fc % 2][:], xn_fm[:, S * fc:S * (fc + 1)], g2_rep[:]), inc="cGP")
                gp_hm[fc] = sch.snap("cGP")

            def emit_h_ts(fc):
                sch.wait("vector", "cGP", gp_hm[fc])
                if fc >= 2:
                    sch.wait("vector", f"dS{fc % 2}", ds_hsp[fc - 2])
                sch.op("vector", lambda e, fc=fc: e.tensor_scalar(
                    h_scr[fc % 2][:], h_scr[fc % 2][:], gam_cm[:, fc:fc + 1],
                    bet_cm[:, fc:fc + 1], ALU.mult, ALU.add), inc="cVE")
                ve_hts[fc] = sch.snap("cVE")
                sch.wait("scalar", "cVE", ve_hts[fc])
                sch.op("scalar", lambda e, fc=fc: e.dma_start(
                    out=bass.AP(hdram, S * fc, [[NCH * S, 128], [1, S]]),
                    in_=h_scr[fc % 2][:]), inc=f"dS{fc % 2}", k=16)
                ds_hsp[fc] = sch.snap(f"dS{fc % 2}")

            emit_h_mul(0)
            emit_h_mul(1)
            ve_vev = {}
            sch.wait("tensor", "dG0", dg_wv)
            for tt in range(NT):
                za = p4[:] if tt % 2 == 0 else p6[:]
                zb = p5[:] if tt % 2 == 0 else p7[:]
                sch.wait("tensor", "cSC", sc_evac2[tt])
                if tt >= 2:
                    sch.wait("tensor", "cVE", ve_vev[tt - 2])
                for fc in range(NCH):
                    sch.op("tensor", lambda e, fc=fc, tt=tt, za=za: e.matmul(
                        za, xn_fm[:, S * fc + 128 * tt:S * fc + 128 * (tt + 1)],
                        wv_bf[:, 1024 * fc:1024 * fc + 512],
                        start=(fc == 0), stop=(fc == NCH - 1)), inc="cPE")
                    sch.op("tensor", lambda e, fc=fc, tt=tt, zb=zb: e.matmul(
                        zb, xn_fm[:, S * fc + 128 * tt:S * fc + 128 * (tt + 1)],
                        wv_bf[:, 1024 * fc + 512:1024 * (fc + 1)],
                        start=(fc == 0), stop=(fc == NCH - 1)), inc="cPE")
                pe_v = sch.snap("cPE")
                sch.wait("vector", "cPE", pe_v)
                sch.op("vector", lambda e, tt=tt, za=za: e.tensor_copy(
                    v_tm[:, 1024 * tt:1024 * tt + 512], za), inc="cVE")
                sch.op("vector", lambda e, tt=tt, zb=zb: e.tensor_copy(
                    v_tm[:, 1024 * tt + 512:1024 * (tt + 1)], zb), inc="cVE")
                ve_vev[tt] = sch.snap("cVE")
                if tt < NCH:
                    emit_h_ts(tt)
                    if tt + 2 < NCH:
                        emit_h_mul(tt + 2)
            ve_v_all = sch.snap("cVE")
            hs.close()
            qe_s = sb(ps, "qe_s", S, BF16)
            qo_s = sb(ps, "qo_s", S, BF16)
            qeo_bf = [sb(ps, f"qeo{i}", 1024, BF16) for i in range(2)]
            rp_t = [sb(ps, f"rp_t{i}", 1024, BF16) for i in range(2)]
            nsq_sm = [sb(ps, f"nsq_sm{i}", 1024, BF16) for i in range(2)]
            sch.wait("vector", "cGP", gp_hm[NCH - 1])
            sch.wait("vector", "dS0", ds_hsp[NCH - 2])
            sch.wait("scalar", "dS1", ds_hsp[NCH - 1])

            ve_rope = {}
            sc_evac = {}
            gp_rope = {}
            dr_rep = {}
            ve_slab = {}
            pe_qk = {}
            dw_slab = {}

            def emit_slab_dmas(it):
                ti2, cp2 = it // 4, it % 4
                w_src = wq_d if ti2 == 0 else wk_d
                for pi, coff in enumerate((128 * cp2, 512 + 128 * cp2)):
                    if (it - 1, pi) in ve_slab:
                        sch.wait("sync", "cVE", ve_slab[(it - 1, pi)])
                    for fc in range(NCH):
                        sch.op("sync", lambda e, fc=fc, coff=coff, w_src=w_src, pi=pi: e.dma_start(
                            out=wstg[pi][:, 128 * fc:128 * (fc + 1)],
                            in_=bass.AP(w_src, 128 * fc * D + coff,
                                        [[D, 128], [1, 128]])), inc=f"dW{pi}", k=16)
                    dw_slab[(it, pi)] = sch.snap(f"dW{pi}")

            def emit_slab_casts(it):
                wp = it % 2
                for pi in range(2):
                    sch.wait("vector", f"dW{pi}", dw_slab[(it, pi)])
                    if it >= 2:
                        sch.wait("vector", "cPE", pe_qk[it - 2])
                    sch.op("vector", lambda e, pi=pi, wp=wp,
                           wdst=(wE if pi == 0 else wO): e.tensor_copy(
                        wdst[wp][:], wstg[pi][:]), inc="cVE")
                    ve_slab[(it, pi)] = sch.snap("cVE")

            first_nmax = {0: True, 1: True}
            gp_sq = {}
            ve_nr = {}
            pe_nr = {}

            def emit_sel(it2, tg2):
                ti2 = it2 // 4
                sch.wait("tensor", "cGP", gp_sq[(it2, tg2)])
                if ve_nr:
                    sch.wait("tensor", "cVE", ve_nr[max(ve_nr)])
                else:
                    sch.wait("tensor", "cVE", ve_v_all)
                sch.op("tensor", lambda e, tg2=tg2: e.matmul(
                    p7[0:4, :], selq[:, 0:4], nsq_sm[tg2 % 2][:, 0:512],
                    start=True, stop=False), inc="cPE")
                sch.op("tensor", lambda e, tg2=tg2: e.matmul(
                    p7[0:4, :], selq[:, 0:4], nsq_sm[tg2 % 2][:, 512:1024],
                    start=False, stop=True), inc="cPE")
                pe_nr[(it2, tg2)] = sch.snap("cPE")
                sch.wait("vector", "cPE", pe_nr[(it2, tg2)])
                sch.op("vector", lambda e: e.reduce_max(
                    stat[0:4, 12:13], p7[0:4, :], axis=AX.X), inc="cVE")
                sch.wait("vector", "cVE")
                if first_nmax[ti2]:
                    sch.op("vector", lambda e, ti2=ti2: e.tensor_copy(
                        stat[0:4, 10 + ti2:11 + ti2], stat[0:4, 12:13]), inc="cVE")
                    first_nmax[ti2] = False
                else:
                    sch.op("vector", lambda e, ti2=ti2: e.tensor_max(
                        stat[0:4, 10 + ti2:11 + ti2],
                        stat[0:4, 10 + ti2:11 + ti2], stat[0:4, 12:13]), inc="cVE")
                ve_nr[(it2, tg2)] = sch.snap("cVE")
            emit_slab_dmas(0)
            emit_slab_casts(0)
            for ti, (w_d_, att) in enumerate(((wq_d, q_att), (wk_d, k_att))):
                for cp in range(4):
                    it = ti * 4 + cp
                    wp = it % 2
                    if it + 1 < 8:
                        emit_slab_dmas(it + 1)
                    sch.wait("tensor", "cVE", ve_slab[(it, 1)])
                    if it == 0:
                        sch.wait("tensor", "cVE", ve_v_all)
                    for tgq in range(4):
                        sl = tgq % 2
                        pe_ = pbig[:, 1024 * sl:1024 * sl + 512]
                        po_ = pbig[:, 1024 * sl + 512:1024 * sl + 1024]
                        key = (it, tgq - 2) if tgq >= 2 else (it - 1, tgq + 2)
                        if key in sc_evac:
                            sch.wait("tensor", "cSC", sc_evac[key])
                        for fc in range(NCH):
                            sch.op("tensor", lambda e, fc=fc, tgq=tgq, pe_=pe_, wp=wp: e.matmul(
                                pe_, wE[wp][:, 128 * fc:128 * (fc + 1)],
                                xn_fm[:, S * fc + 512 * tgq:S * fc + 512 * (tgq + 1)],
                                start=(fc == 0), stop=(fc == NCH - 1)), inc="cPE")
                        for fc in range(NCH):
                            sch.op("tensor", lambda e, fc=fc, tgq=tgq, po_=po_, wp=wp: e.matmul(
                                po_, wO[wp][:, 128 * fc:128 * (fc + 1)],
                                xn_fm[:, S * fc + 512 * tgq:S * fc + 512 * (tgq + 1)],
                                start=(fc == 0), stop=(fc == NCH - 1)), inc="cPE")
                        pe_now = sch.snap("cPE")
                        rp = rp_t[tgq % 2]
                        qeo = qeo_bf[tgq % 2]
                        sch.wait("scalar", "cPE", pe_now)
                        key2 = (it, tgq - 2) if tgq >= 2 else (it - 1, tgq + 2)
                        if key2 in ve_rope:
                            sch.wait("scalar", "cVE", ve_rope[key2])
                        if key2 in gp_sq:
                            sch.wait("scalar", "cGP", gp_sq[key2])
                        sch.op("scalar", lambda e, pe_=pe_, qeo=qeo: e.activation(
                            qeo[:, 0:512], pe_, AF.Copy), inc="cSC")
                        sch.op("scalar", lambda e, po_=po_, qeo=qeo: e.activation(
                            qeo[:, 512:1024], po_, AF.Copy), inc="cSC")
                        sc_evac[(it, tgq)] = sch.snap("cSC")
                        if (it - 1, tgq) in dr_rep:
                            sch.wait("vector", f"dR{(it - 1) % 2}", dr_rep[(it - 1, tgq)])
                        sch.wait("vector", "cSC", sc_evac[(it, tgq)])
                        sch.op("vector", lambda e, qeo=qeo, tgq=tgq, rp=rp: e.tensor_mul(
                            rp[:, 0:512], qeo[:, 0:512], cos_t[:, 512 * tgq:512 * (tgq + 1)]), inc="cVE")
                        sch.op("vector", lambda e, qeo=qeo, tgq=tgq, rp=rp: e.tensor_mul(
                            rp[:, 512:1024], qeo[:, 512:1024], sin_t[:, 512 * tgq:512 * (tgq + 1)]), inc="cVE")
                        sch.wait("vector", "cVE")
                        sch.op("vector", lambda e, tgq=tgq, rp=rp: e.tensor_sub(
                            qe_s[:, 512 * tgq:512 * (tgq + 1)],
                            rp[:, 0:512], rp[:, 512:1024]), inc="cVE")
                        sch.wait("vector", "cVE")
                        sch.op("vector", lambda e, qeo=qeo, tgq=tgq, rp=rp: e.tensor_mul(
                            rp[:, 0:512], qeo[:, 512:1024], cos_t[:, 512 * tgq:512 * (tgq + 1)]), inc="cVE")
                        sch.op("vector", lambda e, qeo=qeo, tgq=tgq, rp=rp: e.tensor_mul(
                            rp[:, 512:1024], qeo[:, 0:512], sin_t[:, 512 * tgq:512 * (tgq + 1)]), inc="cVE")
                        ve_rope[(it, tgq)] = sch.snap("cVE")
                        sch.wait("vector", "cVE")
                        sch.op("vector", lambda e, tgq=tgq, rp=rp: e.tensor_add(
                            qo_s[:, 512 * tgq:512 * (tgq + 1)],
                            rp[:, 0:512], rp[:, 512:1024]), inc="cVE")
                        gp_rope[(it, tgq)] = sch.snap("cVE")
                        sch.wait("gpsimd", "cVE", gp_rope[(it, tgq)])
                        gkey = (it, tgq - 2) if tgq >= 2 else (it - 1, tgq + 2)
                        if gkey in pe_nr:
                            sch.wait("gpsimd", "cPE", pe_nr[gkey])
                        sch.op("gpsimd", lambda e, qeo=qeo, tgq=tgq: e.tensor_mul(
                            nsq_sm[tgq % 2][:], qeo[:], qeo[:]), inc="cGP")
                        gp_sq[(it, tgq)] = sch.snap("cGP")
                        if tgq >= 1:
                            emit_sel(it, tgq - 1)
                        elif it >= 1:
                            emit_sel(it - 1, 3)
                    if it + 1 < 8:
                        emit_slab_casts(it + 1)
                    ve_rope_all = sch.snap("cVE")
                    sch.wait("scalar", "cVE", ve_rope_all)
                    for j in range(4):
                        h = 4 * cp + j
                        m, r = h // 2, 64 * (h % 2)
                        sch.op("scalar", lambda e, j=j, m=m, r=r, att=att: e.dma_start(
                            out=att[r:r + 32, S * m:S * (m + 1)],
                            in_=qe_s[32 * j:32 * (j + 1), :]), inc=f"dR{it % 2}", k=16)
                        sch.op("scalar", lambda e, j=j, m=m, r=r, att=att: e.dma_start(
                            out=att[r + 32:r + 64, S * m:S * (m + 1)],
                            in_=qo_s[32 * j:32 * (j + 1), :]), inc=f"dR{it % 2}", k=16)
                    for tgq in range(4):
                        dr_rep[(it, tgq)] = sch.snap(f"dR{it % 2}")
                    pe_qk[it] = sch.snap("cPE")

            emit_sel(7, 3)
            sch.wait("vector", "cVE", ve_nr[max(ve_nr)])
            for ti2 in range(2):
                sch.op("vector", lambda e, ti2=ti2: e.tensor_copy(
                    statb[0:4, ti2:ti2 + 1], stat[0:4, 10 + ti2:11 + ti2]), inc="cVE")
            ve_cmb = sch.snap("cVE")
            sch.wait("tensor", "cVE", ve_cmb)
            for ti2 in range(2):
                sch.op("tensor", lambda e, ti2=ti2: e.matmul(
                    p6[0:1, ti2:ti2 + 1], ones_row[0:4, 0:1], statb[0:4, ti2:ti2 + 1],
                    start=True, stop=True), inc="cPE")
            pe_cmb = sch.snap("cPE")
            sch.wait("vector", "cPE", pe_cmb)
            sch.op("vector", lambda e: e.tensor_copy(
                stat[:1, 10:12], p6[0:1, 0:2]), inc="cVE")
            sch.wait("scalar", "cVE")
            sch.op("scalar", lambda e: e.activation(
                stat[:1, 13:14], stat[:1, 10:11], AF.Ln), inc="cSC")
            sch.wait("scalar", "cSC")
            sch.op("scalar", lambda e: e.activation(
                stat[:1, 14:15], stat[:1, 11:12], AF.Ln), inc="cSC")
            sch.wait("vector", "cSC")
            sch.op("vector", lambda e: e.tensor_add(
                stat[:1, 15:16], stat[:1, 13:14], stat[:1, 14:15]), inc="cVE")
            sch.wait("scalar", "cVE")
            sch.op("scalar", lambda e: e.activation(
                stat[:1, 12:13], stat[:1, 15:16], AF.Exp, scale=0.5,
                bias=biasc[0:1, 1:2]), inc="cSC")
            sch.wait("vector", "cSC")
            sch.op("vector", lambda e: e.tensor_scalar(
                stat[:1, 12:13], stat[:1, 12:13], 1.0, -1.0, ALU.add, ALU.mult), inc="cVE")
            sch.wait("vector", "cVE")
            sch.op("vector", lambda e: e.tensor_copy(statb[:1, 0:1], stat[:1, 12:13]), inc="cVE")
            ve_ub = sch.snap("cVE")
            sch.wait("tensor", "cVE", ve_ub)
            sch.op("tensor", lambda e: e.matmul(
                p6[:, 0:1], ones_row[0:1, :], statb[0:1, 0:1],
                start=True, stop=True), inc="cPE")
            sch.wait("vector", "cPE")
            sch.op("vector", lambda e: e.tensor_copy(negub[:], p6[:, 0:1]), inc="cVE")
            run_block(nc, "b2", sch)

        with ExitStack() as ps:
            pbig = ps.enter_context(nc.psum_tensor("pbig3", [128, 2048], F32))
            p4 = ps.enter_context(nc.psum_tensor("p4_3", [128, 512], F32))
            p5 = ps.enter_context(nc.psum_tensor("p5_3", [128, 512], F32))
            p6 = ps.enter_context(nc.psum_tensor("p6_3", [128, 512], F32))
            p7 = ps.enter_context(nc.psum_tensor("p7_3", [128, 512], F32))
            wo_bf = sb(ps, "wo_bf", NCH * D, BF16)
            pt_b = [sb(ps, f"ptb{i}", 2048, BF16) for i in range(3)]
            ax = ExitStack()
            lrec = sb(ax, "lrec", 512, F32)
            lrecb = sb(ax, "lrecb", 512, BF16)
            rep_sb = sb(ax, "rep_sb", 512, F32)

            sch = Sched(cts)
            sch.wait("tensor", "dR0")
            sch.wait("tensor", "dR1")
            sch.wait("sync", "dR0")
            sch.wait("sync", "dR1")
            sch.wait("gpsimd", "dR0")
            sch.wait("gpsimd", "dR1")
            for fc in range(NCH):
                sch.op("gpsimd", lambda e, fc=fc: e.dma_start(
                    out=wo_bf[:, 1024 * fc:1024 * (fc + 1)],
                    in_=bass.AP(wo_d, 128 * fc * D, [[D, 128], [1, D]])), inc="dGW", k=16)
            dg_wo = sch.snap("dGW")

            with ExitStack() as ds:
                dq_i = [dq_i0] * 2
                dq_s = [dq_s0] * 2
                dq_o = [dq_o0] * 2

                dq_jobs = []
                for c in range(NCH):
                    for hlf in range(2):
                        dq_jobs.append(("w1", c, hlf))
                for c in range(NFC):
                    dq_jobs.append(("w2", c, 0))
                ndq = len(dq_jobs)
                dq_ve = {}
                dq_dg = {}

                def emit_dq_job(ji):
                    kind, c, hlf = dq_jobs[ji]
                    if kind == "w1":
                        cols = 2048
                        qsrc = bass.AP(q1_d, 128 * c * F + 2048 * hlf, [[F, 128], [1, 2048]])
                        ssrc = bass.AP(s1_d, c * F + 2048 * hlf, [[0, 128], [1, 2048]])
                        dsth = bass.AP(w1dram, 16384 * hlf + 128 * c,
                                       [[NCH * F, 128], [1024, 16], [1, 128]])
                    else:
                        cols = 1024
                        qsrc = bass.AP(q2_d, 128 * c * D, [[D, 128], [1, 1024]])
                        ssrc = bass.AP(s2_d, c * D, [[0, 128], [1, 1024]])
                        dsth = bass.AP(w2dram, 128 * c * D, [[D, 128], [1, 1024]])
                    bi = ji % 2
                    if ji >= 1:
                        sch.wait("sync", "cVE", dq_ve[ji - 1])
                    sch.op("sync", lambda e, qsrc=qsrc, cols=cols, bi=bi: e.dma_start(
                        out=dq_i[bi][:, 0:cols], in_=qsrc), inc=f"dW{bi}", k=16)
                    sch.op("sync", lambda e, ssrc=ssrc, cols=cols, bi=bi: e.dma_start(
                        out=dq_s[bi][:, 0:cols], in_=ssrc), inc=f"dW{bi}", k=16)
                    v = sch.snap(f"dW{bi}")
                    if ji >= 1:
                        sch.wait("vector", "dW3", dq_dg[ji - 1])
                    sch.wait("vector", f"dW{bi}", v)
                    sch.op("vector", lambda e, cols=cols, bi=bi: e.scalar_tensor_tensor(
                        dq_o[bi][:, 0:cols], dq_i[bi][:, 0:cols], -128.0,
                        dq_s[bi][:, 0:cols], ALU.add, ALU.mult), inc="cVE")
                    dq_ve[ji] = sch.snap("cVE")
                    sch.wait("sync", "cVE", dq_ve[ji])
                    sch.op("sync", lambda e, dsth=dsth, cols=cols, bi=bi: e.dma_start(
                        out=dsth, in_=dq_o[bi][:, 0:cols]), inc="dW3", k=16)
                    dq_dg[ji] = sch.snap("dW3")

                dq_next = 0
                dq_quota = [0]
                sc_exp = {}
                pe_avl = {}
                gp_msk = {}
                ve_omul_par = [0, 0]
                ve_repevac_par = [0, 0]
                obank = [p4, p5]
                lbank = [p6, p7]
                pending_ep = [None]
                ve_rc_box = [0]
                gidx_c = [0]
                for qg in range(NQG):
                    for m in range(8):
                        K = 4 * qg + 4
                        gp = gidx_c[0] % 2

                        def emit_scores_exp(kt, qg=qg, m=m, K=K):
                            half = kt % 2
                            pt = pt_b[kt % 3]
                            sc0 = pbig[:, 1024 * half:1024 * half + 512]
                            sc1 = pbig[:, 1024 * half + 512:1024 * half + 1024]
                            prevexp = None
                            if kt >= 2:
                                prevexp = sc_exp[(qg, m, kt - 2)]
                            else:
                                pg = (qg, m - 1) if m > 0 else ((qg - 1, 7) if qg > 0 else None)
                                if pg is not None:
                                    pk = 4 * pg[0] + 4
                                    kk = pk - 2 + kt
                                    if (pg[0], pg[1], kk) in sc_exp:
                                        prevexp = sc_exp[(pg[0], pg[1], kk)]
                            if prevexp is not None:
                                sch.wait("tensor", "cSC", prevexp)
                            for jj in range(2):
                                r = 64 * jj
                                sch.op("tensor", lambda e, jj=jj, m=m, r=r, kt=kt, qg=qg,
                                       dst=(sc0 if jj == 0 else sc1): e.matmul(
                                    dst,
                                    k_att[r:r + 64, S * m + 128 * kt:S * m + 128 * (kt + 1)],
                                    q_att[r:r + 64, S * m + 512 * qg:S * m + 512 * (qg + 1)],
                                    start=True, stop=True, tile_position=(r, 0)), inc="cPE")
                            pe_sc = sch.snap("cPE")
                            sch.wait("scalar", "cPE", pe_sc)
                            prevuse = None
                            if (qg, m, kt - 3) in pe_avl:
                                prevuse = pe_avl[(qg, m, kt - 3)]
                            else:
                                pg = (qg, m - 1) if m > 0 else ((qg - 1, 7) if qg > 0 else None)
                                if pg is not None:
                                    pk = 4 * pg[0] + 4
                                    kk = pk - 3 + kt
                                    if (pg[0], pg[1], kk) in pe_avl:
                                        prevuse = pe_avl[(pg[0], pg[1], kk)]
                            if prevuse is not None:
                                sch.wait("scalar", "cPE", prevuse)
                            sch.op("scalar", lambda e, pt=pt, half=half: e.activation(
                                pt[:, 0:1024], pbig[:, 1024 * half:1024 * half + 1024],
                                AF.Exp, bias=negub[:, 0:1], scale=0.125), inc="cSC")
                            sc_exp[(qg, m, kt)] = sch.snap("cSC")
                            if kt >= 4 * qg:
                                off = (kt - 4 * qg) * 128
                                sch.wait("gpsimd", "cSC", sc_exp[(qg, m, kt)])
                                for jj in range(2):
                                    base = 512 * jj
                                    if off > 0:
                                        sch.op("gpsimd", lambda e, pt=pt, base=base, off=off: e.memset(
                                            pt[:, base:base + off], 0.0), inc="cGP")
                                    sch.op("gpsimd", lambda e, pt=pt, base=base, off=off: e.tensor_mul(
                                        pt[:, base + off:base + off + 128],
                                        pt[:, base + off:base + off + 128], msk_bf[:]), inc="cGP")
                                gp_msk[(qg, m, kt)] = sch.snap("cGP")

                        def emit_av(kt, qg=qg, m=m, K=K, gp=gp):
                            pt = pt_b[kt % 3]
                            if (qg, m, kt) in gp_msk:
                                sch.wait("tensor", "cGP", gp_msk[(qg, m, kt)])
                            else:
                                sch.wait("tensor", "cSC", sc_exp[(qg, m, kt)])
                            if kt == 0:
                                sch.wait("tensor", "cVE", ve_omul_par[gp])
                                sch.wait("tensor", "cVE", ve_repevac_par[gp])
                            for jj in range(2):
                                h = 2 * m + jj
                                cpos = 64 * jj
                                sch.op("tensor", lambda e, pt=pt, h=h, kt=kt, cpos=cpos,
                                       K=K, gp=gp: e.matmul(
                                    obank[gp][cpos:cpos + 64, :],
                                    v_tm[:, 1024 * kt + 64 * h:1024 * kt + 64 * (h + 1)],
                                    pt[:, 512 * (cpos // 64):512 * (cpos // 64 + 1)],
                                    start=(kt == 0), stop=(kt == K - 1),
                                    tile_position=(0, cpos),
                                    skip_group_check=True), inc="cPE")
                            for jj in range(2):
                                sch.op("tensor", lambda e, pt=pt, jj=jj, kt=kt, K=K, gp=gp: e.matmul(
                                    bass.AP(lbank[gp], 32 * jj * 512, [[512, 1], [1, 512]]),
                                    ones_bf[:], pt[:, 512 * jj:512 * (jj + 1)],
                                    start=(kt == 0), stop=(kt == K - 1),
                                    tile_position=(0, 32 * jj),
                                    skip_group_check=True), inc="cPE")
                            pe_avl[(qg, m, kt)] = sch.snap("cPE")

                        def mk_epilogue(qg=qg, m=m, K=K, gp=gp):
                            def ep_a():
                                pe_last = pe_avl[(qg, m, K - 1)]
                                sch.wait("vector", "cPE", pe_last)
                                for jj in range(2):
                                    sch.op("vector", lambda e, jj=jj, gp=gp: e.reciprocal(
                                        lrec[32 * jj:32 * jj + 1, :],
                                        bass.AP(lbank[gp], 32 * jj * 512, [[512, 1], [1, 512]])), inc="cVE")
                                sch.wait("vector", "cVE")
                                for jj in range(2):
                                    sch.op("vector", lambda e, jj=jj: e.tensor_copy(
                                        lrecb[32 * jj:32 * jj + 1, :],
                                        lrec[32 * jj:32 * jj + 1, :]), inc="cVE")
                                ve_rc_box[0] = sch.snap("cVE")
                            def ep_b():
                                sch.wait("tensor", "cVE", ve_rc_box[0])
                                for jj in range(2):
                                    sch.op("tensor", lambda e, jj=jj, gp=gp: e.matmul(
                                        bass.AP(lbank[gp], 64 * jj * 512, [[512, 64], [1, 512]]),
                                        ones_row[32 * jj:32 * jj + 1, 0:64],
                                        lrecb[32 * jj:32 * jj + 1, :],
                                        start=True, stop=True,
                                        tile_position=(32 * jj, 64 * jj)), inc="cPE")
                                pe_rep = sch.snap("cPE")
                                sch.wait("vector", "cPE", pe_rep)
                                sch.op("vector", lambda e, gp=gp: e.tensor_copy(
                                    rep_sb[:, 0:512], lbank[gp][:]), inc="cVE")
                                ve_repevac_par[gp] = sch.snap("cVE")
                                sch.wait("vector", "cVE")
                                sch.op("vector", lambda e, m=m, qg=qg, gp=gp: e.tensor_mul(
                                    o_sb[:, S * m + 512 * qg:S * m + 512 * (qg + 1)],
                                    obank[gp][:], rep_sb[:, 0:512]), inc="cVE")
                                ve_omul_par[gp] = sch.snap("cVE")
                            return ep_a, ep_b

                        for kt in range(K):
                            emit_scores_exp(kt)
                            if kt == 0 and pending_ep[0] is not None:
                                pending_ep[0][0]()
                            if kt == 1:
                                while dq_next < dq_quota[0]:
                                    emit_dq_job(dq_next)
                                    dq_next += 1
                            if kt >= 2:
                                emit_av(kt - 2)
                        emit_av(K - 2)
                        emit_av(K - 1)
                        if pending_ep[0] is not None:
                            pending_ep[0][1]()
                            pending_ep[0] = None
                        pending_ep[0] = mk_epilogue()
                        gidx_c[0] += 1
                        gidx = gidx_c[0]
                        if gidx <= 12:
                            dq_quota[0] = min(ndq, gidx // 3)
                        else:
                            dq_quota[0] = min(ndq, 4 + (gidx - 12) * (ndq - 4) // 18)
                pending_ep[0][0]()
                pending_ep[0][1]()
                pending_ep[0] = None
                ve_oevac_prev = max(ve_omul_par)
                while dq_next < ndq:
                    emit_dq_job(dq_next)
                    dq_next += 1
                ve_dq_all = sch.snap("cVE")
                for eng in ("sync", "vector", "scalar", "gpsimd"):
                    sch.wait(eng, "cVE", ve_dq_all)
                    sch.wait(eng, "dG0")
                    sch.wait(eng, "dG1")
                    sch.wait(eng, "dW3")
            ax.close()
            with ExitStack() as ws:
                w1t3 = [sb(ws, f"w1t3_{i}", NCH * 128, BF16) for i in range(3)]
                pe_att_end = sch.snap("cPE")
                sch.wait("sync", "cPE", pe_att_end)
                sch.wait("gpsimd", "cPE", pe_att_end)
                for ci in range(2):
                    sch.op("sync", lambda e, ci=ci: e.dma_start(
                        out=h_fm[:, 4096 * ci:4096 * (ci + 1)],
                        in_=bass.AP(hdram, 4096 * ci, [[NCH * S, 128], [1, 4096]])), inc="dH", k=16)
                for ci in range(2, 4):
                    sch.op("gpsimd", lambda e, ci=ci: e.dma_start(
                        out=h_fm[:, 4096 * ci:4096 * (ci + 1)],
                        in_=bass.AP(hdram, 4096 * ci, [[NCH * S, 128], [1, 4096]])), inc="dG0", k=16)
                dh_hw = sch.snap("dH")
                dh_sw = sch.snap("dG0")
                for fc in range(16):
                    sch.op("gpsimd", lambda e, fc=fc: e.dma_start(
                        out=w2a[:, 1024 * fc:1024 * (fc + 1)],
                        in_=bass.AP(w2dram, 128 * fc * D, [[D, 128], [1, D]])), inc="dG1", k=16)
                dg_w2a = sch.snap("dG1")

                sch.wait("tensor", "dGW", dg_wo)
                sch.wait("tensor", "cVE", ve_oevac_prev)
                ve_sev = {}
                ds_sp = {}
                pe_y = {}
                sc_ye3 = {}
                safe_ocs = list(range(0, 10)) + list(range(14, NFC))
                fc1_visits = [(ph, oc) for ph in range(2) for oc in safe_ocs] \
                    + [(ph, oc) for ph in range(2) for oc in (10, 11, 12, 13)]
                vic = [0]
                spill_fin = [None]

                def emit_fc1_visit(vi):
                    ph, oc = fc1_visits[vi]
                    wb = w1t3[vi % 3]
                    if vi >= 3:
                        sch.wait("sync", "cPE", pe_y[vi - 3])
                    else:
                        sch.wait("sync", "cVE", ve_oevac_prev)
                    sch.op("sync", lambda e, oc=oc, wb=wb: e.dma_start(
                        out=wb[:],
                        in_=bass.AP(w1dram, 1024 * oc,
                                    [[NCH * F, 128], [128, 8], [1, 128]])), inc=f"dW{vi % 3}", k=16)
                    dwv = sch.snap(f"dW{vi % 3}")
                    if vi == 0:
                        sch.wait("tensor", "dH", dh_hw)
                        sch.wait("tensor", "dG0", dh_sw)
                    sch.wait("tensor", f"dW{vi % 3}", dwv)
                    yp = p6 if vi % 2 == 0 else p7
                    if vi >= 2:
                        sch.wait("tensor", "cSC", sc_ye3[vi - 2])
                    for fc in range(NCH):
                        sch.op("tensor", lambda e, fc=fc, ph=ph, wb=wb, yp=yp: e.matmul(
                            yp[:], wb[:, 128 * fc:128 * (fc + 1)],
                            h_fm[:, S * fc + 512 * ph:S * fc + 512 * (ph + 1)],
                            start=(fc == 0), stop=(fc == NCH - 1)), inc="cPE")
                    pe_y[vi] = sch.snap("cPE")
                    sch.wait("scalar", "cPE", pe_y[vi])
                    if oc in (10, 11, 12, 13):
                        sch.wait("scalar", "dS0", spill_fin[0][0])
                        sch.wait("scalar", "dS1", spill_fin[0][1])
                    sch.op("scalar", lambda e, oc=oc, ph=ph, yp=yp: e.activation(
                        y_sl[:, 1024 * oc + 512 * ph:1024 * oc + 512 * (ph + 1)], yp[:],
                        AF.Silu, bias=b1_cm[:, oc:oc + 1]), inc="cSC")
                    sc_ye3[vi] = sch.snap("cSC")

                for tt in range(NT):
                    za = p4[:] if tt % 2 == 0 else pbig[:, 0:512]
                    zb = p5[:] if tt % 2 == 0 else pbig[:, 512:1024]
                    if tt >= 2:
                        sch.wait("tensor", "cVE", ve_sev[tt - 2])
                    for oc in range(NCH):
                        sch.op("tensor", lambda e, oc=oc, tt=tt, za=za: e.matmul(
                            za, o_sb[:, S * oc + 128 * tt:S * oc + 128 * (tt + 1)],
                            wo_bf[:, 1024 * oc:1024 * oc + 512],
                            start=(oc == 0), stop=(oc == NCH - 1)), inc="cPE")
                        sch.op("tensor", lambda e, oc=oc, tt=tt, zb=zb: e.matmul(
                            zb, o_sb[:, S * oc + 128 * tt:S * oc + 128 * (tt + 1)],
                            wo_bf[:, 1024 * oc + 512:1024 * (oc + 1)],
                            start=(oc == 0), stop=(oc == NCH - 1)), inc="cPE")
                    pe_s = sch.snap("cPE")
                    st = s_st[tt % 2]
                    sch.wait("vector", "cPE", pe_s)
                    if tt >= 2:
                        sch.wait("vector", f"dS{tt % 2}", ds_sp[tt - 2])
                    sch.op("vector", lambda e, st=st, za=za: e.tensor_copy(
                        st[:, 0:512], za), inc="cVE")
                    sch.op("vector", lambda e, st=st, zb=zb: e.tensor_copy(
                        st[:, 512:1024], zb), inc="cVE")
                    ve_sev[tt] = sch.snap("cVE")
                    sch.wait("scalar", "cVE", ve_sev[tt])
                    sch.op("scalar", lambda e, st=st, tt=tt: e.dma_start(
                        out=sdram[128 * tt:128 * (tt + 1), :], in_=st[:]), inc=f"dS{tt % 2}", k=16)
                    ds_sp[tt] = sch.snap(f"dS{tt % 2}")
                    if tt == 7:
                        pe_wo_mid = pe_s
                    if tt == 8:
                        sch.wait("sync", "cPE", pe_wo_mid)
                        for fc in range(16, NFC, 2):
                            sch.op("sync", lambda e, fc=fc: e.dma_start(
                                out=w2b[:, 1024 * (fc - 16):1024 * (fc - 15)],
                                in_=bass.AP(w2dram, 128 * fc * D, [[D, 128], [1, D]])), inc="dW", k=16)
                    if tt == NT - 1:
                        spill_fin[0] = (sch.snap("dS0"), sch.snap("dS1"))
                        pe_wo_end = pe_s
                    if tt >= 2:
                        for _ in range(4):
                            emit_fc1_visit(vic[0])
                            vic[0] += 1
                while vic[0] < len(fc1_visits):
                    emit_fc1_visit(vic[0])
                    vic[0] += 1
                sch.wait("sync", "cPE", pe_wo_end)
                for fc in range(17, NFC, 2):
                    sch.op("sync", lambda e, fc=fc: e.dma_start(
                        out=w2b[:, 1024 * (fc - 16):1024 * (fc - 15)],
                        in_=bass.AP(w2dram, 128 * fc * D, [[D, 128], [1, D]])), inc="dW", k=16)
                dw_w2 = sch.snap("dW")
                b3_fc1_sc = sch.snap("cSC")
                b3_sev_last = ve_sev[NT - 1]
                run_block(nc, "b3", sch)
        if upto < 4:
            return nc

        with ExitStack() as ps:
            pbig = ps.enter_context(nc.psum_tensor("pbig4", [128, 2048], F32))
            p4 = ps.enter_context(nc.psum_tensor("p4_4", [128, 512], F32))
            p5 = ps.enter_context(nc.psum_tensor("p5_4", [128, 512], F32))
            p6 = ps.enter_context(nc.psum_tensor("p6_4", [128, 512], F32))
            p7 = ps.enter_context(nc.psum_tensor("p7_4", [128, 512], F32))
            xb2 = [sb(ps, f"xb2_{i}", D, F32) for i in range(2)]
            sb2 = [sb(ps, f"sb2_{i}", D, F32) for i in range(2)]
            t_f = [sb(ps, f"t_f{i}", D, F32) for i in range(2)]
            f_sb = [sb(ps, f"f_sb{i}", D, F32) for i in range(2)]
            w1t = [sb(ps, f"w1t{i}", NCH * 128, BF16) for i in range(3)]

            sch = Sched(cts)
            sch.wait("sync", "dG0")
            sch.wait("sync", "dG1")
            sch.wait("sync", "dS0")
            sch.wait("sync", "dS1")
            dw_pre = {}
            for oc in range(3):
                sch.op("sync", lambda e, oc=oc: e.dma_start(
                    out=w1t[oc][:],
                    in_=bass.AP(w1dram, 1024 * oc,
                                [[NCH * F, 128], [128, 8], [1, 128]])), inc=f"dW{oc % 3}", k=16)
                dw_pre[oc] = sch.snap(f"dW{oc % 3}")

            pe_z = {}
            ve_ee = {}
            sc_g = {}
            o_done = {}
            ds_out = {}
            NP = 2
            for p in range(NP):
                if p == 1:
                    pe_v2 = {}
                    sc_y2 = {}
                    for oc in range(NFC):
                        wb = w1t[oc % 3]
                        if oc >= 3:
                            sch.wait("sync", "cPE", pe_v2[(oc - 3, 1)])
                            sch.op("sync", lambda e, oc=oc, wb=wb: e.dma_start(
                                out=wb[:],
                                in_=bass.AP(w1dram, 1024 * oc,
                                            [[NCH * F, 128], [128, 8], [1, 128]])), inc=f"dW{oc % 3}", k=16)
                            dw_now = sch.snap(f"dW{oc % 3}")
                        else:
                            dw_now = dw_pre[oc]
                        sch.wait("tensor", f"dW{oc % 3}", dw_now)
                        for ph in range(2):
                            yp = p6 if ph == 0 else p7
                            if oc >= 1:
                                sch.wait("tensor", "cSC", sc_y2[(oc - 1, ph)])
                            for fc in range(NCH):
                                sch.op("tensor", lambda e, fc=fc, ph=ph, wb=wb, yp=yp: e.matmul(
                                    yp[:], wb[:, 128 * fc:128 * (fc + 1)],
                                    h_fm[:, S * fc + 1024 + 512 * ph:S * fc + 1024 + 512 * (ph + 1)],
                                    start=(fc == 0), stop=(fc == NCH - 1)), inc="cPE")
                            pe_v2[(oc, ph)] = sch.snap("cPE")
                            sch.wait("scalar", "cPE", pe_v2[(oc, ph)])
                            if oc < 2:
                                sch.wait("scalar", "cPE", pe_z[(p - 1, 7)])
                            sch.op("scalar", lambda e, oc=oc, ph=ph, yp=yp: e.activation(
                                y_sl[:, 1024 * oc + 512 * ph:1024 * oc + 512 * (ph + 1)], yp[:],
                                AF.Silu, bias=b1_cm[:, oc:oc + 1]), inc="cSC")
                            sc_y2[(oc, ph)] = sch.snap("cSC")
                    sc_fc1_all = sch.snap("cSC")
                else:
                    sc_fc1_all = b3_fc1_sc
                for ttl in range(8):
                    tt = 8 * p + ttl
                    b = tt % 2
                    za = p4[:] if b == 0 else pbig[:, 0:512]
                    zb = p5[:] if b == 0 else pbig[:, 512:1024]
                    if ttl == 0:
                        sch.wait("tensor", "cSC", sc_fc1_all)
                        if p == 0:
                            sch.wait("tensor", "cVE", b3_sev_last)
                            sch.wait("tensor", "dG1", dg_w2a)
                    if tt >= 2:
                        sch.wait("tensor", "cVE", ve_ee[tt - 2])
                    for fc in range(NFC):
                        if p == 0 and ttl == 0 and fc == 16:
                            sch.wait("tensor", "dW", dw_w2)
                        sch.op("tensor", lambda e, fc=fc, ttl=ttl, za=za: e.matmul(
                            za, y_sl[:, 1024 * fc + 128 * ttl:1024 * fc + 128 * (ttl + 1)],
                            (w2a if fc < 16 else w2b)[:, 1024 * (fc % 16):1024 * (fc % 16) + 512],
                            start=(fc == 0), stop=(fc == NFC - 1)), inc="cPE")
                        sch.op("tensor", lambda e, fc=fc, ttl=ttl, zb=zb: e.matmul(
                            zb, y_sl[:, 1024 * fc + 128 * ttl:1024 * fc + 128 * (ttl + 1)],
                            (w2a if fc < 16 else w2b)[:, 1024 * (fc % 16) + 512:1024 * (fc % 16 + 1)],
                            start=(fc == 0), stop=(fc == NFC - 1)), inc="cPE")
                    pe_z[(p, ttl)] = sch.snap("cPE")
                    xb, sbt = xb2[b], sb2[b]
                    if tt >= 2:
                        sem, val = o_done[tt - 2]
                        sch.wait("sync", sem, val)
                    if tt >= 2:
                        sch.wait("sync", "cVE", ve_ee[tt - 2])
                    sch.op("sync", lambda e, xb=xb, tt=tt: e.dma_start(
                        out=xb[:], in_=x_d[128 * tt:128 * (tt + 1), :]), inc=f"dX{tt % 2}", k=16)
                    sch.op("sync", lambda e, sbt=sbt, tt=tt: e.dma_start(
                        out=sbt[:], in_=sdram[128 * tt:128 * (tt + 1), :]), inc=f"dX{tt % 2}", k=16)
                    dl_e = sch.snap(f"dX{tt % 2}")
                    fs = f_sb[b]
                    tf = t_f[b]
                    sch.wait("vector", f"dX{tt % 2}", dl_e)
                    sch.wait("vector", "cPE", pe_z[(p, ttl)])
                    sch.op("vector", lambda e, xb=xb, tt=tt, tf=tf: e.tensor_scalar(
                        tf[:], xb[:], negmu_a[:, tt:tt + 1], r1_a[:, tt:tt + 1],
                        ALU.add, ALU.mult), inc="cVE")
                    sch.wait("vector", "cVE")
                    sch.op("vector", lambda e, tf=tf: e.tensor_add(tf[:], tf[:], b2h_rep[:]), inc="cVE")
                    sch.wait("vector", "cVE")
                    if tt >= 2:
                        sch.wait("vector", "cSC", sc_g[tt - 2])
                        sch.wait("vector", f"dS{tt % 2}", ds_out[tt - 2])
                    sch.op("vector", lambda e, fs=fs, za=za, tf=tf: e.scalar_tensor_tensor(
                        fs[:, 0:512], za, 0.5, tf[:, 0:512], ALU.mult, ALU.add), inc="cVE")
                    sch.op("vector", lambda e, fs=fs, zb=zb, tf=tf: e.scalar_tensor_tensor(
                        fs[:, 512:1024], zb, 0.5, tf[:, 512:1024], ALU.mult, ALU.add), inc="cVE")
                    ve_ee[tt] = sch.snap("cVE")
                    sch.wait("scalar", "cVE", ve_ee[tt])
                    if tt >= 2:
                        sem2, val2 = o_done[tt - 2]
                        sch.wait("scalar", sem2, val2)
                    sch.op("scalar", lambda e, fs=fs: e.activation(
                        fs[:], fs[:], AF.Silu), inc="cSC")
                    sc_g[tt] = sch.snap("cSC")
                    if ttl < 5:
                        sch.wait("gpsimd", "cSC", sc_g[tt])
                        sch.wait("gpsimd", f"dX{tt % 2}", dl_e)
                        if tt >= 2:
                            sch.wait("gpsimd", f"dS{tt % 2}", ds_out[tt - 2])
                        sch.op("gpsimd", lambda e, fs=fs, sbt=sbt: e.tensor_mul(
                            fs[:], fs[:], sbt[:]), inc="cGP")
                        sch.wait("gpsimd", "cGP")
                        sch.op("gpsimd", lambda e, xb=xb, fs=fs: e.tensor_add(
                            fs[:], fs[:], xb[:]), inc="cGP")
                        o_done[tt] = ("cGP", sch.snap("cGP"))
                    else:
                        sch.wait("vector", "cSC", sc_g[tt])
                        sch.op("vector", lambda e, fs=fs, sbt=sbt: e.tensor_mul(
                            fs[:], fs[:], sbt[:]), inc="cVE")
                        sch.wait("vector", "cVE")
                        sch.op("vector", lambda e, xb=xb, fs=fs: e.tensor_add(
                            fs[:], fs[:], xb[:]), inc="cVE")
                        o_done[tt] = ("cVE", sch.snap("cVE"))
                    sem, val = o_done[tt]
                    sch.wait("scalar", sem, val)
                    sch.op("scalar", lambda e, fs=fs, tt=tt: e.dma_start(
                        out=out_d[128 * tt:128 * (tt + 1), :], in_=fs[:]), inc=f"dS{tt % 2}", k=16)
                    ds_out[tt] = sch.snap(f"dS{tt % 2}")
            sch.wait("scalar", "dS0")
            sch.wait("scalar", "dS1")
            run_block(nc, "b4", sch)

    return nc


_CONST_CACHE = {}


def _constants():
    if _CONST_CACHE:
        return _CONST_CACHE
    cols_E = [64 * (4 * cp + j) + 2 * i for cp in range(4) for j in range(4) for i in range(32)]
    _CONST_CACHE["wperm"] = np.array(cols_E + [c + 1 for c in cols_E])
    i = np.arange(32)
    inv_freq = 10000.0 ** (-(2.0 * i) / DK)
    t = np.arange(S, dtype=np.float64)
    ang = np.tile(np.outer(inv_freq, t), (4, 1))

    def split(theta):
        k = np.round(theta / np.pi)
        thr = theta - k * np.pi
        sg = np.where(k % 2 == 0, 1.0, -1.0)
        hi = thr.astype(np.float32)
        lo = (thr - hi.astype(np.float64)).astype(np.float32)
        return hi, lo, sg.astype(np.float32)

    s_hi, s_lo, s_sg = split(ang)
    c_hi, c_lo, c_sg = split(ang + np.pi / 2)
    _CONST_CACHE.update(
        ths_hi=s_hi, ths_lo=s_lo, sgs=s_sg,
        thc_hi=c_hi, thc_lo=c_lo, sgc=c_sg,
        ident=np.eye(128, dtype=np.float32),
        cmask=(np.arange(128)[:, None] <= np.arange(128)[None, :]).astype(np.float32),
    )
    return _CONST_CACHE


def kernel(L0_attn_x, L0_attn_Wq, L0_attn_Wk, L0_attn_Wv, L0_attn_Wo,
           L0_ffn_ln_gamma, L0_ffn_ln_beta, L0_ffn_fc1_q, L0_ffn_fc1_scales,
           L0_ffn_fc1_bias, L0_ffn_fc2_q, L0_ffn_fc2_scales, L0_ffn_fc2_bias):
    c = _constants()
    x = np.ascontiguousarray(np.asarray(L0_attn_x, dtype=np.float32))
    shared = {
        "wq": np.ascontiguousarray(np.asarray(L0_attn_Wq, np.float32)[:, c["wperm"]]),
        "wk": np.ascontiguousarray(np.asarray(L0_attn_Wk, np.float32)[:, c["wperm"]]),
        "wv": np.ascontiguousarray(np.asarray(L0_attn_Wv, np.float32)),
        "wo": np.ascontiguousarray(np.asarray(L0_attn_Wo, np.float32)),
        "ln_gamma": np.ascontiguousarray(np.asarray(L0_ffn_ln_gamma, np.float32)),
        "ln_beta": np.ascontiguousarray(np.asarray(L0_ffn_ln_beta, np.float32)),
        "fc1_q": np.ascontiguousarray(np.asarray(L0_ffn_fc1_q, np.int32)),
        "fc1_scales": np.ascontiguousarray(np.asarray(L0_ffn_fc1_scales, np.float32)),
        "fc1_bias": np.ascontiguousarray(np.asarray(L0_ffn_fc1_bias, np.float32)),
        "fc2_q": np.ascontiguousarray(np.asarray(L0_ffn_fc2_q, np.int32)),
        "fc2_scales": np.ascontiguousarray(np.asarray(L0_ffn_fc2_scales, np.float32)),
        "fc2_bias": np.ascontiguousarray(np.asarray(L0_ffn_fc2_bias, np.float32)),
        "ths_hi": c["ths_hi"], "ths_lo": c["ths_lo"], "sgs": c["sgs"],
        "thc_hi": c["thc_hi"], "thc_lo": c["thc_lo"], "sgc": c["sgc"],
        "ident": c["ident"], "cmask": c["cmask"],
    }
    in_maps = [dict(shared, x=np.ascontiguousarray(x[b])) for b in range(8)]
    nc = build_nc()
    res = run_bass_kernel_spmd(nc, in_maps, core_ids=list(range(8)))
    return np.stack([res.results[b]["out"] for b in range(8)], axis=0)


# version 40
# speedup vs baseline: 1.0337x; 1.0111x over previous
import numpy as np
import ml_dtypes
from contextlib import ExitStack
import concourse.bass as bass
import concourse.mybir as mybir
from concourse.bass_utils import run_bass_kernel_spmd

F32 = mybir.dt.float32
BF16 = mybir.dt.bfloat16
I32 = mybir.dt.int32
AF = mybir.ActivationFunctionType
ALU = mybir.AluOpType
AX = mybir.AxisListType

S = 2048
D = 1024
H = 16
DK = 64
F = 4096
NT = S // 128
NQG = S // 512
NCH = D // 128
NFC = F // 128
EPS = 1e-5
LN8 = float(np.log(0.125))

ENGS = ("sync", "scalar", "vector", "tensor", "gpsimd")


class Ct:
    __slots__ = ("h", "n")

    def __init__(self, h):
        self.h = h
        self.n = 0


class Sched:

    def __init__(self, cts):
        self.ct = cts
        self.streams = {e: [] for e in ENGS}

    def op(self, eng, fn, inc=None, k=1):
        if inc is not None:
            c = self.ct[inc]
            c.n += k

            def g(e, fn=fn, h=c.h, k=k):
                fn(e).then_inc(h, k)
            self.streams[eng].append(g)
        else:
            self.streams[eng].append(fn)

    def wait(self, eng, name, val=None):
        c = self.ct[name]
        v = c.n if val is None else val
        self.streams[eng].append(lambda e, h=c.h, v=v: e.wait_ge(h, v))

    def snap(self, name):
        return self.ct[name].n


def run_block(nc, name, sch):
    with nc.Block(name) as blk:
        for eng in ENGS:
            fns = sch.streams[eng]
            if fns:
                getattr(blk, eng)(lambda e, fns=fns: [f(e) for f in fns])


def fap(t, free_dims, offset=0):
    fe = t.shape[1]
    return bass.AP(t, offset, [[fe, 128]] + [list(d) for d in free_dims])


def pap(t, p0, np_, free_dims, offset=0):
    fe = t.shape[1]
    return bass.AP(t, offset + p0 * fe, [list(d) for d in free_dims[:0]] or None) if False else \
        bass.AP(t, offset + p0 * fe, [[fe, np_]] + [list(d) for d in free_dims])


def strided_parts(t, pstep, pcount, free_dims, offset=0):
    fe = t.shape[1]
    return bass.AP(t, offset, [[fe * pstep, pcount]] + [list(d) for d in free_dims])


MANUAL_BASE = 16896
MANUAL_TOP = MANUAL_BASE + 163840
R1, R2, R3, R4, R5 = (MANUAL_BASE, MANUAL_BASE + 32768, MANUAL_BASE + 65536,
                      MANUAL_BASE + 98304, MANUAL_BASE + 131072)


def build_nc(upto=9):
    nc = bass.Bass("TRN2", target_bir_lowering=False)
    nc.sbuf_base = MANUAL_TOP

    x_d = nc.dram_tensor("x", [S, D], F32, kind="ExternalInput")
    wq_d = nc.dram_tensor("wq", [D, D], F32, kind="ExternalInput")
    wk_d = nc.dram_tensor("wk", [D, D], F32, kind="ExternalInput")
    wv_d = nc.dram_tensor("wv", [D, D], F32, kind="ExternalInput")
    wo_d = nc.dram_tensor("wo", [D, D], F32, kind="ExternalInput")
    gam_d = nc.dram_tensor("ln_gamma", [D], F32, kind="ExternalInput")
    bet_d = nc.dram_tensor("ln_beta", [D], F32, kind="ExternalInput")
    q1_d = nc.dram_tensor("fc1_q", [D, F], I32, kind="ExternalInput")
    s1_d = nc.dram_tensor("fc1_scales", [D // 128, F], F32, kind="ExternalInput")
    b1_d = nc.dram_tensor("fc1_bias", [F], F32, kind="ExternalInput")
    q2_d = nc.dram_tensor("fc2_q", [F, D], I32, kind="ExternalInput")
    s2_d = nc.dram_tensor("fc2_scales", [F // 128, D], F32, kind="ExternalInput")
    b2_d = nc.dram_tensor("fc2_bias", [D], F32, kind="ExternalInput")
    ang_d = {}
    for nm in ("ths_hi", "ths_lo", "sgs", "thc_hi", "thc_lo", "sgc"):
        ang_d[nm] = nc.dram_tensor(nm, [128, S], F32, kind="ExternalInput")
    idn_d = nc.dram_tensor("ident", [128, 128], F32, kind="ExternalInput")
    msk_d = nc.dram_tensor("cmask", [128, 128], F32, kind="ExternalInput")
    out_d = nc.dram_tensor("out", [S, D], F32, kind="ExternalOutput")

    sdram = nc.dram_tensor("s_spill", [S, D], F32, kind="Internal")
    hdram = nc.dram_tensor("h_spill", [128, NCH * S], BF16, kind="Internal")
    g2dram = nc.dram_tensor("g2_row", [16, 128], F32, kind="Internal")
    w1dram = nc.dram_tensor("w1deq", [128, NCH * F], BF16, kind="Internal")
    w2dram = nc.dram_tensor("w2deq", [F, D], BF16, kind="Internal")

    cts = {}

    with ExitStack() as es:
        for name in ("dC", "dX0", "dX1", "dW", "dW0", "dW1", "dW2", "dW3",
                     "dR0", "dR1", "dS", "dS0", "dS1", "dG0", "dG1", "dGW", "dH",
                     "cPE", "cVE", "cSC", "cGP"):
            cts[name] = Ct(es.enter_context(nc.semaphore(name)))

        def sb(stack, name, cols, dt):
            return stack.enter_context(nc.sbuf_tensor(name, [128, cols], dt))

        gam_cm = sb(es, "gam_cm", NCH, F32)
        bet_cm = sb(es, "bet_cm", NCH, F32)
        g2_a = sb(es, "g2_a", NT, F32)
        b2h_rep = sb(es, "b2h_rep", D, F32)
        b1_cm = sb(es, "b1_cm", NFC, F32)
        idn_bf = sb(es, "idn_bf", 128, BF16)
        idn_f32 = sb(es, "idn_f32", 128, F32)
        msk_bf = sb(es, "msk_bf", 128, BF16)
        ones_bf = sb(es, "ones_bf", 1, BF16)
        selq = sb(es, "selq", 4, BF16)
        ones_row = sb(es, "ones_row", 128, BF16)
        negmu_a = sb(es, "negmu_a", NT, F32)
        r1_a = sb(es, "r1_a", NT, F32)
        rg_a = sb(es, "rg_a", NT, F32)
        negub = sb(es, "negub", 1, F32)
        stat = sb(es, "stat", 16, F32)
        biasc = sb(es, "biasc", 2, F32)
        statb = sb(es, "statb", 2, BF16)

        xn_fm = nc.alloc_sbuf_tensor_at("xn_fm", [128, NCH * S], BF16, offset=R1).ap().tensor
        q_att = nc.alloc_sbuf_tensor_at("q_att", [128, (H // 2) * S], BF16, offset=R2).ap().tensor
        k_att = nc.alloc_sbuf_tensor_at("k_att", [128, (H // 2) * S], BF16, offset=R3).ap().tensor
        v_tm = nc.alloc_sbuf_tensor_at("v_tm", [128, NT * D], BF16, offset=R4).ap().tensor
        cos_t = nc.alloc_sbuf_tensor_at("cos_sb", [128, S], BF16, offset=R5).ap().tensor
        sin_t = nc.alloc_sbuf_tensor_at("sin_sb", [128, S], BF16, offset=R5 + 4096).ap().tensor
        wv_bf = nc.alloc_sbuf_tensor_at("wv_bf", [128, NCH * D], BF16, offset=R5 + 8192).ap().tensor
        o_sb = nc.alloc_sbuf_tensor_at("o_sb", [128, NCH * S], BF16, offset=R5).ap().tensor
        dq_i0 = nc.alloc_sbuf_tensor_at("dq_i0", [128, 2048], I32, offset=R1).ap().tensor
        dq_s0 = nc.alloc_sbuf_tensor_at("dq_s0", [128, 2048], F32, offset=R1 + 8192).ap().tensor
        dq_o0 = nc.alloc_sbuf_tensor_at("dq_o0", [128, 2048], BF16, offset=R1 + 16384).ap().tensor
        s_st = [nc.alloc_sbuf_tensor_at(f"s_st{i}", [128, D], F32, offset=R1 + 20480 + 4096 * i).ap().tensor
                for i in range(2)]
        h_fm = nc.alloc_sbuf_tensor_at("h_fm", [128, NCH * S], BF16, offset=R4).ap().tensor
        w2a = nc.alloc_sbuf_tensor_at("w2a", [128, 16 * D], BF16, offset=R3).ap().tensor
        w2b = nc.alloc_sbuf_tensor_at("w2b", [128, 16 * D], BF16, offset=R5).ap().tensor
        y_sl = nc.alloc_sbuf_tensor_at("y_sl", [128, NFC * 1024], BF16, offset=R1).ap().tensor
        with ExitStack() as ps:
            idn_f = sb(ps, "idn_f", 128, F32)
            msk_f = sb(ps, "msk_f", 128, F32)
            cs_a = sb(ps, "cs_a", S, F32)
            cs_b = sb(ps, "cs_b", S, F32)
            sch = Sched(cts)
            for ti, (hi, lo, sg, dst_t) in enumerate(
                    (("ths_hi", "ths_lo", "sgs", sin_t),
                     ("thc_hi", "thc_lo", "sgc", cos_t))):
                if ti == 1:
                    sch.wait("sync", "cVE")
                sch.op("sync", lambda e, hi=hi: e.dma_start(
                    out=cs_a[:], in_=ang_d[hi][:]), inc="dX0", k=16)
                sch.op("sync", lambda e, lo=lo: e.dma_start(
                    out=cs_b[:], in_=ang_d[lo][:]), inc="dX0", k=16)
                sch.wait("vector", "dX0")
                sch.op("vector", lambda e: e.tensor_add(cs_a[:], cs_a[:], cs_b[:]), inc="cVE")
                ve_th = sch.snap("cVE")
                sch.wait("sync", "cVE", ve_th)
                sch.op("sync", lambda e, sg=sg: e.dma_start(
                    out=cs_b[:], in_=ang_d[sg][:]), inc="dX0", k=16)
                sch.wait("scalar", "cVE", ve_th)
                sch.op("scalar", lambda e: e.activation(
                    cs_a[:], cs_a[:], AF.Sin), inc="cSC")
                sch.wait("vector", "cSC")
                sch.wait("vector", "dX0")
                sch.op("vector", lambda e, dst_t=dst_t: e.tensor_mul(
                    dst_t[:], cs_a[:], cs_b[:]), inc="cVE")
            for fc in range(NCH):
                sch.op("sync", lambda e, fc=fc: e.dma_start(
                    out=gam_cm[:, fc:fc + 1],
                    in_=bass.AP(gam_d, 128 * fc, [[1, 128], [1, 1]])), inc="dC", k=16)
                sch.op("sync", lambda e, fc=fc: e.dma_start(
                    out=bet_cm[:, fc:fc + 1],
                    in_=bass.AP(bet_d, 128 * fc, [[1, 128], [1, 1]])), inc="dC", k=16)
            sch.op("sync", lambda e: e.dma_start(
                out=b2h_rep[:], in_=bass.AP(b2_d, 0, [[0, 128], [1, D]])), inc="dC", k=16)
            for oc in range(NFC):
                sch.op("sync", lambda e, oc=oc: e.dma_start(
                    out=b1_cm[:, oc:oc + 1],
                    in_=bass.AP(b1_d, 128 * oc, [[1, 128], [1, 1]])), inc="dC", k=16)
            sch.op("sync", lambda e: e.dma_start(out=idn_f[:], in_=idn_d[:]), inc="dC", k=16)
            sch.op("sync", lambda e: e.dma_start(out=msk_f[:], in_=msk_d[:]), inc="dC", k=16)
            sch.wait("vector", "dC")
            sch.op("vector", lambda e: e.tensor_copy(idn_bf[:], idn_f[:]), inc="cVE")
            sch.op("vector", lambda e: e.tensor_copy(idn_f32[:], idn_f[:]), inc="cVE")
            sch.op("vector", lambda e: e.tensor_copy(msk_bf[:], msk_f[:]), inc="cVE")
            sch.op("vector", lambda e: e.tensor_scalar_mul(b2h_rep[:], b2h_rep[:], 0.5), inc="cVE")

            sch.op("gpsimd", lambda e: e.memset(ones_bf[:], 1.0), inc="cGP")
            sch.op("gpsimd", lambda e: e.memset(selq[:], 0.0), inc="cGP")
            sch.wait("gpsimd", "cGP")
            for j in range(4):
                sch.op("gpsimd", lambda e, j=j: e.memset(
                    selq[32 * j:32 * (j + 1), j:j + 1], 1.0), inc="cGP")
            sch.op("gpsimd", lambda e: e.memset(biasc[:, 0:1], EPS), inc="cGP")
            sch.op("gpsimd", lambda e: e.memset(biasc[:, 1:2], LN8), inc="cGP")
            sch.op("gpsimd", lambda e: e.memset(ones_row[:], 1.0), inc="cGP")
            run_block(nc, "b0", sch)
        if upto < 1:
            return nc

        with ExitStack() as ps:
            xbuf = [sb(ps, f"xb{i}", D, F32) for i in range(2)]
            xnb = [sb(ps, f"xnb{i}", D, BF16) for i in range(2)]
            sqsc = [sb(ps, f"sqsc{i}", D, F32) for i in range(2)]
            st2 = sb(ps, "st2", 80, F32)
            pT = [ps.enter_context(nc.psum_tensor(f"pT{i}", [128, D], BF16)) for i in range(2)]
            pTf = ps.enter_context(nc.psum_tensor("pTf", [128, 128], F32))

            sch = Sched(cts)
            for fc in range(NCH):
                sch.op("gpsimd", lambda e, fc=fc: e.dma_start(
                    out=wv_bf[:, 1024 * fc:1024 * (fc + 1)],
                    in_=bass.AP(wv_d, 128 * fc * D, [[D, 128], [1, D]])), inc="dG0", k=16)
            dg_wv = sch.snap("dG0")
            ve_p1 = {}
            sc_p1 = {}
            for tt in range(NT):
                b = tt % 2
                xb = xbuf[b]
                if tt >= 2:
                    sch.wait("sync", "cVE", ve_p1[tt - 2])
                    sch.wait("sync", "cSC", sc_p1[tt - 2])
                sch.op("sync", lambda e, xb=xb, tt=tt: e.dma_start(
                    out=xb[:], in_=x_d[128 * tt:128 * (tt + 1), :]), inc=f"dX{b}", k=16)
                v = sch.snap(f"dX{b}")
                sch.wait("vector", f"dX{b}", v)
                sch.op("vector", lambda e, xb=xb, tt=tt: e.reduce_sum(
                    st2[:, tt:tt + 1], xb[:], axis=AX.X), inc="cVE")
                ve_p1[tt] = sch.snap("cVE")
                sch.wait("scalar", f"dX{b}", v)
                sch.op("scalar", lambda e, xb=xb, tt=tt: e.activation(
                    sqsc[tt % 2][:], xb[:], AF.Square,
                    accum_out=st2[:, 16 + tt:17 + tt]), inc="cSC")
                sc_p1[tt] = sch.snap("cSC")
            ve_rs_all = sch.snap("cVE")
            sc_sq_all = sch.snap("cSC")
            sch.wait("vector", "cSC", sc_sq_all)
            sch.wait("vector", "cVE", ve_rs_all)
            sch.op("vector", lambda e: e.tensor_scalar_mul(
                negmu_a[:], st2[:, 0:16], -1.0 / D), inc="cVE")
            sch.wait("vector", "cVE")
            sch.op("vector", lambda e: e.tensor_scalar_mul(
                st2[:, 32:48], st2[:, 16:32], 1.0 / D), inc="cVE")
            ve_1 = sch.snap("cVE")
            sch.wait("scalar", "cVE", ve_1)
            sch.op("scalar", lambda e: e.activation(
                st2[:, 48:64], negmu_a[:], AF.Square), inc="cSC")
            sc_1 = sch.snap("cSC")
            sch.wait("vector", "cSC", sc_1)
            sch.op("vector", lambda e: e.scalar_tensor_tensor(
                st2[:, 48:64], st2[:, 48:64], -1.0, st2[:, 32:48],
                ALU.mult, ALU.add), inc="cVE")
            ve_2 = sch.snap("cVE")
            sch.wait("scalar", "cVE", ve_2)
            sch.op("scalar", lambda e: e.activation(
                st2[:, 32:48], st2[:, 48:64], AF.Ln, bias=biasc[:, 0:1]), inc="cSC")
            sch.wait("scalar", "cSC")
            sch.op("scalar", lambda e: e.activation(
                r1_a[:], st2[:, 32:48], AF.Exp, scale=-0.5), inc="cSC")
            sch.op("scalar", lambda e: e.activation(
                st2[:, 64:80], st2[:, 32:48], AF.Exp, scale=-1.0), inc="cSC")
            sc_2 = sch.snap("cSC")
            sch.wait("vector", "cSC", sc_2)
            sch.op("vector", lambda e: e.tensor_mul(
                st2[:, 48:64], st2[:, 48:64], st2[:, 64:80]), inc="cVE")
            ve_3 = sch.snap("cVE")
            sch.wait("scalar", "cVE", ve_3)
            sch.op("scalar", lambda e: e.activation(
                st2[:, 48:64], st2[:, 48:64], AF.Ln, bias=biasc[:, 0:1]), inc="cSC")
            sch.wait("scalar", "cSC")
            sch.op("scalar", lambda e: e.activation(
                st2[:, 48:64], st2[:, 48:64], AF.Exp, scale=-0.5), inc="cSC")
            sc_3 = sch.snap("cSC")
            sch.wait("vector", "cSC", sc_3)
            sch.op("vector", lambda e: e.tensor_copy(g2_a[:], st2[:, 48:64]), inc="cVE")
            sch.op("vector", lambda e: e.tensor_mul(
                rg_a[:], r1_a[:], st2[:, 48:64]), inc="cVE")
            ve_stats = sch.snap("cVE")
            ve_xn = {}
            sc_evac2 = {}
            pe_tp = {}
            dx2 = {}
            sch.wait("vector", "cVE", ve_stats)
            for tt in range(NT):
                b = tt % 2
                xb, xn1, pt = xbuf[b], xnb[b], pT[b]
                if tt >= 2:
                    sch.wait("sync", "cVE", ve_xn[tt - 2])
                else:
                    sch.wait("sync", "cVE", ve_stats)
                    sch.wait("sync", "cSC", sc_3)
                sch.op("sync", lambda e, xb=xb, tt=tt: e.dma_start(
                    out=xb[:], in_=x_d[128 * tt:128 * (tt + 1), :]), inc=f"dX{b}", k=16)
                dx2[tt] = sch.snap(f"dX{b}")
                sch.wait("vector", f"dX{b}", dx2[tt])
                if tt >= 2:
                    sch.wait("vector", "cPE", pe_tp[tt - 2])
                sch.op("vector", lambda e, xb=xb, xn1=xn1, tt=tt: e.tensor_scalar(
                    xn1[:], xb[:], negmu_a[:, tt:tt + 1], r1_a[:, tt:tt + 1],
                    ALU.add, ALU.mult), inc="cVE")
                ve_xn[tt] = sch.snap("cVE")
                sch.wait("tensor", "cVE", ve_xn[tt])
                if tt >= 2:
                    sch.wait("tensor", "cSC", sc_evac2[tt - 2])
                for fc in range(NCH):
                    sch.op("tensor", lambda e, xn1=xn1, pt=pt, fc=fc: e.transpose(
                        pt[:, 128 * fc:128 * (fc + 1)],
                        xn1[:, 128 * fc:128 * (fc + 1)], idn_bf[:]), inc="cPE")
                pe_tp[tt] = sch.snap("cPE")
                sch.wait("scalar", "cPE", pe_tp[tt])
                sch.op("scalar", lambda e, pt=pt, tt=tt: e.activation(
                    fap(xn_fm, [[S, NCH], [1, 128]], offset=128 * tt),
                    fap(pt, [[128, NCH], [1, 128]]), AF.Copy), inc="cSC")
                sc_evac2[tt] = sch.snap("cSC")
            sch.wait("gpsimd", "cVE", ve_stats)
            sch.op("gpsimd", lambda e: e.memset(sqsc[0][:, 0:128], 0.0), inc="cGP")
            sch.wait("vector", "cGP")
            sch.op("vector", lambda e: e.tensor_copy(
                bass.AP(sqsc[0], 0, [[1024, 128], [1, 16]]), g2_a[:]), inc="cVE")
            ve_g2c = sch.snap("cVE")
            sch.wait("tensor", "cVE", ve_g2c)
            sch.op("tensor", lambda e: e.transpose(
                pTf[:, 0:128], bass.AP(sqsc[0], 0, [[1024, 128], [1, 128]]),
                idn_f32[:]), inc="cPE")
            pe_g2t = sch.snap("cPE")
            sch.wait("vector", "cPE", pe_g2t)
            sch.op("vector", lambda e: e.tensor_copy(
                bass.AP(sqsc[1], 0, [[1024, 16], [1, 128]]),
                bass.AP(pTf, 0, [[128, 16], [1, 128]])), inc="cVE")
            ve_g2e = sch.snap("cVE")
            sch.wait("scalar", "cVE", ve_g2e)
            sch.op("scalar", lambda e: e.dma_start(
                out=bass.AP(g2dram, 0, [[128, 16], [1, 128]]),
                in_=bass.AP(sqsc[1], 0, [[1024, 16], [1, 128]])), inc="dH", k=16)
            b1_sc_all = sch.snap("cSC")
            run_block(nc, "b1", sch)
        if upto < 2:
            return nc

        with ExitStack() as ps:
            pbig = ps.enter_context(nc.psum_tensor("pbig2", [128, 2048], F32))
            p4 = ps.enter_context(nc.psum_tensor("p4_2", [128, 512], F32))
            p5 = ps.enter_context(nc.psum_tensor("p5_2", [128, 512], F32))
            p6 = ps.enter_context(nc.psum_tensor("p6_2", [128, 512], F32))
            p7 = ps.enter_context(nc.psum_tensor("p7_2", [128, 512], F32))

            sch = Sched(cts)
            sch.wait("vector", "dC")
            sch.wait("gpsimd", "dC")
            wE = [sb(ps, f"wE{i}", NCH * 128, BF16) for i in range(2)]
            wO = [sb(ps, f"wO{i}", NCH * 128, BF16) for i in range(2)]
            wstg = [sb(ps, f"wstg{i}", 1024, F32) for i in range(2)]
            hs = ExitStack()
            g2_rep = hs.enter_context(nc.sbuf_tensor("g2_rep", [128, S], BF16))
            h_scr = [hs.enter_context(nc.sbuf_tensor(f"h_scr{i}", [128, S], BF16)) for i in range(2)]

            sch.wait("gpsimd", "dH")
            sch.op("gpsimd", lambda e: e.dma_start(
                out=g2_rep[:], in_=bass.AP(g2dram, 0, [[0, 128], [1, S]])), inc="dGW", k=16)
            dh_rep = sch.snap("dGW")
            sch.wait("gpsimd", "dGW", dh_rep)
            gp_hm = {}
            ve_hts = {}
            ds_hsp = {}

            def emit_h_mul(fc):
                if fc < 2:
                    sch.wait("gpsimd", "cSC", b1_sc_all)
                if fc >= 2:
                    sch.wait("gpsimd", "cVE", ve_hts[fc - 2])
                    sch.wait("gpsimd", f"dS{fc % 2}", ds_hsp[fc - 2])
                sch.op("gpsimd", lambda e, fc=fc: e.tensor_mul(
                    h_scr[fc % 2][:], xn_fm[:, S * fc:S * (fc + 1)], g2_rep[:]), inc="cGP")
                gp_hm[fc] = sch.snap("cGP")

            def emit_h_ts(fc):
                sch.wait("vector", "cGP", gp_hm[fc])
                if fc >= 2:
                    sch.wait("vector", f"dS{fc % 2}", ds_hsp[fc - 2])
                sch.op("vector", lambda e, fc=fc: e.tensor_scalar(
                    h_scr[fc % 2][:], h_scr[fc % 2][:], gam_cm[:, fc:fc + 1],
                    bet_cm[:, fc:fc + 1], ALU.mult, ALU.add), inc="cVE")
                ve_hts[fc] = sch.snap("cVE")
                sch.wait("scalar", "cVE", ve_hts[fc])
                sch.op("scalar", lambda e, fc=fc: e.dma_start(
                    out=bass.AP(hdram, S * fc, [[NCH * S, 128], [1, S]]),
                    in_=h_scr[fc % 2][:]), inc=f"dS{fc % 2}", k=16)
                ds_hsp[fc] = sch.snap(f"dS{fc % 2}")

            emit_h_mul(0)
            emit_h_mul(1)
            ve_vev = {}
            sch.wait("tensor", "dG0", dg_wv)
            for tt in range(NT):
                za = p4[:] if tt % 2 == 0 else p6[:]
                zb = p5[:] if tt % 2 == 0 else p7[:]
                sch.wait("tensor", "cSC", sc_evac2[tt])
                if tt >= 2:
                    sch.wait("tensor", "cVE", ve_vev[tt - 2])
                for fc in range(NCH):
                    sch.op("tensor", lambda e, fc=fc, tt=tt, za=za: e.matmul(
                        za, xn_fm[:, S * fc + 128 * tt:S * fc + 128 * (tt + 1)],
                        wv_bf[:, 1024 * fc:1024 * fc + 512],
                        start=(fc == 0), stop=(fc == NCH - 1)), inc="cPE")
                    sch.op("tensor", lambda e, fc=fc, tt=tt, zb=zb: e.matmul(
                        zb, xn_fm[:, S * fc + 128 * tt:S * fc + 128 * (tt + 1)],
                        wv_bf[:, 1024 * fc + 512:1024 * (fc + 1)],
                        start=(fc == 0), stop=(fc == NCH - 1)), inc="cPE")
                pe_v = sch.snap("cPE")
                sch.wait("vector", "cPE", pe_v)
                sch.op("vector", lambda e, tt=tt, za=za: e.tensor_copy(
                    v_tm[:, 1024 * tt:1024 * tt + 512], za), inc="cVE")
                sch.op("vector", lambda e, tt=tt, zb=zb: e.tensor_copy(
                    v_tm[:, 1024 * tt + 512:1024 * (tt + 1)], zb), inc="cVE")
                ve_vev[tt] = sch.snap("cVE")
                if tt < NCH:
                    emit_h_ts(tt)
                    if tt + 2 < NCH:
                        emit_h_mul(tt + 2)
            ve_v_all = sch.snap("cVE")
            hs.close()
            qe_s = sb(ps, "qe_s", S, BF16)
            qo_s = sb(ps, "qo_s", S, BF16)
            qeo_bf = [sb(ps, f"qeo{i}", 1024, BF16) for i in range(2)]
            rp_t = [sb(ps, f"rp_t{i}", 1024, BF16) for i in range(2)]
            nsq_sm = [sb(ps, f"nsq_sm{i}", 1024, BF16) for i in range(2)]
            sch.wait("vector", "cGP", gp_hm[NCH - 1])
            sch.wait("vector", "dS0", ds_hsp[NCH - 2])
            sch.wait("scalar", "dS1", ds_hsp[NCH - 1])

            ve_rope = {}
            sc_evac = {}
            gp_rope = {}
            dr_rep = {}
            ve_slab = {}
            pe_qk = {}
            dw_slab = {}

            def emit_slab_dmas(it):
                ti2, cp2 = it // 4, it % 4
                w_src = wq_d if ti2 == 0 else wk_d
                for pi, coff in enumerate((128 * cp2, 512 + 128 * cp2)):
                    if (it - 1, pi) in ve_slab:
                        sch.wait("sync", "cVE", ve_slab[(it - 1, pi)])
                    for fc in range(NCH):
                        sch.op("sync", lambda e, fc=fc, coff=coff, w_src=w_src, pi=pi: e.dma_start(
                            out=wstg[pi][:, 128 * fc:128 * (fc + 1)],
                            in_=bass.AP(w_src, 128 * fc * D + coff,
                                        [[D, 128], [1, 128]])), inc=f"dW{pi}", k=16)
                    dw_slab[(it, pi)] = sch.snap(f"dW{pi}")

            def emit_slab_casts(it):
                wp = it % 2
                for pi in range(2):
                    sch.wait("vector", f"dW{pi}", dw_slab[(it, pi)])
                    if it >= 2:
                        sch.wait("vector", "cPE", pe_qk[it - 2])
                    sch.op("vector", lambda e, pi=pi, wp=wp,
                           wdst=(wE if pi == 0 else wO): e.tensor_copy(
                        wdst[wp][:], wstg[pi][:]), inc="cVE")
                    ve_slab[(it, pi)] = sch.snap("cVE")

            first_nmax = {0: True, 1: True}
            gp_sq = {}
            ve_nr = {}
            pe_nr = {}

            def emit_sel(it2, tg2):
                ti2 = it2 // 4
                sch.wait("tensor", "cGP", gp_sq[(it2, tg2)])
                if ve_nr:
                    sch.wait("tensor", "cVE", ve_nr[max(ve_nr)])
                else:
                    sch.wait("tensor", "cVE", ve_v_all)
                sch.op("tensor", lambda e, tg2=tg2: e.matmul(
                    p7[0:4, :], selq[:, 0:4], nsq_sm[tg2 % 2][:, 0:512],
                    start=True, stop=False), inc="cPE")
                sch.op("tensor", lambda e, tg2=tg2: e.matmul(
                    p7[0:4, :], selq[:, 0:4], nsq_sm[tg2 % 2][:, 512:1024],
                    start=False, stop=True), inc="cPE")
                pe_nr[(it2, tg2)] = sch.snap("cPE")
                sch.wait("vector", "cPE", pe_nr[(it2, tg2)])
                sch.op("vector", lambda e: e.reduce_max(
                    stat[0:4, 12:13], p7[0:4, :], axis=AX.X), inc="cVE")
                sch.wait("vector", "cVE")
                if first_nmax[ti2]:
                    sch.op("vector", lambda e, ti2=ti2: e.tensor_copy(
                        stat[0:4, 10 + ti2:11 + ti2], stat[0:4, 12:13]), inc="cVE")
                    first_nmax[ti2] = False
                else:
                    sch.op("vector", lambda e, ti2=ti2: e.tensor_max(
                        stat[0:4, 10 + ti2:11 + ti2],
                        stat[0:4, 10 + ti2:11 + ti2], stat[0:4, 12:13]), inc="cVE")
                ve_nr[(it2, tg2)] = sch.snap("cVE")
            emit_slab_dmas(0)
            emit_slab_casts(0)
            for ti, (w_d_, att) in enumerate(((wq_d, q_att), (wk_d, k_att))):
                for cp in range(4):
                    it = ti * 4 + cp
                    wp = it % 2
                    if it + 1 < 8:
                        emit_slab_dmas(it + 1)
                    sch.wait("tensor", "cVE", ve_slab[(it, 1)])
                    if it == 0:
                        sch.wait("tensor", "cVE", ve_v_all)
                    for tgq in range(4):
                        sl = tgq % 2
                        pe_ = pbig[:, 1024 * sl:1024 * sl + 512]
                        po_ = pbig[:, 1024 * sl + 512:1024 * sl + 1024]
                        key = (it, tgq - 2) if tgq >= 2 else (it - 1, tgq + 2)
                        if key in sc_evac:
                            sch.wait("tensor", "cSC", sc_evac[key])
                        for fc in range(NCH):
                            sch.op("tensor", lambda e, fc=fc, tgq=tgq, pe_=pe_, wp=wp: e.matmul(
                                pe_, wE[wp][:, 128 * fc:128 * (fc + 1)],
                                xn_fm[:, S * fc + 512 * tgq:S * fc + 512 * (tgq + 1)],
                                start=(fc == 0), stop=(fc == NCH - 1)), inc="cPE")
                        for fc in range(NCH):
                            sch.op("tensor", lambda e, fc=fc, tgq=tgq, po_=po_, wp=wp: e.matmul(
                                po_, wO[wp][:, 128 * fc:128 * (fc + 1)],
                                xn_fm[:, S * fc + 512 * tgq:S * fc + 512 * (tgq + 1)],
                                start=(fc == 0), stop=(fc == NCH - 1)), inc="cPE")
                        pe_now = sch.snap("cPE")
                        rp = rp_t[tgq % 2]
                        qeo = qeo_bf[tgq % 2]
                        sch.wait("scalar", "cPE", pe_now)
                        key2 = (it, tgq - 2) if tgq >= 2 else (it - 1, tgq + 2)
                        if key2 in ve_rope:
                            sch.wait("scalar", "cVE", ve_rope[key2])
                        if key2 in gp_sq:
                            sch.wait("scalar", "cGP", gp_sq[key2])
                        sch.op("scalar", lambda e, pe_=pe_, qeo=qeo: e.activation(
                            qeo[:, 0:512], pe_, AF.Copy), inc="cSC")
                        sch.op("scalar", lambda e, po_=po_, qeo=qeo: e.activation(
                            qeo[:, 512:1024], po_, AF.Copy), inc="cSC")
                        sc_evac[(it, tgq)] = sch.snap("cSC")
                        if (it - 1, tgq) in dr_rep:
                            sch.wait("vector", f"dR{(it - 1) % 2}", dr_rep[(it - 1, tgq)])
                        sch.wait("vector", "cSC", sc_evac[(it, tgq)])
                        sch.op("vector", lambda e, qeo=qeo, tgq=tgq, rp=rp: e.tensor_mul(
                            rp[:, 0:512], qeo[:, 0:512], cos_t[:, 512 * tgq:512 * (tgq + 1)]), inc="cVE")
                        sch.op("vector", lambda e, qeo=qeo, tgq=tgq, rp=rp: e.tensor_mul(
                            rp[:, 512:1024], qeo[:, 512:1024], sin_t[:, 512 * tgq:512 * (tgq + 1)]), inc="cVE")
                        sch.wait("vector", "cVE")
                        sch.op("vector", lambda e, tgq=tgq, rp=rp: e.tensor_sub(
                            qe_s[:, 512 * tgq:512 * (tgq + 1)],
                            rp[:, 0:512], rp[:, 512:1024]), inc="cVE")
                        sch.wait("vector", "cVE")
                        sch.op("vector", lambda e, qeo=qeo, tgq=tgq, rp=rp: e.tensor_mul(
                            rp[:, 0:512], qeo[:, 512:1024], cos_t[:, 512 * tgq:512 * (tgq + 1)]), inc="cVE")
                        sch.op("vector", lambda e, qeo=qeo, tgq=tgq, rp=rp: e.tensor_mul(
                            rp[:, 512:1024], qeo[:, 0:512], sin_t[:, 512 * tgq:512 * (tgq + 1)]), inc="cVE")
                        ve_rope[(it, tgq)] = sch.snap("cVE")
                        sch.wait("vector", "cVE")
                        sch.op("vector", lambda e, tgq=tgq, rp=rp: e.tensor_add(
                            qo_s[:, 512 * tgq:512 * (tgq + 1)],
                            rp[:, 0:512], rp[:, 512:1024]), inc="cVE")
                        gp_rope[(it, tgq)] = sch.snap("cVE")
                        sch.wait("gpsimd", "cVE", gp_rope[(it, tgq)])
                        gkey = (it, tgq - 2) if tgq >= 2 else (it - 1, tgq + 2)
                        if gkey in pe_nr:
                            sch.wait("gpsimd", "cPE", pe_nr[gkey])
                        sch.op("gpsimd", lambda e, qeo=qeo, tgq=tgq: e.tensor_mul(
                            nsq_sm[tgq % 2][:], qeo[:], qeo[:]), inc="cGP")
                        gp_sq[(it, tgq)] = sch.snap("cGP")
                        if tgq >= 1:
                            emit_sel(it, tgq - 1)
                        elif it >= 1:
                            emit_sel(it - 1, 3)
                    if it + 1 < 8:
                        emit_slab_casts(it + 1)
                    ve_rope_all = sch.snap("cVE")
                    sch.wait("scalar", "cVE", ve_rope_all)
                    for j in range(4):
                        h = 4 * cp + j
                        m, r = h // 2, 64 * (h % 2)
                        sch.op("scalar", lambda e, j=j, m=m, r=r, att=att: e.dma_start(
                            out=att[r:r + 32, S * m:S * (m + 1)],
                            in_=qe_s[32 * j:32 * (j + 1), :]), inc=f"dR{it % 2}", k=16)
                        sch.op("scalar", lambda e, j=j, m=m, r=r, att=att: e.dma_start(
                            out=att[r + 32:r + 64, S * m:S * (m + 1)],
                            in_=qo_s[32 * j:32 * (j + 1), :]), inc=f"dR{it % 2}", k=16)
                    for tgq in range(4):
                        dr_rep[(it, tgq)] = sch.snap(f"dR{it % 2}")
                    pe_qk[it] = sch.snap("cPE")

            emit_sel(7, 3)
            sch.wait("vector", "cVE", ve_nr[max(ve_nr)])
            for ti2 in range(2):
                sch.op("vector", lambda e, ti2=ti2: e.tensor_copy(
                    statb[0:4, ti2:ti2 + 1], stat[0:4, 10 + ti2:11 + ti2]), inc="cVE")
            ve_cmb = sch.snap("cVE")
            sch.wait("tensor", "cVE", ve_cmb)
            for ti2 in range(2):
                sch.op("tensor", lambda e, ti2=ti2: e.matmul(
                    p6[0:1, ti2:ti2 + 1], ones_row[0:4, 0:1], statb[0:4, ti2:ti2 + 1],
                    start=True, stop=True), inc="cPE")
            pe_cmb = sch.snap("cPE")
            sch.wait("vector", "cPE", pe_cmb)
            sch.op("vector", lambda e: e.tensor_copy(
                stat[:1, 10:12], p6[0:1, 0:2]), inc="cVE")
            sch.wait("scalar", "cVE")
            sch.op("scalar", lambda e: e.activation(
                stat[:1, 13:14], stat[:1, 10:11], AF.Ln), inc="cSC")
            sch.wait("scalar", "cSC")
            sch.op("scalar", lambda e: e.activation(
                stat[:1, 14:15], stat[:1, 11:12], AF.Ln), inc="cSC")
            sch.wait("vector", "cSC")
            sch.op("vector", lambda e: e.tensor_add(
                stat[:1, 15:16], stat[:1, 13:14], stat[:1, 14:15]), inc="cVE")
            sch.wait("scalar", "cVE")
            sch.op("scalar", lambda e: e.activation(
                stat[:1, 12:13], stat[:1, 15:16], AF.Exp, scale=0.5,
                bias=biasc[0:1, 1:2]), inc="cSC")
            sch.wait("vector", "cSC")
            sch.op("vector", lambda e: e.tensor_scalar(
                stat[:1, 12:13], stat[:1, 12:13], 1.0, -1.0, ALU.add, ALU.mult), inc="cVE")
            sch.wait("vector", "cVE")
            sch.op("vector", lambda e: e.tensor_copy(statb[:1, 0:1], stat[:1, 12:13]), inc="cVE")
            ve_ub = sch.snap("cVE")
            sch.wait("tensor", "cVE", ve_ub)
            sch.op("tensor", lambda e: e.matmul(
                p6[:, 0:1], ones_row[0:1, :], statb[0:1, 0:1],
                start=True, stop=True), inc="cPE")
            sch.wait("vector", "cPE")
            sch.op("vector", lambda e: e.tensor_copy(negub[:], p6[:, 0:1]), inc="cVE")
            run_block(nc, "b2", sch)

        with ExitStack() as ps:
            pbig = ps.enter_context(nc.psum_tensor("pbig3", [128, 2048], F32))
            p4 = ps.enter_context(nc.psum_tensor("p4_3", [128, 512], F32))
            p5 = ps.enter_context(nc.psum_tensor("p5_3", [128, 512], F32))
            p6 = ps.enter_context(nc.psum_tensor("p6_3", [128, 512], F32))
            p7 = ps.enter_context(nc.psum_tensor("p7_3", [128, 512], F32))
            wo_bf = sb(ps, "wo_bf", NCH * D, BF16)
            pt_b = [sb(ps, f"ptb{i}", 2048, BF16) for i in range(3)]
            ax = ExitStack()
            lrec = sb(ax, "lrec", 512, F32)
            lrecb = sb(ax, "lrecb", 512, BF16)
            rep_sb = sb(ax, "rep_sb", 512, F32)

            sch = Sched(cts)
            sch.wait("tensor", "dR0")
            sch.wait("tensor", "dR1")
            sch.wait("sync", "dR0")
            sch.wait("sync", "dR1")
            sch.wait("gpsimd", "dR0")
            sch.wait("gpsimd", "dR1")
            for fc in range(NCH):
                sch.op("gpsimd", lambda e, fc=fc: e.dma_start(
                    out=wo_bf[:, 1024 * fc:1024 * (fc + 1)],
                    in_=bass.AP(wo_d, 128 * fc * D, [[D, 128], [1, D]])), inc="dGW", k=16)
            dg_wo = sch.snap("dGW")

            with ExitStack() as ds:
                dq_i = [dq_i0] * 2
                dq_s = [dq_s0] * 2
                dq_o = [dq_o0] * 2

                dq_jobs = []
                for c in range(NCH):
                    for hlf in range(2):
                        dq_jobs.append(("w1", c, hlf))
                for c in range(NFC):
                    dq_jobs.append(("w2", c, 0))
                ndq = len(dq_jobs)
                dq_ve = {}
                dq_dg = {}

                def emit_dq_job(ji):
                    kind, c, hlf = dq_jobs[ji]
                    if kind == "w1":
                        cols = 2048
                        co = 0
                        qsrc = bass.AP(q1_d, 128 * c * F + 2048 * hlf, [[F, 128], [1, 2048]])
                        ssrc = bass.AP(s1_d, c * F + 2048 * hlf, [[0, 128], [1, 2048]])
                        dsth = bass.AP(w1dram, 16384 * hlf + 128 * c,
                                       [[NCH * F, 128], [1024, 16], [1, 128]])
                    else:
                        cols = 1024
                        co = 1024 * (ji % 2)
                        qsrc = bass.AP(q2_d, 128 * c * D, [[D, 128], [1, 1024]])
                        ssrc = bass.AP(s2_d, c * D, [[0, 128], [1, 1024]])
                        dsth = bass.AP(w2dram, 128 * c * D, [[D, 128], [1, 1024]])
                    bi = ji % 2
                    deep = kind == "w2" and ji >= 18
                    if deep:
                        sch.wait("sync", "cVE", dq_ve[ji - 2])
                    elif ji >= 1:
                        sch.wait("sync", "cVE", dq_ve[ji - 1])
                    sch.op("sync", lambda e, qsrc=qsrc, cols=cols, co=co: e.dma_start(
                        out=dq_i[0][:, co:co + cols], in_=qsrc), inc=f"dW{bi}", k=16)
                    sch.op("sync", lambda e, ssrc=ssrc, cols=cols, co=co: e.dma_start(
                        out=dq_s[0][:, co:co + cols], in_=ssrc), inc=f"dW{bi}", k=16)
                    v = sch.snap(f"dW{bi}")
                    if deep and ji >= 2:
                        sch.wait("vector", f"dR{ji % 2}", dq_dg[ji - 2])
                    elif ji >= 1:
                        sch.wait("vector", f"dR{(ji - 1) % 2}", dq_dg[ji - 1])
                    sch.wait("vector", f"dW{bi}", v)
                    sch.op("vector", lambda e, cols=cols, co=co: e.scalar_tensor_tensor(
                        dq_o[0][:, co:co + cols], dq_i[0][:, co:co + cols], -128.0,
                        dq_s[0][:, co:co + cols], ALU.add, ALU.mult), inc="cVE")
                    dq_ve[ji] = sch.snap("cVE")
                    sch.wait("sync", "cVE", dq_ve[ji])
                    sch.op("sync", lambda e, dsth=dsth, cols=cols, co=co: e.dma_start(
                        out=dsth, in_=dq_o[0][:, co:co + cols]), inc=f"dR{ji % 2}", k=16)
                    dq_dg[ji] = sch.snap(f"dR{ji % 2}")

                dq_next = 0
                dq_quota = [0]
                sc_exp = {}
                pe_avl = {}
                gp_msk = {}
                ve_omul_par = [0, 0]
                ve_repevac_par = [0, 0]
                obank = [p4, p5]
                lbank = [p6, p7]
                pending_ep = [None]
                ve_rc_box = [0]
                gidx_c = [0]
                for qg in range(NQG):
                    for m in range(8):
                        K = 4 * qg + 4
                        gp = gidx_c[0] % 2

                        def emit_scores_exp(kt, qg=qg, m=m, K=K):
                            half = kt % 2
                            pt = pt_b[kt % 3]
                            sc0 = pbig[:, 1024 * half:1024 * half + 512]
                            sc1 = pbig[:, 1024 * half + 512:1024 * half + 1024]
                            prevexp = None
                            if kt >= 2:
                                prevexp = sc_exp[(qg, m, kt - 2)]
                            else:
                                pg = (qg, m - 1) if m > 0 else ((qg - 1, 7) if qg > 0 else None)
                                if pg is not None:
                                    pk = 4 * pg[0] + 4
                                    kk = pk - 2 + kt
                                    if (pg[0], pg[1], kk) in sc_exp:
                                        prevexp = sc_exp[(pg[0], pg[1], kk)]
                            if prevexp is not None:
                                sch.wait("tensor", "cSC", prevexp)
                            for jj in range(2):
                                r = 64 * jj
                                sch.op("tensor", lambda e, jj=jj, m=m, r=r, kt=kt, qg=qg,
                                       dst=(sc0 if jj == 0 else sc1): e.matmul(
                                    dst,
                                    k_att[r:r + 64, S * m + 128 * kt:S * m + 128 * (kt + 1)],
                                    q_att[r:r + 64, S * m + 512 * qg:S * m + 512 * (qg + 1)],
                                    start=True, stop=True, tile_position=(r, 0)), inc="cPE")
                            pe_sc = sch.snap("cPE")
                            sch.wait("scalar", "cPE", pe_sc)
                            prevuse = None
                            if (qg, m, kt - 3) in pe_avl:
                                prevuse = pe_avl[(qg, m, kt - 3)]
                            else:
                                pg = (qg, m - 1) if m > 0 else ((qg - 1, 7) if qg > 0 else None)
                                if pg is not None:
                                    pk = 4 * pg[0] + 4
                                    kk = pk - 3 + kt
                                    if (pg[0], pg[1], kk) in pe_avl:
                                        prevuse = pe_avl[(pg[0], pg[1], kk)]
                            if prevuse is not None:
                                sch.wait("scalar", "cPE", prevuse)
                            sch.op("scalar", lambda e, pt=pt, half=half: e.activation(
                                pt[:, 0:1024], pbig[:, 1024 * half:1024 * half + 1024],
                                AF.Exp, bias=negub[:, 0:1], scale=0.125), inc="cSC")
                            sc_exp[(qg, m, kt)] = sch.snap("cSC")
                            if kt >= 4 * qg:
                                off = (kt - 4 * qg) * 128
                                sch.wait("gpsimd", "cSC", sc_exp[(qg, m, kt)])
                                for jj in range(2):
                                    base = 512 * jj
                                    if off > 0:
                                        sch.op("gpsimd", lambda e, pt=pt, base=base, off=off: e.memset(
                                            pt[:, base:base + off], 0.0), inc="cGP")
                                    sch.op("gpsimd", lambda e, pt=pt, base=base, off=off: e.tensor_mul(
                                        pt[:, base + off:base + off + 128],
                                        pt[:, base + off:base + off + 128], msk_bf[:]), inc="cGP")
                                gp_msk[(qg, m, kt)] = sch.snap("cGP")

                        def emit_av(kt, qg=qg, m=m, K=K, gp=gp):
                            pt = pt_b[kt % 3]
                            if (qg, m, kt) in gp_msk:
                                sch.wait("tensor", "cGP", gp_msk[(qg, m, kt)])
                            else:
                                sch.wait("tensor", "cSC", sc_exp[(qg, m, kt)])
                            if kt == 0:
                                sch.wait("tensor", "cVE", ve_omul_par[gp])
                                sch.wait("tensor", "cVE", ve_repevac_par[gp])
                            for jj in range(2):
                                h = 2 * m + jj
                                cpos = 64 * jj
                                sch.op("tensor", lambda e, pt=pt, h=h, kt=kt, cpos=cpos,
                                       K=K, gp=gp: e.matmul(
                                    obank[gp][cpos:cpos + 64, :],
                                    v_tm[:, 1024 * kt + 64 * h:1024 * kt + 64 * (h + 1)],
                                    pt[:, 512 * (cpos // 64):512 * (cpos // 64 + 1)],
                                    start=(kt == 0), stop=(kt == K - 1),
                                    tile_position=(0, cpos),
                                    skip_group_check=True), inc="cPE")
                            for jj in range(2):
                                sch.op("tensor", lambda e, pt=pt, jj=jj, kt=kt, K=K, gp=gp: e.matmul(
                                    bass.AP(lbank[gp], 32 * jj * 512, [[512, 1], [1, 512]]),
                                    ones_bf[:], pt[:, 512 * jj:512 * (jj + 1)],
                                    start=(kt == 0), stop=(kt == K - 1),
                                    tile_position=(0, 32 * jj),
                                    skip_group_check=True), inc="cPE")
                            pe_avl[(qg, m, kt)] = sch.snap("cPE")

                        def mk_epilogue(qg=qg, m=m, K=K, gp=gp):
                            def ep_a():
                                pe_last = pe_avl[(qg, m, K - 1)]
                                sch.wait("vector", "cPE", pe_last)
                                for jj in range(2):
                                    sch.op("vector", lambda e, jj=jj, gp=gp: e.reciprocal(
                                        lrec[32 * jj:32 * jj + 1, :],
                                        bass.AP(lbank[gp], 32 * jj * 512, [[512, 1], [1, 512]])), inc="cVE")
                                sch.wait("vector", "cVE")
                                for jj in range(2):
                                    sch.op("vector", lambda e, jj=jj: e.tensor_copy(
                                        lrecb[32 * jj:32 * jj + 1, :],
                                        lrec[32 * jj:32 * jj + 1, :]), inc="cVE")
                                ve_rc_box[0] = sch.snap("cVE")
                            def ep_b():
                                sch.wait("tensor", "cVE", ve_rc_box[0])
                                for jj in range(2):
                                    sch.op("tensor", lambda e, jj=jj, gp=gp: e.matmul(
                                        bass.AP(lbank[gp], 64 * jj * 512, [[512, 64], [1, 512]]),
                                        ones_row[32 * jj:32 * jj + 1, 0:64],
                                        lrecb[32 * jj:32 * jj + 1, :],
                                        start=True, stop=True,
                                        tile_position=(32 * jj, 64 * jj)), inc="cPE")
                                pe_rep = sch.snap("cPE")
                                sch.wait("vector", "cPE", pe_rep)
                                sch.op("vector", lambda e, gp=gp: e.tensor_copy(
                                    rep_sb[:, 0:512], lbank[gp][:]), inc="cVE")
                                ve_repevac_par[gp] = sch.snap("cVE")
                                sch.wait("vector", "cVE")
                                sch.op("vector", lambda e, m=m, qg=qg, gp=gp: e.tensor_mul(
                                    o_sb[:, S * m + 512 * qg:S * m + 512 * (qg + 1)],
                                    obank[gp][:], rep_sb[:, 0:512]), inc="cVE")
                                ve_omul_par[gp] = sch.snap("cVE")
                            return ep_a, ep_b

                        for kt in range(K):
                            emit_scores_exp(kt)
                            if kt == 1 and pending_ep[0] is not None:
                                pending_ep[0][0]()
                            if kt == 1:
                                while dq_next < dq_quota[0]:
                                    emit_dq_job(dq_next)
                                    dq_next += 1
                            if kt >= 2:
                                emit_av(kt - 2)
                        emit_av(K - 2)
                        emit_av(K - 1)
                        if pending_ep[0] is not None:
                            pending_ep[0][1]()
                            pending_ep[0] = None
                        pending_ep[0] = mk_epilogue()
                        gidx_c[0] += 1
                        gidx = gidx_c[0]
                        if gidx <= 12:
                            dq_quota[0] = min(ndq, gidx // 3)
                        else:
                            dq_quota[0] = min(ndq, 4 + (gidx - 12) * (ndq - 4) // 18)
                pending_ep[0][0]()
                pending_ep[0][1]()
                pending_ep[0] = None
                ve_oevac_prev = max(ve_omul_par)
                while dq_next < ndq:
                    emit_dq_job(dq_next)
                    dq_next += 1
                ve_dq_all = sch.snap("cVE")
                for eng in ("sync", "vector", "scalar", "gpsimd"):
                    sch.wait(eng, "cVE", ve_dq_all)
                    sch.wait(eng, "dG0")
                    sch.wait(eng, "dG1")
                    sch.wait(eng, "dR0")
                    sch.wait(eng, "dR1")
            ax.close()
            with ExitStack() as ws:
                w1t3 = [sb(ws, f"w1t3_{i}", NCH * 128, BF16) for i in range(3)]
                pe_att_end = sch.snap("cPE")
                sch.wait("sync", "cPE", pe_att_end)
                sch.wait("gpsimd", "cPE", pe_att_end)
                for ci in range(2):
                    sch.op("sync", lambda e, ci=ci: e.dma_start(
                        out=h_fm[:, 4096 * ci:4096 * (ci + 1)],
                        in_=bass.AP(hdram, 4096 * ci, [[NCH * S, 128], [1, 4096]])), inc="dH", k=16)
                for ci in range(2, 4):
                    sch.op("gpsimd", lambda e, ci=ci: e.dma_start(
                        out=h_fm[:, 4096 * ci:4096 * (ci + 1)],
                        in_=bass.AP(hdram, 4096 * ci, [[NCH * S, 128], [1, 4096]])), inc="dG0", k=16)
                dh_hw = sch.snap("dH")
                dh_sw = sch.snap("dG0")
                for fc in range(16):
                    sch.op("gpsimd", lambda e, fc=fc: e.dma_start(
                        out=w2a[:, 1024 * fc:1024 * (fc + 1)],
                        in_=bass.AP(w2dram, 128 * fc * D, [[D, 128], [1, D]])), inc="dG1", k=16)
                dg_w2a = sch.snap("dG1")

                sch.wait("tensor", "dGW", dg_wo)
                sch.wait("tensor", "cVE", ve_oevac_prev)
                ve_sev = {}
                ds_sp = {}
                pe_y = {}
                sc_ye3 = {}
                safe_ocs = list(range(0, 10)) + list(range(14, NFC))
                fc1_visits = [(ph, oc) for ph in range(2) for oc in safe_ocs] \
                    + [(ph, oc) for ph in range(2) for oc in (10, 11, 12, 13)]
                vic = [0]
                spill_fin = [None]

                def emit_fc1_visit(vi):
                    ph, oc = fc1_visits[vi]
                    wb = w1t3[vi % 3]
                    if vi >= 3:
                        sch.wait("sync", "cPE", pe_y[vi - 3])
                    else:
                        sch.wait("sync", "cVE", ve_oevac_prev)
                    sch.op("sync", lambda e, oc=oc, wb=wb: e.dma_start(
                        out=wb[:],
                        in_=bass.AP(w1dram, 1024 * oc,
                                    [[NCH * F, 128], [128, 8], [1, 128]])), inc=f"dW{vi % 3}", k=16)
                    dwv = sch.snap(f"dW{vi % 3}")
                    if vi == 0:
                        sch.wait("tensor", "dH", dh_hw)
                        sch.wait("tensor", "dG0", dh_sw)
                    sch.wait("tensor", f"dW{vi % 3}", dwv)
                    yp = p6 if vi % 2 == 0 else p7
                    if vi >= 2:
                        sch.wait("tensor", "cSC", sc_ye3[vi - 2])
                    for fc in range(NCH):
                        sch.op("tensor", lambda e, fc=fc, ph=ph, wb=wb, yp=yp: e.matmul(
                            yp[:], wb[:, 128 * fc:128 * (fc + 1)],
                            h_fm[:, S * fc + 512 * ph:S * fc + 512 * (ph + 1)],
                            start=(fc == 0), stop=(fc == NCH - 1)), inc="cPE")
                    pe_y[vi] = sch.snap("cPE")
                    sch.wait("scalar", "cPE", pe_y[vi])
                    if oc in (10, 11, 12, 13):
                        sch.wait("scalar", "dS0", spill_fin[0][0])
                        sch.wait("scalar", "dS1", spill_fin[0][1])
                    sch.op("scalar", lambda e, oc=oc, ph=ph, yp=yp: e.activation(
                        y_sl[:, 1024 * oc + 512 * ph:1024 * oc + 512 * (ph + 1)], yp[:],
                        AF.Silu, bias=b1_cm[:, oc:oc + 1]), inc="cSC")
                    sc_ye3[vi] = sch.snap("cSC")

                for tt in range(NT):
                    za = p4[:] if tt % 2 == 0 else pbig[:, 0:512]
                    zb = p5[:] if tt % 2 == 0 else pbig[:, 512:1024]
                    if tt >= 2:
                        sch.wait("tensor", "cVE", ve_sev[tt - 2])
                    for oc in range(NCH):
                        sch.op("tensor", lambda e, oc=oc, tt=tt, za=za: e.matmul(
                            za, o_sb[:, S * oc + 128 * tt:S * oc + 128 * (tt + 1)],
                            wo_bf[:, 1024 * oc:1024 * oc + 512],
                            start=(oc == 0), stop=(oc == NCH - 1)), inc="cPE")
                        sch.op("tensor", lambda e, oc=oc, tt=tt, zb=zb: e.matmul(
                            zb, o_sb[:, S * oc + 128 * tt:S * oc + 128 * (tt + 1)],
                            wo_bf[:, 1024 * oc + 512:1024 * (oc + 1)],
                            start=(oc == 0), stop=(oc == NCH - 1)), inc="cPE")
                    pe_s = sch.snap("cPE")
                    st = s_st[tt % 2]
                    sch.wait("vector", "cPE", pe_s)
                    if tt >= 2:
                        sch.wait("vector", f"dS{tt % 2}", ds_sp[tt - 2])
                    sch.op("vector", lambda e, st=st, za=za: e.tensor_copy(
                        st[:, 0:512], za), inc="cVE")
                    sch.op("vector", lambda e, st=st, zb=zb: e.tensor_copy(
                        st[:, 512:1024], zb), inc="cVE")
                    ve_sev[tt] = sch.snap("cVE")
                    sch.wait("scalar", "cVE", ve_sev[tt])
                    sch.op("scalar", lambda e, st=st, tt=tt: e.dma_start(
                        out=sdram[128 * tt:128 * (tt + 1), :], in_=st[:]), inc=f"dS{tt % 2}", k=16)
                    ds_sp[tt] = sch.snap(f"dS{tt % 2}")
                    if tt == 7:
                        pe_wo_mid = pe_s
                    if tt == 8:
                        sch.wait("sync", "cPE", pe_wo_mid)
                        for fc in range(16, NFC, 2):
                            sch.op("sync", lambda e, fc=fc: e.dma_start(
                                out=w2b[:, 1024 * (fc - 16):1024 * (fc - 15)],
                                in_=bass.AP(w2dram, 128 * fc * D, [[D, 128], [1, D]])), inc="dW", k=16)
                    if tt == NT - 1:
                        spill_fin[0] = (sch.snap("dS0"), sch.snap("dS1"))
                        pe_wo_end = pe_s
                    if tt >= 2:
                        for _ in range(4):
                            emit_fc1_visit(vic[0])
                            vic[0] += 1
                while vic[0] < len(fc1_visits):
                    emit_fc1_visit(vic[0])
                    vic[0] += 1
                sch.wait("sync", "cPE", pe_wo_end)
                for fc in range(17, NFC, 2):
                    sch.op("sync", lambda e, fc=fc: e.dma_start(
                        out=w2b[:, 1024 * (fc - 16):1024 * (fc - 15)],
                        in_=bass.AP(w2dram, 128 * fc * D, [[D, 128], [1, D]])), inc="dW", k=16)
                dw_w2 = sch.snap("dW")
                b3_fc1_sc = sch.snap("cSC")
                b3_sev_last = ve_sev[NT - 1]
                run_block(nc, "b3", sch)
        if upto < 4:
            return nc

        with ExitStack() as ps:
            pbig = ps.enter_context(nc.psum_tensor("pbig4", [128, 2048], F32))
            p4 = ps.enter_context(nc.psum_tensor("p4_4", [128, 512], F32))
            p5 = ps.enter_context(nc.psum_tensor("p5_4", [128, 512], F32))
            p6 = ps.enter_context(nc.psum_tensor("p6_4", [128, 512], F32))
            p7 = ps.enter_context(nc.psum_tensor("p7_4", [128, 512], F32))
            xb2 = [sb(ps, f"xb2_{i}", D, F32) for i in range(2)]
            sb2 = [sb(ps, f"sb2_{i}", D, F32) for i in range(2)]
            t_f = [sb(ps, f"t_f{i}", D, F32) for i in range(2)]
            f_sb = [sb(ps, f"f_sb{i}", D, F32) for i in range(2)]
            w1t = [sb(ps, f"w1t{i}", NCH * 128, BF16) for i in range(3)]

            sch = Sched(cts)
            sch.wait("sync", "dG0")
            sch.wait("sync", "dG1")
            sch.wait("sync", "dS0")
            sch.wait("sync", "dS1")
            dw_pre = {}
            for oc in range(3):
                sch.op("sync", lambda e, oc=oc: e.dma_start(
                    out=w1t[oc][:],
                    in_=bass.AP(w1dram, 1024 * oc,
                                [[NCH * F, 128], [128, 8], [1, 128]])), inc=f"dW{oc % 3}", k=16)
                dw_pre[oc] = sch.snap(f"dW{oc % 3}")

            pe_z = {}
            ve_ee = {}
            sc_g = {}
            o_done = {}
            ds_out = {}
            NP = 2
            for p in range(NP):
                if p == 1:
                    pe_v2 = {}
                    sc_y2 = {}
                    for oc in range(NFC):
                        wb = w1t[oc % 3]
                        if oc >= 3:
                            sch.wait("sync", "cPE", pe_v2[(oc - 3, 1)])
                            sch.op("sync", lambda e, oc=oc, wb=wb: e.dma_start(
                                out=wb[:],
                                in_=bass.AP(w1dram, 1024 * oc,
                                            [[NCH * F, 128], [128, 8], [1, 128]])), inc=f"dW{oc % 3}", k=16)
                            dw_now = sch.snap(f"dW{oc % 3}")
                        else:
                            dw_now = dw_pre[oc]
                        sch.wait("tensor", f"dW{oc % 3}", dw_now)
                        for ph in range(2):
                            yp = p6 if ph == 0 else p7
                            if oc >= 1:
                                sch.wait("tensor", "cSC", sc_y2[(oc - 1, ph)])
                            for fc in range(NCH):
                                sch.op("tensor", lambda e, fc=fc, ph=ph, wb=wb, yp=yp: e.matmul(
                                    yp[:], wb[:, 128 * fc:128 * (fc + 1)],
                                    h_fm[:, S * fc + 1024 + 512 * ph:S * fc + 1024 + 512 * (ph + 1)],
                                    start=(fc == 0), stop=(fc == NCH - 1)), inc="cPE")
                            pe_v2[(oc, ph)] = sch.snap("cPE")
                            sch.wait("scalar", "cPE", pe_v2[(oc, ph)])
                            if oc < 2:
                                sch.wait("scalar", "cPE", pe_z[(p - 1, 7)])
                            sch.op("scalar", lambda e, oc=oc, ph=ph, yp=yp: e.activation(
                                y_sl[:, 1024 * oc + 512 * ph:1024 * oc + 512 * (ph + 1)], yp[:],
                                AF.Silu, bias=b1_cm[:, oc:oc + 1]), inc="cSC")
                            sc_y2[(oc, ph)] = sch.snap("cSC")
                    sc_fc1_all = sch.snap("cSC")
                else:
                    sc_fc1_all = b3_fc1_sc
                for ttl in range(8):
                    tt = 8 * p + ttl
                    b = tt % 2
                    za = p4[:] if b == 0 else pbig[:, 0:512]
                    zb = p5[:] if b == 0 else pbig[:, 512:1024]
                    if ttl == 0:
                        sch.wait("tensor", "cSC", sc_fc1_all)
                        if p == 0:
                            sch.wait("tensor", "cVE", b3_sev_last)
                            sch.wait("tensor", "dG1", dg_w2a)
                    if tt >= 2:
                        sch.wait("tensor", "cVE", ve_ee[tt - 2])
                    for fc in range(NFC):
                        if p == 0 and ttl == 0 and fc == 16:
                            sch.wait("tensor", "dW", dw_w2)
                        sch.op("tensor", lambda e, fc=fc, ttl=ttl, za=za: e.matmul(
                            za, y_sl[:, 1024 * fc + 128 * ttl:1024 * fc + 128 * (ttl + 1)],
                            (w2a if fc < 16 else w2b)[:, 1024 * (fc % 16):1024 * (fc % 16) + 512],
                            start=(fc == 0), stop=(fc == NFC - 1)), inc="cPE")
                        sch.op("tensor", lambda e, fc=fc, ttl=ttl, zb=zb: e.matmul(
                            zb, y_sl[:, 1024 * fc + 128 * ttl:1024 * fc + 128 * (ttl + 1)],
                            (w2a if fc < 16 else w2b)[:, 1024 * (fc % 16) + 512:1024 * (fc % 16 + 1)],
                            start=(fc == 0), stop=(fc == NFC - 1)), inc="cPE")
                    pe_z[(p, ttl)] = sch.snap("cPE")
                    xb, sbt = xb2[b], sb2[b]
                    if tt >= 2:
                        sem, val = o_done[tt - 2]
                        sch.wait("sync", sem, val)
                    if tt >= 2:
                        sch.wait("sync", "cVE", ve_ee[tt - 2])
                    sch.op("sync", lambda e, xb=xb, tt=tt: e.dma_start(
                        out=xb[:], in_=x_d[128 * tt:128 * (tt + 1), :]), inc=f"dX{tt % 2}", k=16)
                    sch.op("sync", lambda e, sbt=sbt, tt=tt: e.dma_start(
                        out=sbt[:], in_=sdram[128 * tt:128 * (tt + 1), :]), inc=f"dX{tt % 2}", k=16)
                    dl_e = sch.snap(f"dX{tt % 2}")
                    fs = f_sb[b]
                    tf = t_f[b]
                    sch.wait("vector", f"dX{tt % 2}", dl_e)
                    sch.wait("vector", "cPE", pe_z[(p, ttl)])
                    sch.op("vector", lambda e, xb=xb, tt=tt, tf=tf: e.tensor_scalar(
                        tf[:], xb[:], negmu_a[:, tt:tt + 1], r1_a[:, tt:tt + 1],
                        ALU.add, ALU.mult), inc="cVE")
                    sch.wait("vector", "cVE")
                    sch.op("vector", lambda e, tf=tf: e.tensor_add(tf[:], tf[:], b2h_rep[:]), inc="cVE")
                    sch.wait("vector", "cVE")
                    if tt >= 2:
                        sch.wait("vector", "cSC", sc_g[tt - 2])
                        sch.wait("vector", f"dS{tt % 2}", ds_out[tt - 2])
                    sch.op("vector", lambda e, fs=fs, za=za, tf=tf: e.scalar_tensor_tensor(
                        fs[:, 0:512], za, 0.5, tf[:, 0:512], ALU.mult, ALU.add), inc="cVE")
                    sch.op("vector", lambda e, fs=fs, zb=zb, tf=tf: e.scalar_tensor_tensor(
                        fs[:, 512:1024], zb, 0.5, tf[:, 512:1024], ALU.mult, ALU.add), inc="cVE")
                    ve_ee[tt] = sch.snap("cVE")
                    sch.wait("scalar", "cVE", ve_ee[tt])
                    if tt >= 2:
                        sem2, val2 = o_done[tt - 2]
                        sch.wait("scalar", sem2, val2)
                    sch.op("scalar", lambda e, fs=fs: e.activation(
                        fs[:], fs[:], AF.Silu), inc="cSC")
                    sc_g[tt] = sch.snap("cSC")
                    if ttl < 5:
                        sch.wait("gpsimd", "cSC", sc_g[tt])
                        sch.wait("gpsimd", f"dX{tt % 2}", dl_e)
                        if tt >= 2:
                            sch.wait("gpsimd", f"dS{tt % 2}", ds_out[tt - 2])
                        sch.op("gpsimd", lambda e, fs=fs, sbt=sbt: e.tensor_mul(
                            fs[:], fs[:], sbt[:]), inc="cGP")
                        sch.wait("gpsimd", "cGP")
                        sch.op("gpsimd", lambda e, xb=xb, fs=fs: e.tensor_add(
                            fs[:], fs[:], xb[:]), inc="cGP")
                        o_done[tt] = ("cGP", sch.snap("cGP"))
                    else:
                        sch.wait("vector", "cSC", sc_g[tt])
                        sch.op("vector", lambda e, fs=fs, sbt=sbt: e.tensor_mul(
                            fs[:], fs[:], sbt[:]), inc="cVE")
                        sch.wait("vector", "cVE")
                        sch.op("vector", lambda e, xb=xb, fs=fs: e.tensor_add(
                            fs[:], fs[:], xb[:]), inc="cVE")
                        o_done[tt] = ("cVE", sch.snap("cVE"))
                    sem, val = o_done[tt]
                    sch.wait("scalar", sem, val)
                    sch.op("scalar", lambda e, fs=fs, tt=tt: e.dma_start(
                        out=out_d[128 * tt:128 * (tt + 1), :], in_=fs[:]), inc=f"dS{tt % 2}", k=16)
                    ds_out[tt] = sch.snap(f"dS{tt % 2}")
            sch.wait("scalar", "dS0")
            sch.wait("scalar", "dS1")
            run_block(nc, "b4", sch)

    return nc


_CONST_CACHE = {}


def _constants():
    if _CONST_CACHE:
        return _CONST_CACHE
    cols_E = [64 * (4 * cp + j) + 2 * i for cp in range(4) for j in range(4) for i in range(32)]
    _CONST_CACHE["wperm"] = np.array(cols_E + [c + 1 for c in cols_E])
    i = np.arange(32)
    inv_freq = 10000.0 ** (-(2.0 * i) / DK)
    t = np.arange(S, dtype=np.float64)
    ang = np.tile(np.outer(inv_freq, t), (4, 1))

    def split(theta):
        k = np.round(theta / np.pi)
        thr = theta - k * np.pi
        sg = np.where(k % 2 == 0, 1.0, -1.0)
        hi = thr.astype(np.float32)
        lo = (thr - hi.astype(np.float64)).astype(np.float32)
        return hi, lo, sg.astype(np.float32)

    s_hi, s_lo, s_sg = split(ang)
    c_hi, c_lo, c_sg = split(ang + np.pi / 2)
    _CONST_CACHE.update(
        ths_hi=s_hi, ths_lo=s_lo, sgs=s_sg,
        thc_hi=c_hi, thc_lo=c_lo, sgc=c_sg,
        ident=np.eye(128, dtype=np.float32),
        cmask=(np.arange(128)[:, None] <= np.arange(128)[None, :]).astype(np.float32),
    )
    return _CONST_CACHE


def kernel(L0_attn_x, L0_attn_Wq, L0_attn_Wk, L0_attn_Wv, L0_attn_Wo,
           L0_ffn_ln_gamma, L0_ffn_ln_beta, L0_ffn_fc1_q, L0_ffn_fc1_scales,
           L0_ffn_fc1_bias, L0_ffn_fc2_q, L0_ffn_fc2_scales, L0_ffn_fc2_bias):
    c = _constants()
    x = np.ascontiguousarray(np.asarray(L0_attn_x, dtype=np.float32))
    shared = {
        "wq": np.ascontiguousarray(np.asarray(L0_attn_Wq, np.float32)[:, c["wperm"]]),
        "wk": np.ascontiguousarray(np.asarray(L0_attn_Wk, np.float32)[:, c["wperm"]]),
        "wv": np.ascontiguousarray(np.asarray(L0_attn_Wv, np.float32)),
        "wo": np.ascontiguousarray(np.asarray(L0_attn_Wo, np.float32)),
        "ln_gamma": np.ascontiguousarray(np.asarray(L0_ffn_ln_gamma, np.float32)),
        "ln_beta": np.ascontiguousarray(np.asarray(L0_ffn_ln_beta, np.float32)),
        "fc1_q": np.ascontiguousarray(np.asarray(L0_ffn_fc1_q, np.int32)),
        "fc1_scales": np.ascontiguousarray(np.asarray(L0_ffn_fc1_scales, np.float32)),
        "fc1_bias": np.ascontiguousarray(np.asarray(L0_ffn_fc1_bias, np.float32)),
        "fc2_q": np.ascontiguousarray(np.asarray(L0_ffn_fc2_q, np.int32)),
        "fc2_scales": np.ascontiguousarray(np.asarray(L0_ffn_fc2_scales, np.float32)),
        "fc2_bias": np.ascontiguousarray(np.asarray(L0_ffn_fc2_bias, np.float32)),
        "ths_hi": c["ths_hi"], "ths_lo": c["ths_lo"], "sgs": c["sgs"],
        "thc_hi": c["thc_hi"], "thc_lo": c["thc_lo"], "sgc": c["sgc"],
        "ident": c["ident"], "cmask": c["cmask"],
    }
    in_maps = [dict(shared, x=np.ascontiguousarray(x[b])) for b in range(8)]
    nc = build_nc()
    res = run_bass_kernel_spmd(nc, in_maps, core_ids=list(range(8)))
    return np.stack([res.results[b]["out"] for b in range(8)], axis=0)
